# version 26
# baseline (speedup 1.0000x reference)
import numpy as np
import ml_dtypes

import concourse.bass as bass
import concourse.mybir as mybir
from concourse.bass_utils import run_bass_kernel_spmd

BF16 = mybir.dt.bfloat16
F32 = mybir.dt.float32
AF = mybir.ActivationFunctionType
ALU = mybir.AluOpType
AXX = mybir.AxisListType.X

NCORE = 8
NSEQ = 4
T = 2048
DM = 1024
NH, NKV, HD = 8, 4, 128
DEXP, NEXP = 512, 8
SCALE = float(np.float32(1.0) / np.sqrt(np.float32(128.0)))
MASKVAL = -1.0e6

NQS = T // 512
NQB = T // 256

_off = [0]
def _alloc(n):
    o = _off[0]; _off[0] += n; return o

R_X    = _alloc(16 * 1024)
R_MASK = _alloc(384)
R_ID   = _alloc(128)
R_PROT = _alloc(128)
R_SEL  = _alloc(1024)
R_WCAT = _alloc(128)
R_B2   = _alloc(1024)
R_LNT  = _alloc(4096)
R_SQ   = _alloc(1024)
R_LNF  = _alloc(4096)
R_HS   = _alloc(512)
R_GH   = _alloc(2048)
R_GFM  = _alloc(512)
R_PT   = _alloc(2048)
R_QF   = _alloc(16384)
R_KF   = _alloc(8192)
R_VA   = _alloc(4 * 16 * 129)
R_E    = _alloc(32768)
NB = _off[0]

EXB = R_QF + 64
def exp_slot(e):
    return EXB + 8192 * e
W_QKV = exp_slot(4)
W_O = exp_slot(6)
assert exp_slot(4) == R_E
assert exp_slot(8) == NB

_f = [0]
def _falloc(n):
    o = _f[0]; _f[0] += n; return o
F_CENT = _falloc(1024)
F_ST   = _falloc(128)
F_REC  = _falloc(8)
F_GAT  = _falloc(128)
F_B1   = _falloc(32)
F_COS  = _falloc(512)
F_SIN  = _falloc(512)
NF = _f[0]
assert NB * 2 + NF * 4 <= 212000, (NB, NF)

def _st(parity):
    b = F_ST + 64 * parity
    return dict(SUM=b, SQ=b + 4, M=b + 8, MSQ=b + 12, V=b + 16, RS=b + 20,
                NMR=b + 24, SV=b + 28, XN=b + 32, IXN=b + 36, T0=b + 40, T1=b + 44)
GB_L, GB_T, GB_M1, GB_NM1, GB_M2, GB_S, GB_R, GB_E = (
    F_GAT + 0, F_GAT + 32, F_GAT + 64, F_GAT + 68, F_GAT + 72, F_GAT + 76, F_GAT + 80,
    F_GAT + 88)


def _host_consts():
    bf = ml_dtypes.bfloat16
    try:
        import jax
        import jax.numpy as jnp
        with jax.default_device(jax.devices("cpu")[0]):
            inv_freq = 1.0 / 10000.0 ** (jnp.arange(0, HD, 2, dtype=jnp.float32) / HD)
            freqs = jnp.outer(jnp.arange(T, dtype=jnp.float32), inv_freq)
            emb = jnp.concatenate((freqs, freqs), axis=-1)
            cos = np.asarray(jnp.cos(emb), dtype=np.float32)
            sin = np.asarray(jnp.sin(emb), dtype=np.float32)
    except Exception:
        inv_freq = (1.0 / (10000.0 ** (np.arange(0, HD, 2, dtype=np.float32) / np.float32(HD)))).astype(np.float32)
        freqs = np.outer(np.arange(T, dtype=np.float32), inv_freq).astype(np.float32)
        emb = np.concatenate([freqs, freqs], axis=1)
        cos = np.cos(emb).astype(np.float32)
        sin = np.sin(emb).astype(np.float32)
    sin_signed = sin.copy()
    sin_signed[:, :64] *= -1.0
    c_cos = np.ascontiguousarray(cos.T).astype(np.float32)
    c_sin = np.ascontiguousarray(sin_signed.T).astype(np.float32)
    p = np.arange(128)[:, None]
    v = np.arange(128, 512)[None, :]
    c_mask = np.where(p <= v - 256, 0.0, MASKVAL).astype(bf)
    c_id = np.eye(128, dtype=np.float32).astype(bf)
    prot = np.zeros((128, 128), dtype=np.float32)
    prot[np.arange(128), (np.arange(128) + 64) % 128] = 1.0
    c_prot = prot.astype(bf)
    sel = np.zeros((8, 8, 128), dtype=np.float32)
    for e in range(8):
        sel[e, e, :] = 1.0
    c_sel = sel.transpose(1, 0, 2).reshape(8, 1024).astype(bf)
    return dict(c_cos=c_cos, c_sin=c_sin, c_mask=c_mask, c_id=c_id,
                c_prot=c_prot, c_sel=np.ascontiguousarray(c_sel))


class Prog:
    ENGS = ("tensor", "vector", "scalar", "gpsimd", "sync")

    def __init__(self):
        self.ops = {e: [] for e in self.ENGS}
        self.cnt = {}
        self.known = {e: {} for e in self.ENGS}

    def emit(self, eng, fn, waits=(), inc=None, inc_by=1):
        w = []
        for s, v in waits:
            if v <= 0 or self.known[eng].get(s, 0) >= v:
                continue
            self.known[eng][s] = v
            w.append((s, v))
        val = None
        if inc is not None:
            self.cnt[inc] = self.cnt.get(inc, 0) + inc_by
            val = self.cnt[inc]

        def op(engine, sems, w=w, fn=fn, inc=inc, inc_by=inc_by):
            for s, v in w:
                engine.wait_ge(sems[s], v)
            ins = fn(engine)
            if inc is not None and ins is not None:
                ins.then_inc(sems[inc], inc_by)
        self.ops[eng].append(op)
        return (inc, val) if inc is not None else None

    def c(self, sem):
        return self.cnt.get(sem, 0)


class Buf:
    def __init__(self):
        self.w = {}
        self.r = {}

    @staticmethod
    def _m(d, ev):
        if ev is None:
            return
        s, v = ev
        if d.get(s, 0) < v:
            d[s] = v

    def wrote(self, ev):
        self._m(self.w, ev)

    def read(self, ev):
        self._m(self.r, ev)

    @property
    def need_for_read(self):
        return list(self.w.items())

    @property
    def need_for_write(self):
        out = dict(self.w)
        for s, v in self.r.items():
            if out.get(s, 0) < v:
                out[s] = v
        return list(out.items())


def build_nc():
    nc = bass.Bass("TRN2", target_bir_lowering=False)
    xin = nc.dram_tensor("x", [NSEQ, T, DM], F32, kind="ExternalInput").ap()
    wq = nc.dram_tensor("wq", [DM, NH * HD], F32, kind="ExternalInput").ap()
    wk = nc.dram_tensor("wk", [DM, NKV * HD], F32, kind="ExternalInput").ap()
    wv = nc.dram_tensor("wv", [DM, NKV * HD], F32, kind="ExternalInput").ap()
    wo = nc.dram_tensor("wo", [NH * HD, DM], F32, kind="ExternalInput").ap()
    w1 = nc.dram_tensor("w1", [NEXP, DM, DEXP], F32, kind="ExternalInput").ap()
    b1d = nc.dram_tensor("b1", [128, 32], F32, kind="ExternalInput").ap()
    w2 = nc.dram_tensor("w2", [NEXP, DEXP, DM], F32, kind="ExternalInput").ap()
    b2d = nc.dram_tensor("b2", [NEXP, DM], F32, kind="ExternalInput").ap()
    cent = nc.dram_tensor("cent", [NEXP, DM], F32, kind="ExternalInput").ap()
    wrtd = nc.dram_tensor("wrt", [NEXP, DM], F32, kind="ExternalInput").ap()
    cc = {}
    for n, shp in (("c_mask", [128, 384]), ("c_id", [128, 128]),
                   ("c_prot", [128, 128]), ("c_sel", [8, 1024])):
        cc[n] = nc.dram_tensor(n, shp, BF16, kind="ExternalInput").ap()
    for n in ("c_cos", "c_sin"):
        cc[n] = nc.dram_tensor(n, [128, T], F32, kind="ExternalInput").ap()
    out = nc.dram_tensor("out", [NSEQ, T, DM], F32, kind="ExternalOutput").ap()
    wstage = nc.dram_tensor("wstage", [128, 24576 + NEXP * 8192], BF16, kind="Internal").ap()

    import contextlib
    with contextlib.ExitStack() as ctx:
        AB = ctx.enter_context(nc.sbuf_tensor("AB", [128, NB], BF16))
        AFP = ctx.enter_context(nc.sbuf_tensor("AFp", [128, NF], F32))
        pb = [ctx.enter_context(nc.psum_tensor(f"pb{i}", [128, 512], F32))
              for i in range(8)]
        semnames = (("t", "v", "s", "g", "dc", "dcs", "dwi", "dcen", "dwq", "dwo", "hwq", "hwo")
                    + tuple(f"dx{i}" for i in range(4))
                    + tuple(f"do{i}" for i in range(4))
                    + tuple(f"dwe{e}" for e in range(NEXP))
                    + tuple(f"hwe{e}" for e in range(NEXP)))
        sems = {n: ctx.enter_context(nc.semaphore(f"sem_{n}")) for n in semnames}

        P = Prog()
        ab = AB.ap()
        af = AFP.ap()

        def B(off, n, p0=0, p1=128):
            return ab[p0:p1, off:off + n]

        def Fp(off, n, p0=0, p1=128):
            return af[p0:p1, off:off + n]

        bank = [Buf() for _ in range(8)]
        bufs = {}
        def buf(name):
            if name not in bufs:
                bufs[name] = Buf()
            return bufs[name]

        def dma(queue, outap, inap, sem, rbufs=(), wbufs=()):
            waits = {}
            for b in rbufs:
                for s, v in b.need_for_read:
                    waits[s] = max(waits.get(s, 0), v)
            for b in wbufs:
                for s, v in b.need_for_write:
                    waits[s] = max(waits.get(s, 0), v)
            ev = P.emit(queue, lambda e, o=outap, i=inap: e.dma_start(out=o, in_=i),
                        waits=list(waits.items()), inc=sem, inc_by=16)
            for b in rbufs:
                b.read(ev)
            for b in wbufs:
                b.wrote(ev)
            return ev

        def run(eng, fn, read=(), write=(), extra=(), inc=None):
            waits = {}
            for b in read:
                for s, v in b.need_for_read:
                    waits[s] = max(waits.get(s, 0), v)
            for b in write:
                for s, v in b.need_for_write:
                    waits[s] = max(waits.get(s, 0), v)
            for s, v in extra:
                waits[s] = max(waits.get(s, 0), v)
            ev = P.emit(eng, fn, waits=list(waits.items()), inc=inc)
            for b in read:
                b.read(ev if ev is not None else (eng_sem(eng), P.c(eng_sem(eng))))
            for b in write:
                b.wrote(ev)
            return ev

        def eng_sem(eng):
            return {"tensor": "t", "vector": "v", "scalar": "s", "gpsimd": "g"}.get(eng, "t")

        cosb, sinb, maskb, idb, protb, selb = (buf(x) for x in
            ("cos", "sin", "mask", "id", "prot", "sel"))
        wqkv, wob, wcat, bb1, bb2 = (buf(x) for x in ("wqkv", "wob", "wcat", "b1", "b2"))
        ex = [buf(f"ex{e}") for e in range(NEXP)]
        xp = [buf(f"xp{i}") for i in range(4)]
        qf = [buf(f"qf{j}") for j in range(4)]
        kf = [buf(f"kf{j}") for j in range(4)]
        vaug = [buf(f"va{j}") for j in range(4)]
        lnt = [buf("lnt_a"), buf("lnt_b"), buf("lnt_c")]
        sq = buf("sq")
        lnf = [buf(f"lnf{k}") for k in range(8)]
        ptb = [buf(f"pt{i}") for i in range(4)]
        hsb = [buf(f"hs{i}") for i in range(2)]
        ghb, gfm = [buf("gh0"), buf("gh1")], buf("gfm")
        stats, gat, orec, cstage = buf("stats"), buf("gat"), buf("orec"), buf("cstage")

        dma("sync", B(R_MASK, 384), cc["c_mask"], "dc", wbufs=(maskb,))
        dma("sync", B(R_ID, 128), cc["c_id"], "dc", wbufs=(idb,))
        dma("sync", B(R_PROT, 128), cc["c_prot"], "dc", wbufs=(protb,))
        dma("sync", B(R_SEL, 1024, 0, 8), cc["c_sel"], "dc", wbufs=(selb,))
        for b_ in (cosb, sinb, maskb, idb, protb, selb):
            b_.wrote(("dc", P.c("dc")))

        for part_ in range(4):
            dma("gpsimd",
                B(R_X + part_ * 4096, 4096).rearrange("p (i d) -> p i d", d=1024),
                xin[0, part_ * 512:(part_ + 1) * 512, :].rearrange("(i p) d -> p i d", p=128),
                f"dx{part_}", wbufs=(xp[part_],))
        dma("gpsimd", Fp(F_B1, 32), b1d, "dwi", wbufs=(bb1,))
        dma("gpsimd", B(R_B2, 1024, 0, 8), b2d, "dwi", wbufs=(bb2,))
        bb1.wrote(("dwi", P.c("dwi")))
        bb2.wrote(("dwi", P.c("dwi")))

        ones_ap = ab[:, R_VA + 128: R_VA + 64 * 129: 129]

        for idx, (src, colo) in enumerate(((cent, 0), (wrtd, 8))):
            dma("gpsimd", Fp(F_CENT, 1024, 0, 8), src, "dcen", wbufs=(cstage,))
            if idx == 0:
                run("scalar",
                    lambda e: e.activation(B(R_LNT, 1024, 0, 8), Fp(F_CENT, 1024, 0, 8),
                                           AF.Square, accum_out=Fp(F_GAT, 1, 0, 8)),
                    read=(cstage,), write=(lnt[0], gat), inc="s")
                run("scalar", lambda e: e.activation(Fp(F_GAT + 1, 1, 0, 8), Fp(F_GAT, 1, 0, 8), AF.Ln),
                    read=(gat,), write=(gat,), inc="s")
                run("scalar", lambda e: e.activation(Fp(F_GAT + 2, 1, 0, 8), Fp(F_GAT + 1, 1, 0, 8),
                                                     AF.Exp, scale=0.5),
                    read=(gat,), write=(gat,), inc="s")
                run("vector", lambda e: e.tensor_scalar_max(Fp(F_GAT + 3, 1, 0, 8), Fp(F_GAT + 2, 1, 0, 8), 1e-12),
                    read=(gat,), write=(gat,), inc="v")
                run("vector", lambda e: e.reciprocal(Fp(F_GAT + 4, 1, 0, 8), Fp(F_GAT + 3, 1, 0, 8)),
                    read=(gat,), write=(gat,), inc="v")
                run("vector",
                    lambda e: e.tensor_scalar(B(R_LNT, 1024, 0, 8), Fp(F_CENT, 1024, 0, 8),
                                              Fp(F_GAT + 4, 1, 0, 8), None, ALU.mult),
                    read=(cstage, gat), write=(lnt[0],), inc="v")
            else:
                run("vector", lambda e: e.tensor_copy(B(R_LNT, 1024, 0, 8), Fp(F_CENT, 1024, 0, 8)),
                    read=(cstage,), write=(lnt[0],), inc="v")
            for k in range(8):
                run("tensor",
                    lambda e, k=k: e.matmul(pb[0][:, 0:8], B(R_LNT + k * 128, 128, 0, 8),
                                            B(R_ID, 8, 0, 8), start=True, stop=True),
                    read=(lnt[0], idb), write=(bank[0],), inc="t")
                run("scalar",
                    lambda e, k=k, c=colo: e.copy(B(R_WCAT + k * 16 + c, 8), pb[0][:, 0:8]),
                    read=(bank[0],), write=(wcat,), inc="s")

        stg_qkv, stg_wo = buf("stg_qkv"), buf("stg_wo")
        stg_ex = [buf(f"stg_ex{e}") for e in range(NEXP)]

        def load_qkv_weights(staged):
            if staged:
                dma("sync", B(W_QKV, 16384), wstage[:, 0:16384], "hwq",
                    rbufs=(stg_qkv,), wbufs=(wqkv, ex[4], ex[5]))
            else:
                dma("gpsimd", B(W_QKV, 8192).rearrange("p (k n) -> p k n", n=1024),
                    wq.rearrange("(k p) n -> p k n", p=128), "dwq", wbufs=(wqkv, ex[4]))
                dma("gpsimd", B(W_QKV + 8192, 4096).rearrange("p (k n) -> p k n", n=512),
                    wk.rearrange("(k p) n -> p k n", p=128), "dwq", wbufs=(wqkv, ex[4], ex[5]))
                dma("gpsimd", B(W_QKV + 12288, 4096).rearrange("p (k n) -> p k n", n=512),
                    wv.rearrange("(k p) n -> p k n", p=128), "dwq", wbufs=(wqkv, ex[5]))
            wqkv.wrote(("dwq", P.c("dwq")))

        def stage_qkv_wo():
            dma("sync", wstage[:, 0:16384], B(W_QKV, 16384), "hwq", rbufs=(wqkv,),
                wbufs=(stg_qkv,))
            dma("sync", wstage[:, 16384:24576], B(W_O, 8192), "hwo", rbufs=(wob,),
                wbufs=(stg_wo,))

        def load_wo(staged):
            if staged:
                dma("sync", B(W_O, 8192), wstage[:, 16384:24576], "hwo",
                    rbufs=(stg_wo,), wbufs=(wob, ex[6]))
            else:
                dma("gpsimd", B(W_O, 8192).rearrange("p (k n) -> p k n", n=1024),
                    wo.rearrange("(k p) n -> p k n", p=128), "dwo", wbufs=(wob, ex[6]))

        def load_expert(e, staged):
            base = exp_slot(e)
            confl = [ex[e]]
            if e in (4, 5):
                confl.append(wqkv)
            elif e == 6:
                confl.append(wob)
            elif e <= 3:
                confl += qf + kf + vaug + [ptb[0], ptb[1]]
            if staged:
                dma("sync", B(base, 8192), wstage[:, 24576 + e * 8192: 24576 + (e + 1) * 8192],
                    f"hwe{e}", rbufs=(stg_ex[e],), wbufs=tuple(confl))
            else:
                dma("gpsimd", B(base, 4096).rearrange("p (k n) -> p k n", n=512),
                    w1[e].rearrange("(k p) n -> p k n", p=128), f"dwe{e}", wbufs=tuple(confl))
                dma("gpsimd", B(base + 4096, 4096).rearrange("p (k n) -> p k n", n=1024),
                    w2[e].rearrange("(k p) n -> p k n", p=128), f"dwe{e}", wbufs=tuple(confl))
                dma("sync", wstage[:, 24576 + e * 8192: 24576 + (e + 1) * 8192], B(base, 8192),
                    f"hwe{e}", rbufs=tuple([ex[e]] + confl[1:]), wbufs=(stg_ex[e],))
            ex[e].wrote((f"dwe{e}", P.c(f"dwe{e}")))

        load_qkv_weights(staged=False)
        load_wo(staged=False)
        stage_qkv_wo()

        EXPERT_ORDER = [4, 5, 7, 6, 0, 1, 2, 3]

        statsb = [buf("stats0"), buf("stats1")]

        def ln_accum(qs, i4):
            pr = qs % 2
            S = _st(pr)
            st = statsb[pr]
            tile = qs * 4 + i4
            xap = B(R_X + tile * 1024, 1024)
            run("vector", lambda e, i4=i4, xap=xap, S=S: e.reduce_sum(Fp(S["SUM"] + i4, 1), xap, AXX),
                read=(xp[tile // 4],), write=(st,), inc="v")
            run("scalar",
                lambda e, i4=i4, xap=xap, S=S: e.activation(B(R_LNT + 3072, 1024), xap, AF.Square,
                                                           accum_out=Fp(S["SQ"] + i4, 1)),
                read=(xp[tile // 4],), write=(lnt[2], st), inc="s")

        def ln_math(qs, want_xnorm, lo=0, hi=4):
            pr = qs % 2
            S = _st(pr)
            st = statsb[pr]
            n_ = hi - lo
            run("vector", lambda e, S=S, lo=lo, n_=n_: e.tensor_scalar(Fp(S["M"] + lo, n_), Fp(S["SUM"] + lo, n_), 1.0 / 1024.0, None, ALU.mult),
                read=(st,), write=(st,), inc="v")
            run("vector", lambda e, S=S, lo=lo, n_=n_: e.tensor_scalar(Fp(S["MSQ"] + lo, n_), Fp(S["SQ"] + lo, n_), 1.0 / 1024.0, None, ALU.mult),
                read=(st,), write=(st,), inc="v")
            run("vector", lambda e, S=S, lo=lo, n_=n_: e.tensor_tensor(Fp(S["T0"] + lo, n_), Fp(S["M"] + lo, n_), Fp(S["M"] + lo, n_), ALU.mult),
                read=(st,), write=(st,), inc="v")
            run("vector", lambda e, S=S, lo=lo, n_=n_: e.tensor_tensor(Fp(S["V"] + lo, n_), Fp(S["MSQ"] + lo, n_), Fp(S["T0"] + lo, n_), ALU.subtract),
                read=(st,), write=(st,), inc="v")
            run("vector", lambda e, S=S, lo=lo, n_=n_: e.tensor_scalar(Fp(S["T1"] + lo, n_), Fp(S["V"] + lo, n_), 1e-5, None, ALU.add),
                read=(st,), write=(st,), inc="v")
            run("scalar", lambda e, S=S, lo=lo, n_=n_: e.activation(Fp(S["T0"] + lo, n_), Fp(S["T1"] + lo, n_), AF.Ln),
                read=(st,), write=(st,), inc="s")
            run("scalar", lambda e, S=S, lo=lo, n_=n_: e.activation(Fp(S["RS"] + lo, n_), Fp(S["T0"] + lo, n_), AF.Exp, scale=-0.5),
                read=(st,), write=(st,), inc="s")
            run("vector", lambda e, S=S, lo=lo, n_=n_: e.scalar_tensor_tensor(Fp(S["NMR"] + lo, n_), Fp(S["M"] + lo, n_), -1.0, Fp(S["RS"] + lo, n_),
                                                                ALU.mult, ALU.mult),
                read=(st,), write=(st,), inc="v")
            if want_xnorm:
                run("scalar", lambda e, S=S, lo=lo, n_=n_: e.activation(Fp(S["T1"] + lo, n_), Fp(S["V"] + lo, n_), AF.Ln),
                    read=(st,), write=(st,), inc="s")
                run("scalar", lambda e, S=S, lo=lo, n_=n_: e.activation(Fp(S["SV"] + lo, n_), Fp(S["T1"] + lo, n_), AF.Exp, scale=0.5),
                    read=(st,), write=(st,), inc="s")
                run("vector", lambda e, S=S, lo=lo, n_=n_: e.tensor_tensor(Fp(S["XN"] + lo, n_), Fp(S["SV"] + lo, n_), Fp(S["RS"] + lo, n_), ALU.mult),
                    read=(st,), write=(st,), inc="v")
                run("vector", lambda e, S=S, lo=lo, n_=n_: e.tensor_scalar(Fp(S["XN"] + lo, n_), Fp(S["XN"] + lo, n_), 32.0, 1e-12, ALU.mult, ALU.max),
                    read=(st,), write=(st,), inc="v")
                run("vector", lambda e, S=S, lo=lo, n_=n_: e.reciprocal(Fp(S["IXN"] + lo, n_), Fp(S["XN"] + lo, n_)),
                    read=(st,), write=(st,), inc="v")

        def ln_stats(qs, want_xnorm):
            for i4 in range(4):
                ln_accum(qs, i4)
            ln_math(qs, want_xnorm)

        def ln_norm(qs):
            pr = qs % 2
            S = _st(pr)
            st = statsb[pr]
            for i4 in range(4):
                ln_norm_tile(qs, i4)

        def ln_norm_tile(qs, i4):
            pr = qs % 2
            S = _st(pr)
            st = statsb[pr]
            tile = qs * 4 + i4
            lb = lnt[0] if i4 < 2 else (lnt[1] if i4 == 2 else lnt[2])
            run("vector",
                lambda e, i4=i4, tile=tile, S=S: e.tensor_scalar(
                    B(R_LNT + i4 * 1024, 1024), B(R_X + tile * 1024, 1024),
                    Fp(S["RS"] + i4, 1), Fp(S["NMR"] + i4, 1), ALU.mult, ALU.add),
                read=(xp[tile // 4], st), write=(lb,), inc="v")

        def ln_trans(qs):
            for k in range(8):
                tb = k % 2
                for i4 in range(4):
                    lb = lnt[0] if i4 < 2 else (lnt[1] if i4 == 2 else lnt[2])
                    run("tensor",
                        lambda e, k=k, i4=i4, tb=tb: e.matmul(
                            pb[tb][:, i4 * 128:(i4 + 1) * 128],
                            B(R_LNT + i4 * 1024 + k * 128, 128), B(R_ID, 128),
                            start=True, stop=True),
                        read=(lb, idb), write=(bank[tb],),
                        inc="t" if i4 == 3 else None)
                run("scalar", lambda e, k=k, tb=tb: e.copy(B(R_LNF + k * 512, 512), pb[tb][:, 0:512]),
                    read=(bank[tb],), write=(lnf[k],), inc="s")

        def load_x(s, part):
            dma("gpsimd",
                B(R_X + part * 4096, 4096).rearrange("p (i d) -> p i d", d=1024),
                xin[s, part * 512:(part + 1) * 512, :].rearrange("(i p) d -> p i d", p=128),
                f"dx{part}", wbufs=(xp[part],))

        for s in range(NSEQ):
            run("vector", lambda e: e.memset(ones_ap, 1.0), write=tuple(vaug), inc="v")

            for qs in range(NQS):
                if qs == 0 and s == 0:
                    ln_stats(0, want_xnorm=False)
                    ln_norm(0)
                dma("sync", Fp(F_COS, 512), cc["c_cos"][:, qs * 512:(qs + 1) * 512],
                    "dcs", wbufs=(cosb,))
                dma("sync", Fp(F_SIN, 512), cc["c_sin"][:, qs * 512:(qs + 1) * 512],
                    "dcs", wbufs=(sinb,))
                cosb.wrote(("dcs", P.c("dcs")))
                sinb.wrote(("dcs", P.c("dcs")))
                ln_trans(qs)
                cosap = Fp(F_COS, 512)
                sinap = Fp(F_SIN, 512)

                def qkv_epilogue(m, qs=qs):
                    pp = 6 + (m % 2)
                    rb = 2 + (m % 2)
                    sl = (m % 2) * 512
                    if m < 8:
                        dstoff, dstbuf = R_QF + m * T + qs * 512, qf[qs]
                    else:
                        dstoff, dstbuf = R_KF + (m - 8) * T + qs * 512, kf[qs]
                    run("scalar", lambda e, pp=pp, sl=sl: e.copy(B(R_SQ + sl, 512), pb[pp][:, 0:512]),
                        read=(bank[pp],), write=(sq,), inc="s")
                    run("tensor",
                        lambda e, rb=rb, sl=sl: e.matmul(pb[rb][:, 0:512], B(R_PROT, 128),
                                                         B(R_SQ + sl, 512), start=True, stop=True),
                        read=(sq, protb), write=(bank[rb],), inc="t")
                    run("vector",
                        lambda e, cosap=cosap, sl=sl: e.tensor_tensor(
                            B(R_GH + 1024 + sl, 512), B(R_SQ + sl, 512), cosap, ALU.mult),
                        read=(sq, cosb), write=(ghb[1],), inc="v")
                    run("vector",
                        lambda e, sinap=sinap, rb=rb, sl=sl: e.tensor_tensor(
                            B(R_GH + sl, 512), pb[rb][:, 0:512], sinap, ALU.mult),
                        read=(bank[rb], sinb), write=(ghb[0],), inc="v")
                    run("vector",
                        lambda e, dstoff=dstoff, sl=sl: e.tensor_tensor(
                            B(dstoff, 512), B(R_GH + 1024 + sl, 512),
                            B(R_GH + sl, 512), ALU.add),
                        read=(ghb[0], ghb[1]), write=(dstbuf,), inc="v")

                for m in range(12):
                    pp = 6 + (m % 2)
                    if m < 8:
                        wbase, wstride, mm = W_QKV, 1024, m
                    else:
                        wbase, wstride, mm = W_QKV + 8192, 512, m - 8
                    for k in range(8):
                        run("tensor",
                            lambda e, k=k, mm=mm, pp=pp, wbase=wbase, wstride=wstride:
                            e.matmul(pb[pp][:, 0:512],
                                     B(wbase + k * wstride + mm * 128, 128),
                                     B(R_LNF + k * 512, 512),
                                     start=(k == 0), stop=(k == 7)),
                            read=(lnf[k], wqkv), write=(bank[pp],),
                            inc="t" if k == 7 else None)
                    if m >= 1:
                        qkv_epilogue(m - 1)
                qkv_epilogue(11)

                def v_evac(i4, qs=qs):
                    pp = 6 + (i4 % 2)
                    j = qs * 4 + i4
                    for g in range(4):
                        run("vector",
                            lambda e, g=g, j=j, pp=pp: e.tensor_copy(
                                B(R_VA + (g * 16 + j) * 129, 128), pb[pp][:, g * 128:(g + 1) * 128]),
                            read=(bank[pp],), write=(vaug[qs],), inc="v")

                for i4 in range(4):
                    pp = 6 + (i4 % 2)
                    for k in range(8):
                        run("tensor",
                            lambda e, k=k, i4=i4, pp=pp: e.matmul(
                                pb[pp][:, 0:512],
                                B(R_LNF + k * 512 + i4 * 128, 128),
                                B(W_QKV + 12288 + k * 512, 512),
                                start=(k == 0), stop=(k == 7)),
                            read=(lnf[k], wqkv), write=(bank[pp],),
                            inc="t" if k == 7 else None)
                    if i4 >= 1:
                        v_evac(i4 - 1)
                        if qs < NQS - 1:
                            ln_accum(qs + 1, i4 - 1)
                            if i4 == 3:
                                ln_math(qs + 1, want_xnorm=False, lo=0, hi=3)
                                for i4_ in range(3):
                                    ln_norm_tile(qs + 1, i4_)
                v_evac(3)
                if qs < NQS - 1:
                    ln_accum(qs + 1, 3)
                    ln_math(qs + 1, want_xnorm=False, lo=3, hi=4)
                    ln_norm_tile(qs + 1, 3)

            pend_otrans = [None]
            p2_fill = []
            for qb in range(NQB):
                if qb == 1:
                    load_expert(4, staged=(s > 0))
                    load_expert(5, staged=(s > 0))
                    if s == 0:
                        load_expert(7, staged=False)
                if qb == 2:
                    p2_fill.extend(
                        [lambda i4_=i4_: ln_accum(0, i4_) for i4_ in range(4)]
                        + [lambda: ln_math(0, want_xnorm=True)]
                        + [lambda i4_=i4_: ln_norm_tile(0, i4_) for i4_ in range(4)])
                nch = 2 * (qb + 1)

                def emit_otrans(g, tb=None):
                    tb = 6 + (g % 2) if tb is None else tb
                    for ri in range(4):
                        run("tensor",
                            lambda e, ri=ri, g=g, tb=tb: e.matmul(
                                pb[tb][:, ri * 128:(ri + 1) * 128],
                                B(R_SQ + (g % 2) * 512 + ri * 128, 128), B(R_ID, 128),
                                start=True, stop=True),
                            read=(sq, idb), write=(bank[tb],),
                            inc="t" if ri == 3 else None)
                    run("vector",
                        lambda e, g=g, tb=tb: e.tensor_copy(B(R_LNF + g * 512, 512), pb[tb][:, 0:512]),
                        read=(bank[tb],), write=(lnf[g],), inc="v")

                for g in range(4):
                    def s_chunk(c, g=g, qb=qb):
                        sb = c % 2
                        pt = c % 4
                        diag = (c >= 2 * qb)
                        if diag:
                            delta = 128 * c - 256 * qb
                            run("tensor",
                                lambda e, sb=sb, delta=delta: e.matmul(
                                    pb[sb][:, 0:512].rearrange("p (h n) -> p h n", n=256),
                                    B(R_ID, 128),
                                    B(R_MASK + 128 - delta, 256).unsqueeze(1).broadcast_to([128, 2, 256]),
                                    start=True, stop=False),
                                read=(idb, maskb), write=(bank[sb],))
                        run("tensor",
                            lambda e, sb=sb, c=c, g=g, qb=qb, diag=diag: e.matmul(
                                pb[sb][:, 0:512].rearrange("p (h n) -> p h n", n=256),
                                B(R_KF + g * T + c * 128, 128),
                                B(R_QF + 2 * g * T + qb * 256, 2 * T).rearrange(
                                    "p (h t) -> p h t", t=T)[:, :, 0:256],
                                start=not diag, stop=True),
                            read=(kf[c // 4], qf[qb // 2]), write=(bank[sb],), inc="t")
                        run("scalar",
                            lambda e, sb=sb, pt=pt: e.activation(B(R_PT + pt * 512, 512), pb[sb][:, 0:512],
                                                                 AF.Exp, scale=SCALE),
                            read=(bank[sb],), write=(ptb[pt],), inc="s")

                    def pv_chunk(c, g=g, nch=nch):
                        pt = c % 4
                        for hl in range(2):
                            for qsub in range(2):
                                ob = 2 + hl * 2 + qsub
                                run("tensor",
                                    lambda e, pt=pt, hl=hl, qsub=qsub, ob=ob, c=c, g=g, nch=nch:
                                    e.matmul(pb[ob][:, 0:129],
                                             B(R_PT + pt * 512 + hl * 256 + qsub * 128, 128),
                                             B(R_VA + (g * 16 + c) * 129, 129),
                                             start=(c == 0), stop=(c == nch - 1)),
                                    read=(ptb[pt], vaug[c // 4]), write=(bank[ob],),
                                    inc="t" if c == nch - 1 else None)

                    PVD = 3
                    ot_pt = min(3, nch - 1)
                    for c in range(nch):
                        s_chunk(c)
                        if c == ot_pt and pend_otrans[0] is not None:
                            emit_otrans(pend_otrans[0])
                            pend_otrans[0] = None
                        if c >= PVD:
                            pv_chunk(c - PVD)
                    for c in range(max(0, nch - PVD), nch):
                        pv_chunk(c)
                    for hl in range(2):
                        for qsub in range(2):
                            ob = 2 + hl * 2 + qsub
                            ri = hl * 2 + qsub + (g % 2) * 4
                            run("vector",
                                lambda e, ob=ob, ri=ri: e.reciprocal(Fp(F_REC + ri, 1), pb[ob][:, 128:129]),
                                read=(bank[ob],), write=(orec,), inc="v")
                            run("vector",
                                lambda e, ob=ob, ri=ri, g=g, hl=hl, qsub=qsub: e.tensor_scalar(
                                    B(R_SQ + (g % 2) * 512 + (hl * 2 + qsub) * 128, 128),
                                    pb[ob][:, 0:128], Fp(F_REC + ri, 1), None, ALU.mult),
                                read=(bank[ob], orec), write=(sq,), inc="v")
                    pend_otrans[0] = g
                    if p2_fill:
                        p2_fill.pop(0)()
                for qsub in range(2):
                    tile = qb * 2 + qsub
                    for half in range(2):
                        pp = 6 + half
                        for hc in range(8):
                            run("tensor",
                                lambda e, hc=hc, qsub=qsub, half=half, pp=pp: e.matmul(
                                    pb[pp][:, 0:512],
                                    B(R_LNF + hc * 256 + qsub * 128, 128),
                                    B(W_O + hc * 1024 + half * 512, 512),
                                    start=(hc == 0), stop=(hc == 7)),
                                read=(lnf[hc // 2], wob), write=(bank[pp],),
                                inc="t" if hc == 7 else None)
                            if qsub == 0 and half == 0 and hc == 5 and pend_otrans[0] is not None:
                                emit_otrans(pend_otrans[0], tb=0)
                                pend_otrans[0] = None
                        run("vector",
                            lambda e, tile=tile, half=half, pp=pp: e.tensor_tensor(
                                B(R_X + tile * 1024 + half * 512, 512),
                                B(R_X + tile * 1024 + half * 512, 512),
                                pb[pp][:, 0:512], ALU.add),
                            read=(bank[pp], xp[tile // 4]), write=(xp[tile // 4],), inc="v")

            for e_ in (6, 0, 1, 2, 3):
                load_expert(e_, staged=(s > 0))

            for qs in range(NQS):
                ln_trans(qs)
                pr = qs % 2
                S = _st(pr)
                st = statsb[pr]
                for sub in range(4):
                    for k in range(8):
                        run("tensor",
                            lambda e, k=k, sub=sub: e.matmul(
                                pb[0][:, sub * 128:sub * 128 + 16],
                                B(R_LNF + k * 512 + sub * 128, 128),
                                B(R_WCAT + k * 16, 16),
                                start=(k == 0), stop=(k == 7)),
                            read=(lnf[k], wcat), write=(bank[0],),
                            inc="t" if k == 7 else None)
                cosp = pb[0][:, 0:512].rearrange("p (s c) -> p s c", c=128)[:, :, 0:8]
                routep = pb[0][:, 0:512].rearrange("p (s c) -> p s c", c=128)[:, :, 8:16]
                def b3(off, n1=4, n2=8):
                    return Fp(off, n1 * n2).rearrange("p (s c) -> p s c", c=n2)
                def bc(off):
                    return Fp(off, 4).unsqueeze(2).broadcast_to([128, 4, 8])
                run("vector",
                    lambda e, S=S: e.tensor_tensor(b3(GB_T), cosp, bc(S["IXN"]), ALU.mult),
                    read=(bank[0], st), write=(gat,), inc="v")
                run("vector", lambda e: e.tensor_tensor(b3(GB_L), b3(GB_T), routep, ALU.add),
                    read=(bank[0], gat), write=(gat,), inc="v")
                run("vector", lambda e: e.reduce_max(Fp(GB_M1, 4), b3(GB_L), AXX),
                    read=(gat,), write=(gat,), inc="v")
                run("vector", lambda e: e.tensor_scalar(Fp(GB_NM1, 4), Fp(GB_M1, 4), -1.0, None, ALU.mult),
                    read=(gat,), write=(gat,), inc="v")
                run("vector", lambda e: e.tensor_tensor(b3(GB_L), b3(GB_L), bc(GB_NM1), ALU.add),
                    read=(gat,), write=(gat,), inc="v")
                run("vector", lambda e: e.tensor_scalar(b3(GB_T), b3(GB_L), 0.0, None, ALU.is_ge),
                    read=(gat,), write=(gat,), inc="v")
                run("vector", lambda e: e.scalar_tensor_tensor(b3(GB_T), b3(GB_T), -1e30, b3(GB_L),
                                                               ALU.mult, ALU.add),
                    read=(gat,), write=(gat,), inc="v")
                run("vector", lambda e: e.reduce_max(Fp(GB_M2, 4), b3(GB_T), AXX),
                    read=(gat,), write=(gat,), inc="v")
                run("vector", lambda e: e.tensor_tensor(b3(GB_T), b3(GB_L), bc(GB_M2), ALU.is_ge),
                    read=(gat,), write=(gat,), inc="v")
                run("scalar", lambda e: e.activation(b3(GB_E), b3(GB_L), AF.Exp),
                    read=(gat,), write=(gat,), inc="s")
                run("vector", lambda e: e.tensor_tensor(b3(GB_L), b3(GB_E), b3(GB_T), ALU.mult),
                    read=(gat,), write=(gat,), inc="v")
                run("vector", lambda e: e.reduce_sum(Fp(GB_S, 4), b3(GB_L), AXX),
                    read=(gat,), write=(gat,), inc="v")
                run("vector", lambda e: e.reciprocal(Fp(GB_R, 4), Fp(GB_S, 4)),
                    read=(gat,), write=(gat,), inc="v")
                run("vector",
                    lambda e: e.tensor_tensor(
                        B(R_SQ, 32).rearrange("p (s c) -> p s c", c=8),
                        b3(GB_L), bc(GB_R), ALU.mult),
                    read=(gat,), write=(sq,), inc="v")
                def emit_gatesT():
                    for sub in range(4):
                        run("tensor",
                            lambda e, sub=sub: e.matmul(
                                pb[1][0:8, sub * 128:(sub + 1) * 128],
                                B(R_SQ + sub * 8, 8), B(R_ID, 128),
                                start=True, stop=True),
                            read=(sq, idb), write=(bank[1],),
                            inc="t" if sub == 3 else None)
                    run("scalar", lambda e: e.copy(B(R_GFM, 512, 0, 8), pb[1][0:8, 0:512]),
                        read=(bank[1],), write=(gfm,), inc="s")
                gatesT_pending = [True]

                for half_t in range(2):
                    if half_t == 1 and qs < NQS - 1:
                        ln_next = (qs + 1, True)
                    elif half_t == 1 and qs == NQS - 1 and s < NSEQ - 1:
                        ln_next = (0, False)
                    else:
                        ln_next = None
                    def w2_mms(ee, ei, stop=False):
                        base = exp_slot(ee)
                        gho = (ei % 2) * 1024
                        for qsub in range(2):
                            for half in range(2):
                                yb = 2 + qsub * 2 + half
                                for ko in range(4):
                                    run("tensor",
                                        lambda e, qsub=qsub, half=half, ko=ko, yb=yb, base=base,
                                               gho=gho, st=(ei == 0 and ko == 0),
                                               sp=(stop and ko == 3):
                                        e.matmul(pb[yb][:, 0:512],
                                                 B(R_GH + gho + ko * 256 + qsub * 128, 128),
                                                 B(base + 4096 + ko * 1024 + half * 512, 512),
                                                 start=st, stop=sp),
                                        read=(ghb[gho // 1024], ex[ee]), write=(bank[yb],),
                                        inc="t" if ko == 3 else None)

                    for ei, ee in enumerate(EXPERT_ORDER):
                        base = exp_slot(ee)
                        gb = ei % 2
                        gho = (ei % 2) * 1024

                        def h_group(ko, base=base, ee=ee, half_t=half_t):
                            pp = 6 + (ko % 2)
                            for k in range(8):
                                run("tensor",
                                    lambda e, k=k, ko=ko, base=base, half_t=half_t, pp=pp: e.matmul(
                                        pb[pp][:, 0:256],
                                        B(base + k * 512 + ko * 128, 128),
                                        B(R_LNF + k * 512 + half_t * 256, 256),
                                        start=(k == 0), stop=(k == 7)),
                                    read=(lnf[k], ex[ee]), write=(bank[pp],),
                                    inc="t" if k == 7 else None)

                        def gelu_gh(ko, ee=ee, gb=gb, gho=gho):
                            pp = 6 + (ko % 2)
                            run("scalar",
                                lambda e, ko=ko, ee=ee, pp=pp: e.activation(
                                    B(R_HS + (ko % 2) * 256, 256), pb[pp][:, 0:256],
                                    AF.Gelu, bias=Fp(F_B1 + ee * 4 + ko, 1)),
                                read=(bank[pp], bb1), write=(hsb[ko % 2],), inc="s")
                            run("vector",
                                lambda e, ko=ko, gb=gb, gho=gho: e.tensor_tensor(
                                    B(R_GH + gho + ko * 256, 256), B(R_HS + (ko % 2) * 256, 256),
                                    pb[gb][:, 0:256], ALU.mult),
                                read=(hsb[ko % 2], bank[gb]), write=(ghb[gho // 1024],), inc="v")

                        def emit_gb(ee=ee, half_t=half_t, gb=gb):
                            run("tensor",
                                lambda e, ee=ee, half_t=half_t, gb=gb: e.matmul(
                                    pb[gb][:, 0:256],
                                    B(R_SEL + ee * 128, 128, 0, 8),
                                    B(R_GFM + half_t * 256, 256, 0, 8),
                                    start=True, stop=True),
                                read=(gfm, selb), write=(bank[gb],), inc="t")

                        if gatesT_pending[0]:
                            h_group(0)
                            emit_gatesT()
                            gatesT_pending[0] = False
                            emit_gb()
                            gelu_gh(0)
                            for ko in range(1, 4):
                                h_group(ko)
                                gelu_gh(ko)
                        else:
                            emit_gb()
                            for ko in range(4):
                                h_group(ko)
                                gelu_gh(ko)
                        if ei >= 1:
                            w2_mms(EXPERT_ORDER[ei - 1], ei - 1)
                        if ln_next is not None and 2 <= ei <= 5:
                            if ei <= 3:
                                ln_accum(ln_next[0], 2 * (ei - 2))
                                ln_accum(ln_next[0], 2 * (ei - 2) + 1)
                            elif ei == 4:
                                ln_math(ln_next[0], want_xnorm=ln_next[1])
                            else:
                                for i4_ in range(4):
                                    ln_norm_tile(ln_next[0], i4_)
                    for qsub in range(2):
                        for half in range(2):
                            yb = 2 + qsub * 2 + half
                            run("tensor",
                                lambda e, qsub=qsub, half=half, yb=yb, half_t=half_t: e.matmul(
                                    pb[yb][:, 0:512],
                                    B(R_GFM + (half_t * 2 + qsub) * 128, 128, 0, 8),
                                    B(R_B2 + half * 512, 512, 0, 8),
                                    start=False, stop=False),
                                read=(gfm, bb2), write=(bank[yb],))
                    w2_mms(EXPERT_ORDER[7], 7, stop=True)
                    if qs == NQS - 1 and half_t == 1 and s < NSEQ - 1:
                        load_qkv_weights(staged=True)
                        load_wo(staged=True)
                    for qsub in range(2):
                        for half in range(2):
                            yb = 2 + qsub * 2 + half
                            tile = qs * 4 + half_t * 2 + qsub
                            run("vector",
                                lambda e, tile=tile, half=half, yb=yb: e.tensor_tensor(
                                    B(R_X + tile * 1024 + half * 512, 512),
                                    B(R_X + tile * 1024 + half * 512, 512),
                                    pb[yb][:, 0:512], ALU.add),
                                read=(bank[yb], xp[tile // 4]), write=(xp[tile // 4],), inc="v")
                dma("gpsimd",
                    out[s, qs * 512:(qs + 1) * 512, :].rearrange("(i p) d -> p i d", p=128),
                    B(R_X + qs * 4096, 4096).rearrange("p (i d) -> p i d", d=1024),
                    f"do{qs}", rbufs=(xp[qs],))
                if s < NSEQ - 1:
                    load_x(s + 1, qs)



        P.emit("gpsimd", lambda e: None,
               waits=[(f"do{i}", 16 * NSEQ) for i in range(4)])

        with nc.Block() as block:
            @block.tensor
            def _(eng):
                for op in P.ops["tensor"]:
                    op(eng, sems)

            @block.vector
            def _(eng):
                for op in P.ops["vector"]:
                    op(eng, sems)

            @block.scalar
            def _(eng):
                for op in P.ops["scalar"]:
                    op(eng, sems)

            @block.gpsimd
            def _(eng):
                for op in P.ops["gpsimd"]:
                    op(eng, sems)

            @block.sync
            def _(eng):
                for op in P.ops["sync"]:
                    op(eng, sems)

    return nc


_NC_CACHE = [None]


def kernel(L0_attn_hidden_states, L0_attn_Wq, L0_attn_Wk, L0_attn_Wv, L0_attn_Wo,
           L0_ffn_w1, L0_ffn_b1, L0_ffn_w2, L0_ffn_b2, L0_ffn_centroids,
           L0_ffn_w_route):
    x = np.ascontiguousarray(np.asarray(L0_attn_hidden_states, dtype=np.float32))
    consts = _host_consts()
    base = dict(wq=np.ascontiguousarray(L0_attn_Wq, dtype=np.float32),
                wk=np.ascontiguousarray(L0_attn_Wk, dtype=np.float32),
                wv=np.ascontiguousarray(L0_attn_Wv, dtype=np.float32),
                wo=np.ascontiguousarray(L0_attn_Wo, dtype=np.float32),
                w1=np.ascontiguousarray(L0_ffn_w1, dtype=np.float32),
                b1=np.ascontiguousarray(np.asarray(L0_ffn_b1, dtype=np.float32)
                                        .reshape(8, 4, 128).transpose(2, 0, 1)
                                        .reshape(128, 32)),
                w2=np.ascontiguousarray(L0_ffn_w2, dtype=np.float32),
                b2=np.ascontiguousarray(L0_ffn_b2, dtype=np.float32),
                cent=np.ascontiguousarray(L0_ffn_centroids, dtype=np.float32),
                wrt=np.ascontiguousarray(L0_ffn_w_route, dtype=np.float32),
                **consts)
    if _NC_CACHE[0] is None:
        _NC_CACHE[0] = build_nc()
    nc = _NC_CACHE[0]
    in_maps = []
    for c in range(NCORE):
        m = dict(base)
        m["x"] = np.ascontiguousarray(x[c * NSEQ:(c + 1) * NSEQ])
        in_maps.append(m)
    res = run_bass_kernel_spmd(nc, in_maps, core_ids=list(range(NCORE)))
    return np.concatenate([r["out"] for r in res.results], axis=0).reshape(32, T, DM)


# version 27
# speedup vs baseline: 1.0363x; 1.0363x over previous
import numpy as np
import ml_dtypes

import concourse.bass as bass
import concourse.mybir as mybir
from concourse.bass_utils import run_bass_kernel_spmd

BF16 = mybir.dt.bfloat16
F32 = mybir.dt.float32
AF = mybir.ActivationFunctionType
ALU = mybir.AluOpType
AXX = mybir.AxisListType.X

NCORE = 8
NSEQ = 4
T = 2048
DM = 1024
NH, NKV, HD = 8, 4, 128
DEXP, NEXP = 512, 8
SCALE = float(np.float32(1.0) / np.sqrt(np.float32(128.0)))
MASKVAL = -1.0e6

NQS = T // 512
NQB = T // 256

_off = [0]
def _alloc(n):
    o = _off[0]; _off[0] += n; return o

R_X    = _alloc(16 * 1024)
R_MASK = _alloc(384)
R_ID   = _alloc(128)
R_PROT = _alloc(128)
R_SEL  = _alloc(1024)
R_WCAT = _alloc(128)
R_B2   = _alloc(1024)
R_LNT  = _alloc(4096)
R_SQ   = _alloc(1024)
R_LNF  = _alloc(4096)
R_HS   = _alloc(512)
R_GH   = _alloc(2048)
R_GFM  = _alloc(512)
R_PT   = _alloc(2048)
R_QF   = _alloc(16384)
R_KF   = _alloc(8192)
R_VA   = _alloc(4 * 16 * 129)
R_E    = _alloc(32768)
NB = _off[0]

EXB = R_QF + 64
def exp_slot(e):
    return EXB + 8192 * e
W_QKV = exp_slot(4)
W_O = exp_slot(6)
assert exp_slot(4) == R_E
assert exp_slot(8) == NB

_f = [0]
def _falloc(n):
    o = _f[0]; _f[0] += n; return o
F_CENT = _falloc(1024)
F_ST   = _falloc(128)
F_REC  = _falloc(8)
F_GAT  = _falloc(128)
F_B1   = _falloc(32)
F_COS  = _falloc(512)
F_SIN  = _falloc(512)
NF = _f[0]
assert NB * 2 + NF * 4 <= 212000, (NB, NF)

def _st(parity):
    b = F_ST + 64 * parity
    return dict(SUM=b, SQ=b + 4, M=b + 8, MSQ=b + 12, V=b + 16, RS=b + 20,
                NMR=b + 24, SV=b + 28, XN=b + 32, IXN=b + 36, T0=b + 40, T1=b + 44)
GB_L, GB_T, GB_M1, GB_NM1, GB_M2, GB_S, GB_R, GB_E = (
    F_GAT + 0, F_GAT + 32, F_GAT + 64, F_GAT + 68, F_GAT + 72, F_GAT + 76, F_GAT + 80,
    F_GAT + 88)


def _host_consts():
    bf = ml_dtypes.bfloat16
    try:
        import jax
        import jax.numpy as jnp
        with jax.default_device(jax.devices("cpu")[0]):
            inv_freq = 1.0 / 10000.0 ** (jnp.arange(0, HD, 2, dtype=jnp.float32) / HD)
            freqs = jnp.outer(jnp.arange(T, dtype=jnp.float32), inv_freq)
            emb = jnp.concatenate((freqs, freqs), axis=-1)
            cos = np.asarray(jnp.cos(emb), dtype=np.float32)
            sin = np.asarray(jnp.sin(emb), dtype=np.float32)
    except Exception:
        inv_freq = (1.0 / (10000.0 ** (np.arange(0, HD, 2, dtype=np.float32) / np.float32(HD)))).astype(np.float32)
        freqs = np.outer(np.arange(T, dtype=np.float32), inv_freq).astype(np.float32)
        emb = np.concatenate([freqs, freqs], axis=1)
        cos = np.cos(emb).astype(np.float32)
        sin = np.sin(emb).astype(np.float32)
    sin_signed = sin.copy()
    sin_signed[:, :64] *= -1.0
    c_cos = np.ascontiguousarray(cos.T).astype(np.float32)
    c_sin = np.ascontiguousarray(sin_signed.T).astype(np.float32)
    p = np.arange(128)[:, None]
    v = np.arange(128, 512)[None, :]
    c_mask = np.where(p <= v - 256, 0.0, MASKVAL).astype(bf)
    c_id = np.eye(128, dtype=np.float32).astype(bf)
    prot = np.zeros((128, 128), dtype=np.float32)
    prot[np.arange(128), (np.arange(128) + 64) % 128] = 1.0
    c_prot = prot.astype(bf)
    sel = np.zeros((8, 8, 128), dtype=np.float32)
    for e in range(8):
        sel[e, e, :] = 1.0
    c_sel = sel.transpose(1, 0, 2).reshape(8, 1024).astype(bf)
    return dict(c_cos=c_cos, c_sin=c_sin, c_mask=c_mask, c_id=c_id,
                c_prot=c_prot, c_sel=np.ascontiguousarray(c_sel))


class Prog:
    ENGS = ("tensor", "vector", "scalar", "gpsimd", "sync")

    def __init__(self):
        self.ops = {e: [] for e in self.ENGS}
        self.cnt = {}
        self.known = {e: {} for e in self.ENGS}

    def emit(self, eng, fn, waits=(), inc=None, inc_by=1):
        w = []
        for s, v in waits:
            if v <= 0 or self.known[eng].get(s, 0) >= v:
                continue
            self.known[eng][s] = v
            w.append((s, v))
        val = None
        if inc is not None:
            self.cnt[inc] = self.cnt.get(inc, 0) + inc_by
            val = self.cnt[inc]

        def op(engine, sems, w=w, fn=fn, inc=inc, inc_by=inc_by):
            for s, v in w:
                engine.wait_ge(sems[s], v)
            ins = fn(engine)
            if inc is not None and ins is not None:
                ins.then_inc(sems[inc], inc_by)
        self.ops[eng].append(op)
        return (inc, val) if inc is not None else None

    def c(self, sem):
        return self.cnt.get(sem, 0)


class Buf:
    def __init__(self):
        self.w = {}
        self.r = {}

    @staticmethod
    def _m(d, ev):
        if ev is None:
            return
        s, v = ev
        if d.get(s, 0) < v:
            d[s] = v

    def wrote(self, ev):
        self._m(self.w, ev)

    def read(self, ev):
        self._m(self.r, ev)

    @property
    def need_for_read(self):
        return list(self.w.items())

    @property
    def need_for_write(self):
        out = dict(self.w)
        for s, v in self.r.items():
            if out.get(s, 0) < v:
                out[s] = v
        return list(out.items())


def build_nc():
    nc = bass.Bass("TRN2", target_bir_lowering=False)
    xin = nc.dram_tensor("x", [NSEQ, T, DM], F32, kind="ExternalInput").ap()
    wq = nc.dram_tensor("wq", [DM, NH * HD], F32, kind="ExternalInput").ap()
    wk = nc.dram_tensor("wk", [DM, NKV * HD], F32, kind="ExternalInput").ap()
    wv = nc.dram_tensor("wv", [DM, NKV * HD], F32, kind="ExternalInput").ap()
    wo = nc.dram_tensor("wo", [NH * HD, DM], F32, kind="ExternalInput").ap()
    w1 = nc.dram_tensor("w1", [NEXP, DM, DEXP], F32, kind="ExternalInput").ap()
    b1d = nc.dram_tensor("b1", [128, 32], F32, kind="ExternalInput").ap()
    w2 = nc.dram_tensor("w2", [NEXP, DEXP, DM], F32, kind="ExternalInput").ap()
    b2d = nc.dram_tensor("b2", [NEXP, DM], F32, kind="ExternalInput").ap()
    cent = nc.dram_tensor("cent", [NEXP, DM], F32, kind="ExternalInput").ap()
    wrtd = nc.dram_tensor("wrt", [NEXP, DM], F32, kind="ExternalInput").ap()
    cc = {}
    for n, shp in (("c_mask", [128, 384]), ("c_id", [128, 128]),
                   ("c_prot", [128, 128]), ("c_sel", [8, 1024])):
        cc[n] = nc.dram_tensor(n, shp, BF16, kind="ExternalInput").ap()
    for n in ("c_cos", "c_sin"):
        cc[n] = nc.dram_tensor(n, [128, T], F32, kind="ExternalInput").ap()
    out = nc.dram_tensor("out", [NSEQ, T, DM], F32, kind="ExternalOutput").ap()
    wstage = nc.dram_tensor("wstage", [128, 24576 + NEXP * 8192], BF16, kind="Internal").ap()

    import contextlib
    with contextlib.ExitStack() as ctx:
        AB = ctx.enter_context(nc.sbuf_tensor("AB", [128, NB], BF16))
        AFP = ctx.enter_context(nc.sbuf_tensor("AFp", [128, NF], F32))
        pb = [ctx.enter_context(nc.psum_tensor(f"pb{i}", [128, 512], F32))
              for i in range(8)]
        semnames = (("t", "v", "s", "g", "dc", "dcs", "dwi", "dcen", "dwq", "dwo", "hwq", "hwo")
                    + tuple(f"dx{i}" for i in range(4))
                    + tuple(f"do{i}" for i in range(4))
                    + tuple(f"dwe{e}" for e in range(NEXP))
                    + tuple(f"hwe{e}" for e in range(NEXP)))
        sems = {n: ctx.enter_context(nc.semaphore(f"sem_{n}")) for n in semnames}

        P = Prog()
        ab = AB.ap()
        af = AFP.ap()

        def B(off, n, p0=0, p1=128):
            return ab[p0:p1, off:off + n]

        def Fp(off, n, p0=0, p1=128):
            return af[p0:p1, off:off + n]

        bank = [Buf() for _ in range(8)]
        bufs = {}
        def buf(name):
            if name not in bufs:
                bufs[name] = Buf()
            return bufs[name]

        def dma(queue, outap, inap, sem, rbufs=(), wbufs=()):
            waits = {}
            for b in rbufs:
                for s, v in b.need_for_read:
                    waits[s] = max(waits.get(s, 0), v)
            for b in wbufs:
                for s, v in b.need_for_write:
                    waits[s] = max(waits.get(s, 0), v)
            ev = P.emit(queue, lambda e, o=outap, i=inap: e.dma_start(out=o, in_=i),
                        waits=list(waits.items()), inc=sem, inc_by=16)
            for b in rbufs:
                b.read(ev)
            for b in wbufs:
                b.wrote(ev)
            return ev

        def run(eng, fn, read=(), write=(), extra=(), inc=None):
            waits = {}
            for b in read:
                for s, v in b.need_for_read:
                    waits[s] = max(waits.get(s, 0), v)
            for b in write:
                for s, v in b.need_for_write:
                    waits[s] = max(waits.get(s, 0), v)
            for s, v in extra:
                waits[s] = max(waits.get(s, 0), v)
            ev = P.emit(eng, fn, waits=list(waits.items()), inc=inc)
            for b in read:
                b.read(ev if ev is not None else (eng_sem(eng), P.c(eng_sem(eng))))
            for b in write:
                b.wrote(ev)
            return ev

        def eng_sem(eng):
            return {"tensor": "t", "vector": "v", "scalar": "s", "gpsimd": "g"}.get(eng, "t")

        cosb, sinb, maskb, idb, protb, selb = (buf(x) for x in
            ("cos", "sin", "mask", "id", "prot", "sel"))
        wqkv, wob, wcat, bb1, bb2 = (buf(x) for x in ("wqkv", "wob", "wcat", "b1", "b2"))
        ex = [buf(f"ex{e}") for e in range(NEXP)]
        xp = [buf(f"xp{i}") for i in range(4)]
        qf = [buf(f"qf{j}") for j in range(4)]
        kf = [buf(f"kf{j}") for j in range(4)]
        vaug = [buf(f"va{j}") for j in range(4)]
        lnt = [buf("lnt_a"), buf("lnt_b"), buf("lnt_c")]
        sq = buf("sq")
        lnf = [buf(f"lnf{k}") for k in range(8)]
        ptb = [buf(f"pt{i}") for i in range(4)]
        hsb = [buf(f"hs{i}") for i in range(2)]
        ghb, gfm = [buf("gh0"), buf("gh1")], buf("gfm")
        stats, gat, orec, cstage = buf("stats"), buf("gat"), buf("orec"), buf("cstage")

        dma("sync", B(R_MASK, 384), cc["c_mask"], "dc", wbufs=(maskb,))
        dma("sync", B(R_ID, 128), cc["c_id"], "dc", wbufs=(idb,))
        dma("sync", B(R_PROT, 128), cc["c_prot"], "dc", wbufs=(protb,))
        dma("sync", B(R_SEL, 1024, 0, 8), cc["c_sel"], "dc", wbufs=(selb,))
        for b_ in (cosb, sinb, maskb, idb, protb, selb):
            b_.wrote(("dc", P.c("dc")))

        for part_ in range(4):
            dma("gpsimd",
                B(R_X + part_ * 4096, 4096).rearrange("p (i d) -> p i d", d=1024),
                xin[0, part_ * 512:(part_ + 1) * 512, :].rearrange("(i p) d -> p i d", p=128),
                f"dx{part_}", wbufs=(xp[part_],))
        dma("gpsimd", Fp(F_B1, 32), b1d, "dwi", wbufs=(bb1,))
        dma("gpsimd", B(R_B2, 1024, 0, 8), b2d, "dwi", wbufs=(bb2,))
        bb1.wrote(("dwi", P.c("dwi")))
        bb2.wrote(("dwi", P.c("dwi")))

        ones_ap = ab[:, R_VA + 128: R_VA + 64 * 129: 129]

        for idx, (src, colo) in enumerate(((cent, 0), (wrtd, 8))):
            dma("gpsimd", Fp(F_CENT, 1024, 0, 8), src, "dcen", wbufs=(cstage,))
            if idx == 0:
                run("scalar",
                    lambda e: e.activation(B(R_LNT, 1024, 0, 8), Fp(F_CENT, 1024, 0, 8),
                                           AF.Square, accum_out=Fp(F_GAT, 1, 0, 8)),
                    read=(cstage,), write=(lnt[0], gat), inc="s")
                run("scalar", lambda e: e.activation(Fp(F_GAT + 1, 1, 0, 8), Fp(F_GAT, 1, 0, 8), AF.Ln),
                    read=(gat,), write=(gat,), inc="s")
                run("scalar", lambda e: e.activation(Fp(F_GAT + 2, 1, 0, 8), Fp(F_GAT + 1, 1, 0, 8),
                                                     AF.Exp, scale=0.5),
                    read=(gat,), write=(gat,), inc="s")
                run("vector", lambda e: e.tensor_scalar_max(Fp(F_GAT + 3, 1, 0, 8), Fp(F_GAT + 2, 1, 0, 8), 1e-12),
                    read=(gat,), write=(gat,), inc="v")
                run("vector", lambda e: e.reciprocal(Fp(F_GAT + 4, 1, 0, 8), Fp(F_GAT + 3, 1, 0, 8)),
                    read=(gat,), write=(gat,), inc="v")
                run("vector",
                    lambda e: e.tensor_scalar(B(R_LNT, 1024, 0, 8), Fp(F_CENT, 1024, 0, 8),
                                              Fp(F_GAT + 4, 1, 0, 8), None, ALU.mult),
                    read=(cstage, gat), write=(lnt[0],), inc="v")
            else:
                run("vector", lambda e: e.tensor_copy(B(R_LNT, 1024, 0, 8), Fp(F_CENT, 1024, 0, 8)),
                    read=(cstage,), write=(lnt[0],), inc="v")
            for k in range(8):
                run("tensor",
                    lambda e, k=k: e.matmul(pb[0][:, 0:8], B(R_LNT + k * 128, 128, 0, 8),
                                            B(R_ID, 8, 0, 8), start=True, stop=True),
                    read=(lnt[0], idb), write=(bank[0],), inc="t")
                run("scalar",
                    lambda e, k=k, c=colo: e.copy(B(R_WCAT + k * 16 + c, 8), pb[0][:, 0:8]),
                    read=(bank[0],), write=(wcat,), inc="s")

        stg_qkv, stg_wo = buf("stg_qkv"), buf("stg_wo")
        stg_ex = [buf(f"stg_ex{e}") for e in range(NEXP)]

        def load_qkv_weights(staged):
            if staged:
                dma("sync", B(W_QKV, 16384), wstage[:, 0:16384], "hwq",
                    rbufs=(stg_qkv,), wbufs=(wqkv, ex[4], ex[5]))
            else:
                dma("gpsimd", B(W_QKV, 8192).rearrange("p (k n) -> p k n", n=1024),
                    wq.rearrange("(k p) n -> p k n", p=128), "dwq", wbufs=(wqkv, ex[4]))
                dma("gpsimd", B(W_QKV + 8192, 4096).rearrange("p (k n) -> p k n", n=512),
                    wk.rearrange("(k p) n -> p k n", p=128), "dwq", wbufs=(wqkv, ex[4], ex[5]))
                dma("gpsimd", B(W_QKV + 12288, 4096).rearrange("p (k n) -> p k n", n=512),
                    wv.rearrange("(k p) n -> p k n", p=128), "dwq", wbufs=(wqkv, ex[5]))
            wqkv.wrote(("dwq", P.c("dwq")))

        def stage_qkv_wo():
            dma("sync", wstage[:, 0:16384], B(W_QKV, 16384), "hwq", rbufs=(wqkv,),
                wbufs=(stg_qkv,))
            dma("sync", wstage[:, 16384:24576], B(W_O, 8192), "hwo", rbufs=(wob,),
                wbufs=(stg_wo,))

        def load_wo(staged):
            if staged:
                dma("sync", B(W_O, 8192), wstage[:, 16384:24576], "hwo",
                    rbufs=(stg_wo,), wbufs=(wob, ex[6]))
            else:
                dma("gpsimd", B(W_O, 8192).rearrange("p (k n) -> p k n", n=1024),
                    wo.rearrange("(k p) n -> p k n", p=128), "dwo", wbufs=(wob, ex[6]))

        def load_expert(e, staged):
            base = exp_slot(e)
            confl = [ex[e]]
            if e in (4, 5):
                confl.append(wqkv)
            elif e == 6:
                confl.append(wob)
            elif e <= 3:
                confl += qf + kf + vaug + [ptb[0], ptb[1]]
            if staged:
                dma("sync", B(base, 8192), wstage[:, 24576 + e * 8192: 24576 + (e + 1) * 8192],
                    f"hwe{e}", rbufs=(stg_ex[e],), wbufs=tuple(confl))
            else:
                dma("gpsimd", B(base, 4096).rearrange("p (k n) -> p k n", n=512),
                    w1[e].rearrange("(k p) n -> p k n", p=128), f"dwe{e}", wbufs=tuple(confl))
                dma("gpsimd", B(base + 4096, 4096).rearrange("p (k n) -> p k n", n=1024),
                    w2[e].rearrange("(k p) n -> p k n", p=128), f"dwe{e}", wbufs=tuple(confl))
                dma("sync", wstage[:, 24576 + e * 8192: 24576 + (e + 1) * 8192], B(base, 8192),
                    f"hwe{e}", rbufs=tuple([ex[e]] + confl[1:]), wbufs=(stg_ex[e],))
            ex[e].wrote((f"dwe{e}", P.c(f"dwe{e}")))

        load_qkv_weights(staged=False)
        load_wo(staged=False)
        stage_qkv_wo()

        EXPERT_ORDER = [4, 5, 7, 6, 0, 1, 2, 3]

        statsb = [buf("stats0"), buf("stats1")]

        def ln_accum(qs, i4):
            pr = qs % 2
            S = _st(pr)
            st = statsb[pr]
            tile = qs * 4 + i4
            xap = B(R_X + tile * 1024, 1024)
            run("vector", lambda e, i4=i4, xap=xap, S=S: e.reduce_sum(Fp(S["SUM"] + i4, 1), xap, AXX),
                read=(xp[tile // 4],), write=(st,), inc="v")
            run("scalar",
                lambda e, i4=i4, xap=xap, S=S: e.activation(B(R_LNT + 3072, 1024), xap, AF.Square,
                                                           accum_out=Fp(S["SQ"] + i4, 1)),
                read=(xp[tile // 4],), write=(lnt[2], st), inc="s")

        def ln_math(qs, want_xnorm, lo=0, hi=4):
            pr = qs % 2
            S = _st(pr)
            st = statsb[pr]
            n_ = hi - lo
            run("vector", lambda e, S=S, lo=lo, n_=n_: e.tensor_scalar(Fp(S["M"] + lo, n_), Fp(S["SUM"] + lo, n_), 1.0 / 1024.0, None, ALU.mult),
                read=(st,), write=(st,), inc="v")
            run("vector", lambda e, S=S, lo=lo, n_=n_: e.tensor_scalar(Fp(S["MSQ"] + lo, n_), Fp(S["SQ"] + lo, n_), 1.0 / 1024.0, None, ALU.mult),
                read=(st,), write=(st,), inc="v")
            run("vector", lambda e, S=S, lo=lo, n_=n_: e.tensor_tensor(Fp(S["T0"] + lo, n_), Fp(S["M"] + lo, n_), Fp(S["M"] + lo, n_), ALU.mult),
                read=(st,), write=(st,), inc="v")
            run("vector", lambda e, S=S, lo=lo, n_=n_: e.tensor_tensor(Fp(S["V"] + lo, n_), Fp(S["MSQ"] + lo, n_), Fp(S["T0"] + lo, n_), ALU.subtract),
                read=(st,), write=(st,), inc="v")
            run("vector", lambda e, S=S, lo=lo, n_=n_: e.tensor_scalar(Fp(S["T1"] + lo, n_), Fp(S["V"] + lo, n_), 1e-5, None, ALU.add),
                read=(st,), write=(st,), inc="v")
            run("scalar", lambda e, S=S, lo=lo, n_=n_: e.activation(Fp(S["T0"] + lo, n_), Fp(S["T1"] + lo, n_), AF.Ln),
                read=(st,), write=(st,), inc="s")
            run("scalar", lambda e, S=S, lo=lo, n_=n_: e.activation(Fp(S["RS"] + lo, n_), Fp(S["T0"] + lo, n_), AF.Exp, scale=-0.5),
                read=(st,), write=(st,), inc="s")
            run("vector", lambda e, S=S, lo=lo, n_=n_: e.scalar_tensor_tensor(Fp(S["NMR"] + lo, n_), Fp(S["M"] + lo, n_), -1.0, Fp(S["RS"] + lo, n_),
                                                                ALU.mult, ALU.mult),
                read=(st,), write=(st,), inc="v")
            if want_xnorm:
                run("scalar", lambda e, S=S, lo=lo, n_=n_: e.activation(Fp(S["T1"] + lo, n_), Fp(S["V"] + lo, n_), AF.Ln),
                    read=(st,), write=(st,), inc="s")
                run("scalar", lambda e, S=S, lo=lo, n_=n_: e.activation(Fp(S["SV"] + lo, n_), Fp(S["T1"] + lo, n_), AF.Exp, scale=0.5),
                    read=(st,), write=(st,), inc="s")
                run("vector", lambda e, S=S, lo=lo, n_=n_: e.tensor_tensor(Fp(S["XN"] + lo, n_), Fp(S["SV"] + lo, n_), Fp(S["RS"] + lo, n_), ALU.mult),
                    read=(st,), write=(st,), inc="v")
                run("vector", lambda e, S=S, lo=lo, n_=n_: e.tensor_scalar(Fp(S["XN"] + lo, n_), Fp(S["XN"] + lo, n_), 32.0, 1e-12, ALU.mult, ALU.max),
                    read=(st,), write=(st,), inc="v")
                run("vector", lambda e, S=S, lo=lo, n_=n_: e.reciprocal(Fp(S["IXN"] + lo, n_), Fp(S["XN"] + lo, n_)),
                    read=(st,), write=(st,), inc="v")

        def ln_stats(qs, want_xnorm):
            for i4 in range(4):
                ln_accum(qs, i4)
            ln_math(qs, want_xnorm)

        def ln_norm(qs):
            pr = qs % 2
            S = _st(pr)
            st = statsb[pr]
            for i4 in range(4):
                ln_norm_tile(qs, i4)

        def ln_norm_tile(qs, i4):
            pr = qs % 2
            S = _st(pr)
            st = statsb[pr]
            tile = qs * 4 + i4
            lb = lnt[0] if i4 < 2 else (lnt[1] if i4 == 2 else lnt[2])
            run("vector",
                lambda e, i4=i4, tile=tile, S=S: e.tensor_scalar(
                    B(R_LNT + i4 * 1024, 1024), B(R_X + tile * 1024, 1024),
                    Fp(S["RS"] + i4, 1), Fp(S["NMR"] + i4, 1), ALU.mult, ALU.add),
                read=(xp[tile // 4], st), write=(lb,), inc="v")

        def ln_trans(qs):
            for k in range(8):
                tb = k % 2
                for i4 in range(4):
                    lb = lnt[0] if i4 < 2 else (lnt[1] if i4 == 2 else lnt[2])
                    run("tensor",
                        lambda e, k=k, i4=i4, tb=tb: e.matmul(
                            pb[tb][:, i4 * 128:(i4 + 1) * 128],
                            B(R_LNT + i4 * 1024 + k * 128, 128), B(R_ID, 128),
                            start=True, stop=True),
                        read=(lb, idb), write=(bank[tb],),
                        inc="t" if i4 == 3 else None)
                run("scalar", lambda e, k=k, tb=tb: e.copy(B(R_LNF + k * 512, 512), pb[tb][:, 0:512]),
                    read=(bank[tb],), write=(lnf[k],), inc="s")

        def load_x(s, part):
            dma("gpsimd",
                B(R_X + part * 4096, 4096).rearrange("p (i d) -> p i d", d=1024),
                xin[s, part * 512:(part + 1) * 512, :].rearrange("(i p) d -> p i d", p=128),
                f"dx{part}", wbufs=(xp[part],))

        for s in range(NSEQ):
            run("vector", lambda e: e.memset(ones_ap, 1.0), write=tuple(vaug), inc="v")

            for qs in range(NQS):
                if qs == 0 and s == 0:
                    ln_stats(0, want_xnorm=False)
                    ln_norm(0)
                dma("sync", Fp(F_COS, 512), cc["c_cos"][:, qs * 512:(qs + 1) * 512],
                    "dcs", wbufs=(cosb,))
                dma("sync", Fp(F_SIN, 512), cc["c_sin"][:, qs * 512:(qs + 1) * 512],
                    "dcs", wbufs=(sinb,))
                cosb.wrote(("dcs", P.c("dcs")))
                sinb.wrote(("dcs", P.c("dcs")))
                ln_trans(qs)
                cosap = Fp(F_COS, 512)
                sinap = Fp(F_SIN, 512)

                def qkv_epilogue(m, qs=qs):
                    pp = 6 + (m % 2)
                    rb = 2 + (m % 2)
                    sl = (m % 2) * 512
                    if m < 8:
                        dstoff, dstbuf = R_QF + m * T + qs * 512, qf[qs]
                    else:
                        dstoff, dstbuf = R_KF + (m - 8) * T + qs * 512, kf[qs]
                    run("scalar", lambda e, pp=pp, sl=sl: e.copy(B(R_SQ + sl, 512), pb[pp][:, 0:512]),
                        read=(bank[pp],), write=(sq,), inc="s")
                    run("tensor",
                        lambda e, rb=rb, sl=sl: e.matmul(pb[rb][:, 0:512], B(R_PROT, 128),
                                                         B(R_SQ + sl, 512), start=True, stop=True),
                        read=(sq, protb), write=(bank[rb],), inc="t")
                    run("vector",
                        lambda e, cosap=cosap, sl=sl: e.tensor_tensor(
                            B(R_GH + 1024 + sl, 512), B(R_SQ + sl, 512), cosap, ALU.mult),
                        read=(sq, cosb), write=(ghb[1],), inc="v")
                    run("vector",
                        lambda e, sinap=sinap, rb=rb, sl=sl: e.tensor_tensor(
                            B(R_GH + sl, 512), pb[rb][:, 0:512], sinap, ALU.mult),
                        read=(bank[rb], sinb), write=(ghb[0],), inc="v")
                    run("vector",
                        lambda e, dstoff=dstoff, sl=sl: e.tensor_tensor(
                            B(dstoff, 512), B(R_GH + 1024 + sl, 512),
                            B(R_GH + sl, 512), ALU.add),
                        read=(ghb[0], ghb[1]), write=(dstbuf,), inc="v")

                for m in range(12):
                    pp = 6 + (m % 2)
                    if m < 8:
                        wbase, wstride, mm = W_QKV, 1024, m
                    else:
                        wbase, wstride, mm = W_QKV + 8192, 512, m - 8
                    for k in range(8):
                        run("tensor",
                            lambda e, k=k, mm=mm, pp=pp, wbase=wbase, wstride=wstride:
                            e.matmul(pb[pp][:, 0:512],
                                     B(wbase + k * wstride + mm * 128, 128),
                                     B(R_LNF + k * 512, 512),
                                     start=(k == 0), stop=(k == 7)),
                            read=(lnf[k], wqkv), write=(bank[pp],),
                            inc="t" if k == 7 else None)
                    if m >= 1:
                        qkv_epilogue(m - 1)
                qkv_epilogue(11)

                def v_evac(i4, qs=qs):
                    pp = 6 + (i4 % 2)
                    j = qs * 4 + i4
                    for g in range(4):
                        run("vector",
                            lambda e, g=g, j=j, pp=pp: e.tensor_copy(
                                B(R_VA + (g * 16 + j) * 129, 128), pb[pp][:, g * 128:(g + 1) * 128]),
                            read=(bank[pp],), write=(vaug[qs],), inc="v")

                for i4 in range(4):
                    pp = 6 + (i4 % 2)
                    for k in range(8):
                        run("tensor",
                            lambda e, k=k, i4=i4, pp=pp: e.matmul(
                                pb[pp][:, 0:512],
                                B(R_LNF + k * 512 + i4 * 128, 128),
                                B(W_QKV + 12288 + k * 512, 512),
                                start=(k == 0), stop=(k == 7)),
                            read=(lnf[k], wqkv), write=(bank[pp],),
                            inc="t" if k == 7 else None)
                    if i4 >= 1:
                        v_evac(i4 - 1)
                        if qs < NQS - 1:
                            ln_accum(qs + 1, i4 - 1)
                    if i4 == 3 and qs < NQS - 1:
                        ln_accum(qs + 1, 3)
                        ln_math(qs + 1, want_xnorm=False)
                        for i4_ in range(4):
                            ln_norm_tile(qs + 1, i4_)
                v_evac(3)

            pend_otrans = [None]
            p2_fill = []
            for qb in range(NQB):
                if qb == 1:
                    load_expert(4, staged=(s > 0))
                    load_expert(5, staged=(s > 0))
                    if s == 0:
                        load_expert(7, staged=False)
                if qb == 2:
                    p2_fill.extend(
                        [lambda i4_=i4_: ln_accum(0, i4_) for i4_ in range(4)]
                        + [lambda: ln_math(0, want_xnorm=True)]
                        + [lambda i4_=i4_: ln_norm_tile(0, i4_) for i4_ in range(4)])
                nch = 2 * (qb + 1)

                def emit_otrans(g, tb=None):
                    tb = 6 + (g % 2) if tb is None else tb
                    for ri in range(4):
                        run("tensor",
                            lambda e, ri=ri, g=g, tb=tb: e.matmul(
                                pb[tb][:, ri * 128:(ri + 1) * 128],
                                B(R_SQ + (g % 2) * 512 + ri * 128, 128), B(R_ID, 128),
                                start=True, stop=True),
                            read=(sq, idb), write=(bank[tb],),
                            inc="t" if ri == 3 else None)
                    run("vector",
                        lambda e, g=g, tb=tb: e.tensor_copy(B(R_LNF + g * 512, 512), pb[tb][:, 0:512]),
                        read=(bank[tb],), write=(lnf[g],), inc="v")

                for g in range(4):
                    def s_chunk(c, g=g, qb=qb):
                        sb = c % 2
                        pt = c % 4
                        diag = (c >= 2 * qb)
                        if diag:
                            delta = 128 * c - 256 * qb
                            run("tensor",
                                lambda e, sb=sb, delta=delta: e.matmul(
                                    pb[sb][:, 0:512].rearrange("p (h n) -> p h n", n=256),
                                    B(R_ID, 128),
                                    B(R_MASK + 128 - delta, 256).unsqueeze(1).broadcast_to([128, 2, 256]),
                                    start=True, stop=False),
                                read=(idb, maskb), write=(bank[sb],))
                        run("tensor",
                            lambda e, sb=sb, c=c, g=g, qb=qb, diag=diag: e.matmul(
                                pb[sb][:, 0:512].rearrange("p (h n) -> p h n", n=256),
                                B(R_KF + g * T + c * 128, 128),
                                B(R_QF + 2 * g * T + qb * 256, 2 * T).rearrange(
                                    "p (h t) -> p h t", t=T)[:, :, 0:256],
                                start=not diag, stop=True),
                            read=(kf[c // 4], qf[qb // 2]), write=(bank[sb],), inc="t")
                        run("scalar",
                            lambda e, sb=sb, pt=pt: e.activation(B(R_PT + pt * 512, 512), pb[sb][:, 0:512],
                                                                 AF.Exp, scale=SCALE),
                            read=(bank[sb],), write=(ptb[pt],), inc="s")

                    def pv_chunk(c, g=g, nch=nch):
                        pt = c % 4
                        for hl in range(2):
                            for qsub in range(2):
                                ob = 2 + hl * 2 + qsub
                                run("tensor",
                                    lambda e, pt=pt, hl=hl, qsub=qsub, ob=ob, c=c, g=g, nch=nch:
                                    e.matmul(pb[ob][:, 0:129],
                                             B(R_PT + pt * 512 + hl * 256 + qsub * 128, 128),
                                             B(R_VA + (g * 16 + c) * 129, 129),
                                             start=(c == 0), stop=(c == nch - 1)),
                                    read=(ptb[pt], vaug[c // 4]), write=(bank[ob],),
                                    inc="t" if c == nch - 1 else None)

                    PVD = 3
                    ot_pt = min(3, nch - 1)
                    for c in range(nch):
                        s_chunk(c)
                        if c == ot_pt and pend_otrans[0] is not None:
                            emit_otrans(pend_otrans[0])
                            pend_otrans[0] = None
                        if c >= PVD:
                            pv_chunk(c - PVD)
                    for c in range(max(0, nch - PVD), nch):
                        pv_chunk(c)
                    for hl in range(2):
                        for qsub in range(2):
                            ob = 2 + hl * 2 + qsub
                            ri = hl * 2 + qsub + (g % 2) * 4
                            run("vector",
                                lambda e, ob=ob, ri=ri: e.reciprocal(Fp(F_REC + ri, 1), pb[ob][:, 128:129]),
                                read=(bank[ob],), write=(orec,), inc="v")
                            run("vector",
                                lambda e, ob=ob, ri=ri, g=g, hl=hl, qsub=qsub: e.tensor_scalar(
                                    B(R_SQ + (g % 2) * 512 + (hl * 2 + qsub) * 128, 128),
                                    pb[ob][:, 0:128], Fp(F_REC + ri, 1), None, ALU.mult),
                                read=(bank[ob], orec), write=(sq,), inc="v")
                    pend_otrans[0] = g
                    if p2_fill:
                        p2_fill.pop(0)()
                for qsub in range(2):
                    tile = qb * 2 + qsub
                    for half in range(2):
                        pp = 6 + half
                        for hc in range(8):
                            run("tensor",
                                lambda e, hc=hc, qsub=qsub, half=half, pp=pp: e.matmul(
                                    pb[pp][:, 0:512],
                                    B(R_LNF + hc * 256 + qsub * 128, 128),
                                    B(W_O + hc * 1024 + half * 512, 512),
                                    start=(hc == 0), stop=(hc == 7)),
                                read=(lnf[hc // 2], wob), write=(bank[pp],),
                                inc="t" if hc == 7 else None)
                            if qsub == 0 and half == 0 and hc == 5 and pend_otrans[0] is not None:
                                emit_otrans(pend_otrans[0], tb=0)
                                pend_otrans[0] = None
                        run("vector",
                            lambda e, tile=tile, half=half, pp=pp: e.tensor_tensor(
                                B(R_X + tile * 1024 + half * 512, 512),
                                B(R_X + tile * 1024 + half * 512, 512),
                                pb[pp][:, 0:512], ALU.add),
                            read=(bank[pp], xp[tile // 4]), write=(xp[tile // 4],), inc="v")

            for e_ in (6, 0, 1, 2, 3):
                load_expert(e_, staged=(s > 0))

            for qs in range(NQS):
                ln_trans(qs)
                pr = qs % 2
                S = _st(pr)
                st = statsb[pr]
                for sub in range(4):
                    for k in range(8):
                        run("tensor",
                            lambda e, k=k, sub=sub: e.matmul(
                                pb[0][:, sub * 128:sub * 128 + 16],
                                B(R_LNF + k * 512 + sub * 128, 128),
                                B(R_WCAT + k * 16, 16),
                                start=(k == 0), stop=(k == 7)),
                            read=(lnf[k], wcat), write=(bank[0],),
                            inc="t" if k == 7 else None)
                cosp = pb[0][:, 0:512].rearrange("p (s c) -> p s c", c=128)[:, :, 0:8]
                routep = pb[0][:, 0:512].rearrange("p (s c) -> p s c", c=128)[:, :, 8:16]
                def b3(off, n1=4, n2=8):
                    return Fp(off, n1 * n2).rearrange("p (s c) -> p s c", c=n2)
                def bc(off):
                    return Fp(off, 4).unsqueeze(2).broadcast_to([128, 4, 8])
                run("vector",
                    lambda e, S=S: e.tensor_tensor(b3(GB_T), cosp, bc(S["IXN"]), ALU.mult),
                    read=(bank[0], st), write=(gat,), inc="v")
                run("vector", lambda e: e.tensor_tensor(b3(GB_L), b3(GB_T), routep, ALU.add),
                    read=(bank[0], gat), write=(gat,), inc="v")
                run("vector", lambda e: e.reduce_max(Fp(GB_M1, 4), b3(GB_L), AXX),
                    read=(gat,), write=(gat,), inc="v")
                run("vector", lambda e: e.tensor_scalar(Fp(GB_NM1, 4), Fp(GB_M1, 4), -1.0, None, ALU.mult),
                    read=(gat,), write=(gat,), inc="v")
                run("vector", lambda e: e.tensor_tensor(b3(GB_L), b3(GB_L), bc(GB_NM1), ALU.add),
                    read=(gat,), write=(gat,), inc="v")
                run("vector", lambda e: e.tensor_scalar(b3(GB_T), b3(GB_L), 0.0, None, ALU.is_ge),
                    read=(gat,), write=(gat,), inc="v")
                run("vector", lambda e: e.scalar_tensor_tensor(b3(GB_T), b3(GB_T), -1e30, b3(GB_L),
                                                               ALU.mult, ALU.add),
                    read=(gat,), write=(gat,), inc="v")
                run("vector", lambda e: e.reduce_max(Fp(GB_M2, 4), b3(GB_T), AXX),
                    read=(gat,), write=(gat,), inc="v")
                run("vector", lambda e: e.tensor_tensor(b3(GB_T), b3(GB_L), bc(GB_M2), ALU.is_ge),
                    read=(gat,), write=(gat,), inc="v")
                run("scalar", lambda e: e.activation(b3(GB_E), b3(GB_L), AF.Exp),
                    read=(gat,), write=(gat,), inc="s")
                run("vector", lambda e: e.tensor_tensor(b3(GB_L), b3(GB_E), b3(GB_T), ALU.mult),
                    read=(gat,), write=(gat,), inc="v")
                run("vector", lambda e: e.reduce_sum(Fp(GB_S, 4), b3(GB_L), AXX),
                    read=(gat,), write=(gat,), inc="v")
                run("vector", lambda e: e.reciprocal(Fp(GB_R, 4), Fp(GB_S, 4)),
                    read=(gat,), write=(gat,), inc="v")
                run("vector",
                    lambda e: e.tensor_tensor(
                        B(R_SQ, 32).rearrange("p (s c) -> p s c", c=8),
                        b3(GB_L), bc(GB_R), ALU.mult),
                    read=(gat,), write=(sq,), inc="v")
                def emit_gatesT():
                    for sub in range(4):
                        run("tensor",
                            lambda e, sub=sub: e.matmul(
                                pb[1][0:8, sub * 128:(sub + 1) * 128],
                                B(R_SQ + sub * 8, 8), B(R_ID, 128),
                                start=True, stop=True),
                            read=(sq, idb), write=(bank[1],),
                            inc="t" if sub == 3 else None)
                    run("scalar", lambda e: e.copy(B(R_GFM, 512, 0, 8), pb[1][0:8, 0:512]),
                        read=(bank[1],), write=(gfm,), inc="s")
                gatesT_pending = [True]

                for half_t in range(2):
                    if half_t == 1 and qs < NQS - 1:
                        ln_next = (qs + 1, True)
                    elif half_t == 1 and qs == NQS - 1 and s < NSEQ - 1:
                        ln_next = (0, False)
                    else:
                        ln_next = None
                    def w2_mms(ee, ei, stop=False):
                        base = exp_slot(ee)
                        gho = (ei % 2) * 1024
                        for qsub in range(2):
                            for half in range(2):
                                yb = 2 + qsub * 2 + half
                                for ko in range(4):
                                    run("tensor",
                                        lambda e, qsub=qsub, half=half, ko=ko, yb=yb, base=base,
                                               gho=gho, st=(ei == 0 and ko == 0),
                                               sp=(stop and ko == 3):
                                        e.matmul(pb[yb][:, 0:512],
                                                 B(R_GH + gho + ko * 256 + qsub * 128, 128),
                                                 B(base + 4096 + ko * 1024 + half * 512, 512),
                                                 start=st, stop=sp),
                                        read=(ghb[gho // 1024], ex[ee]), write=(bank[yb],),
                                        inc="t" if ko == 3 else None)

                    for ei, ee in enumerate(EXPERT_ORDER):
                        base = exp_slot(ee)
                        gb = ei % 2
                        gho = (ei % 2) * 1024

                        def h_group(ko, base=base, ee=ee, half_t=half_t):
                            pp = 6 + (ko % 2)
                            for k in range(8):
                                run("tensor",
                                    lambda e, k=k, ko=ko, base=base, half_t=half_t, pp=pp: e.matmul(
                                        pb[pp][:, 0:256],
                                        B(base + k * 512 + ko * 128, 128),
                                        B(R_LNF + k * 512 + half_t * 256, 256),
                                        start=(k == 0), stop=(k == 7)),
                                    read=(lnf[k], ex[ee]), write=(bank[pp],),
                                    inc="t" if k == 7 else None)

                        def gelu_gh(ko, ee=ee, gb=gb, gho=gho):
                            pp = 6 + (ko % 2)
                            run("scalar",
                                lambda e, ko=ko, ee=ee, pp=pp: e.activation(
                                    B(R_HS + (ko % 2) * 256, 256), pb[pp][:, 0:256],
                                    AF.Gelu, bias=Fp(F_B1 + ee * 4 + ko, 1)),
                                read=(bank[pp], bb1), write=(hsb[ko % 2],), inc="s")
                            run("vector",
                                lambda e, ko=ko, gb=gb, gho=gho: e.tensor_tensor(
                                    B(R_GH + gho + ko * 256, 256), B(R_HS + (ko % 2) * 256, 256),
                                    pb[gb][:, 0:256], ALU.mult),
                                read=(hsb[ko % 2], bank[gb]), write=(ghb[gho // 1024],), inc="v")

                        def emit_gb(ee=ee, half_t=half_t, gb=gb):
                            run("tensor",
                                lambda e, ee=ee, half_t=half_t, gb=gb: e.matmul(
                                    pb[gb][:, 0:256],
                                    B(R_SEL + ee * 128, 128, 0, 8),
                                    B(R_GFM + half_t * 256, 256, 0, 8),
                                    start=True, stop=True),
                                read=(gfm, selb), write=(bank[gb],), inc="t")

                        if gatesT_pending[0]:
                            h_group(0)
                            emit_gatesT()
                            gatesT_pending[0] = False
                            emit_gb()
                            gelu_gh(0)
                            for ko in range(1, 4):
                                h_group(ko)
                                gelu_gh(ko)
                        else:
                            emit_gb()
                            for ko in range(4):
                                h_group(ko)
                                gelu_gh(ko)
                        if ei >= 1:
                            w2_mms(EXPERT_ORDER[ei - 1], ei - 1)
                        if ln_next is not None and 2 <= ei <= 5:
                            if ei <= 3:
                                ln_accum(ln_next[0], 2 * (ei - 2))
                                ln_accum(ln_next[0], 2 * (ei - 2) + 1)
                            elif ei == 4:
                                ln_math(ln_next[0], want_xnorm=ln_next[1])
                            else:
                                for i4_ in range(4):
                                    ln_norm_tile(ln_next[0], i4_)
                    for qsub in range(2):
                        for half in range(2):
                            yb = 2 + qsub * 2 + half
                            run("tensor",
                                lambda e, qsub=qsub, half=half, yb=yb, half_t=half_t: e.matmul(
                                    pb[yb][:, 0:512],
                                    B(R_GFM + (half_t * 2 + qsub) * 128, 128, 0, 8),
                                    B(R_B2 + half * 512, 512, 0, 8),
                                    start=False, stop=False),
                                read=(gfm, bb2), write=(bank[yb],))
                    w2_mms(EXPERT_ORDER[7], 7, stop=True)
                    if qs == NQS - 1 and half_t == 1 and s < NSEQ - 1:
                        load_qkv_weights(staged=True)
                        load_wo(staged=True)
                    for qsub in range(2):
                        for half in range(2):
                            yb = 2 + qsub * 2 + half
                            tile = qs * 4 + half_t * 2 + qsub
                            run("vector",
                                lambda e, tile=tile, half=half, yb=yb: e.tensor_tensor(
                                    B(R_X + tile * 1024 + half * 512, 512),
                                    B(R_X + tile * 1024 + half * 512, 512),
                                    pb[yb][:, 0:512], ALU.add),
                                read=(bank[yb], xp[tile // 4]), write=(xp[tile // 4],), inc="v")
                dma("gpsimd",
                    out[s, qs * 512:(qs + 1) * 512, :].rearrange("(i p) d -> p i d", p=128),
                    B(R_X + qs * 4096, 4096).rearrange("p (i d) -> p i d", d=1024),
                    f"do{qs}", rbufs=(xp[qs],))
                if s < NSEQ - 1:
                    load_x(s + 1, qs)



        P.emit("gpsimd", lambda e: None,
               waits=[(f"do{i}", 16 * NSEQ) for i in range(4)])

        with nc.Block() as block:
            @block.tensor
            def _(eng):
                for op in P.ops["tensor"]:
                    op(eng, sems)

            @block.vector
            def _(eng):
                for op in P.ops["vector"]:
                    op(eng, sems)

            @block.scalar
            def _(eng):
                for op in P.ops["scalar"]:
                    op(eng, sems)

            @block.gpsimd
            def _(eng):
                for op in P.ops["gpsimd"]:
                    op(eng, sems)

            @block.sync
            def _(eng):
                for op in P.ops["sync"]:
                    op(eng, sems)

    return nc


_NC_CACHE = [None]


def kernel(L0_attn_hidden_states, L0_attn_Wq, L0_attn_Wk, L0_attn_Wv, L0_attn_Wo,
           L0_ffn_w1, L0_ffn_b1, L0_ffn_w2, L0_ffn_b2, L0_ffn_centroids,
           L0_ffn_w_route):
    x = np.ascontiguousarray(np.asarray(L0_attn_hidden_states, dtype=np.float32))
    consts = _host_consts()
    base = dict(wq=np.ascontiguousarray(L0_attn_Wq, dtype=np.float32),
                wk=np.ascontiguousarray(L0_attn_Wk, dtype=np.float32),
                wv=np.ascontiguousarray(L0_attn_Wv, dtype=np.float32),
                wo=np.ascontiguousarray(L0_attn_Wo, dtype=np.float32),
                w1=np.ascontiguousarray(L0_ffn_w1, dtype=np.float32),
                b1=np.ascontiguousarray(np.asarray(L0_ffn_b1, dtype=np.float32)
                                        .reshape(8, 4, 128).transpose(2, 0, 1)
                                        .reshape(128, 32)),
                w2=np.ascontiguousarray(L0_ffn_w2, dtype=np.float32),
                b2=np.ascontiguousarray(L0_ffn_b2, dtype=np.float32),
                cent=np.ascontiguousarray(L0_ffn_centroids, dtype=np.float32),
                wrt=np.ascontiguousarray(L0_ffn_w_route, dtype=np.float32),
                **consts)
    if _NC_CACHE[0] is None:
        _NC_CACHE[0] = build_nc()
    nc = _NC_CACHE[0]
    in_maps = []
    for c in range(NCORE):
        m = dict(base)
        m["x"] = np.ascontiguousarray(x[c * NSEQ:(c + 1) * NSEQ])
        in_maps.append(m)
    res = run_bass_kernel_spmd(nc, in_maps, core_ids=list(range(NCORE)))
    return np.concatenate([r["out"] for r in res.results], axis=0).reshape(32, T, DM)


# version 28 (final)
# speedup vs baseline: 1.0604x; 1.0233x over previous
import numpy as np
import ml_dtypes

import concourse.bass as bass
import concourse.mybir as mybir
from concourse.bass_utils import run_bass_kernel_spmd

BF16 = mybir.dt.bfloat16
F32 = mybir.dt.float32
AF = mybir.ActivationFunctionType
ALU = mybir.AluOpType
AXX = mybir.AxisListType.X

NCORE = 8
NSEQ = 4
T = 2048
DM = 1024
NH, NKV, HD = 8, 4, 128
DEXP, NEXP = 512, 8
SCALE = float(np.float32(1.0) / np.sqrt(np.float32(128.0)))
MASKVAL = -1.0e6

NQS = T // 512
NQB = T // 256

_off = [0]
def _alloc(n):
    o = _off[0]; _off[0] += n; return o

R_X    = _alloc(16 * 1024)
R_MASK = _alloc(384)
R_ID   = _alloc(128)
R_PROT = _alloc(128)
R_SEL  = _alloc(1024)
R_WCAT = _alloc(128)
R_B2   = _alloc(1024)
R_LNT  = _alloc(4096)
R_SQ   = _alloc(1024)
R_LNF  = _alloc(4096)
R_HS   = _alloc(512)
R_GH   = _alloc(2048)
R_GFM  = _alloc(512)
R_PT   = _alloc(2048)
R_QF   = _alloc(16384)
R_KF   = _alloc(8192)
R_VA   = _alloc(4 * 16 * 129)
R_E    = _alloc(32768)
NB = _off[0]

EXB = R_QF + 64
def exp_slot(e):
    return EXB + 8192 * e
W_QKV = exp_slot(4)
W_O = exp_slot(6)
assert exp_slot(4) == R_E
assert exp_slot(8) == NB

_f = [0]
def _falloc(n):
    o = _f[0]; _f[0] += n; return o
F_CENT = _falloc(1024)
F_ST   = _falloc(128)
F_REC  = _falloc(8)
F_GAT  = _falloc(128)
F_B1   = _falloc(32)
F_COS  = _falloc(512)
F_SIN  = _falloc(512)
NF = _f[0]
assert NB * 2 + NF * 4 <= 212000, (NB, NF)

def _st(parity):
    b = F_ST + 64 * parity
    return dict(SUM=b, SQ=b + 4, M=b + 8, MSQ=b + 12, V=b + 16, RS=b + 20,
                NMR=b + 24, SV=b + 28, XN=b + 32, IXN=b + 36, T0=b + 40, T1=b + 44)
GB_L, GB_T, GB_M1, GB_NM1, GB_M2, GB_S, GB_R, GB_E = (
    F_GAT + 0, F_GAT + 32, F_GAT + 64, F_GAT + 68, F_GAT + 72, F_GAT + 76, F_GAT + 80,
    F_GAT + 88)


def _host_consts():
    bf = ml_dtypes.bfloat16
    try:
        import jax
        import jax.numpy as jnp
        with jax.default_device(jax.devices("cpu")[0]):
            inv_freq = 1.0 / 10000.0 ** (jnp.arange(0, HD, 2, dtype=jnp.float32) / HD)
            freqs = jnp.outer(jnp.arange(T, dtype=jnp.float32), inv_freq)
            emb = jnp.concatenate((freqs, freqs), axis=-1)
            cos = np.asarray(jnp.cos(emb), dtype=np.float32)
            sin = np.asarray(jnp.sin(emb), dtype=np.float32)
    except Exception:
        inv_freq = (1.0 / (10000.0 ** (np.arange(0, HD, 2, dtype=np.float32) / np.float32(HD)))).astype(np.float32)
        freqs = np.outer(np.arange(T, dtype=np.float32), inv_freq).astype(np.float32)
        emb = np.concatenate([freqs, freqs], axis=1)
        cos = np.cos(emb).astype(np.float32)
        sin = np.sin(emb).astype(np.float32)
    sin_signed = sin.copy()
    sin_signed[:, :64] *= -1.0
    c_cos = np.ascontiguousarray(cos.T).astype(np.float32)
    c_sin = np.ascontiguousarray(sin_signed.T).astype(np.float32)
    p = np.arange(128)[:, None]
    v = np.arange(128, 512)[None, :]
    c_mask = np.where(p <= v - 256, 0.0, MASKVAL).astype(bf)
    c_id = np.eye(128, dtype=np.float32).astype(bf)
    prot = np.zeros((128, 128), dtype=np.float32)
    prot[np.arange(128), (np.arange(128) + 64) % 128] = 1.0
    c_prot = prot.astype(bf)
    sel = np.zeros((8, 8, 128), dtype=np.float32)
    for e in range(8):
        sel[e, e, :] = 1.0
    c_sel = sel.transpose(1, 0, 2).reshape(8, 1024).astype(bf)
    return dict(c_cos=c_cos, c_sin=c_sin, c_mask=c_mask, c_id=c_id,
                c_prot=c_prot, c_sel=np.ascontiguousarray(c_sel))


class Prog:
    ENGS = ("tensor", "vector", "scalar", "gpsimd", "sync")

    def __init__(self):
        self.ops = {e: [] for e in self.ENGS}
        self.cnt = {}
        self.known = {e: {} for e in self.ENGS}

    def emit(self, eng, fn, waits=(), inc=None, inc_by=1):
        w = []
        for s, v in waits:
            if v <= 0 or self.known[eng].get(s, 0) >= v:
                continue
            self.known[eng][s] = v
            w.append((s, v))
        val = None
        if inc is not None:
            self.cnt[inc] = self.cnt.get(inc, 0) + inc_by
            val = self.cnt[inc]

        def op(engine, sems, w=w, fn=fn, inc=inc, inc_by=inc_by):
            for s, v in w:
                engine.wait_ge(sems[s], v)
            ins = fn(engine)
            if inc is not None and ins is not None:
                ins.then_inc(sems[inc], inc_by)
        self.ops[eng].append(op)
        return (inc, val) if inc is not None else None

    def c(self, sem):
        return self.cnt.get(sem, 0)


class Buf:
    def __init__(self):
        self.w = {}
        self.r = {}

    @staticmethod
    def _m(d, ev):
        if ev is None:
            return
        s, v = ev
        if d.get(s, 0) < v:
            d[s] = v

    def wrote(self, ev):
        self._m(self.w, ev)

    def read(self, ev):
        self._m(self.r, ev)

    @property
    def need_for_read(self):
        return list(self.w.items())

    @property
    def need_for_write(self):
        out = dict(self.w)
        for s, v in self.r.items():
            if out.get(s, 0) < v:
                out[s] = v
        return list(out.items())


def build_nc():
    nc = bass.Bass("TRN2", target_bir_lowering=False)
    xin = nc.dram_tensor("x", [NSEQ, T, DM], F32, kind="ExternalInput").ap()
    wq = nc.dram_tensor("wq", [DM, NH * HD], F32, kind="ExternalInput").ap()
    wk = nc.dram_tensor("wk", [DM, NKV * HD], F32, kind="ExternalInput").ap()
    wv = nc.dram_tensor("wv", [DM, NKV * HD], F32, kind="ExternalInput").ap()
    wo = nc.dram_tensor("wo", [NH * HD, DM], F32, kind="ExternalInput").ap()
    w1 = nc.dram_tensor("w1", [NEXP, DM, DEXP], F32, kind="ExternalInput").ap()
    b1d = nc.dram_tensor("b1", [128, 32], F32, kind="ExternalInput").ap()
    w2 = nc.dram_tensor("w2", [NEXP, DEXP, DM], F32, kind="ExternalInput").ap()
    b2d = nc.dram_tensor("b2", [NEXP, DM], F32, kind="ExternalInput").ap()
    cent = nc.dram_tensor("cent", [NEXP, DM], F32, kind="ExternalInput").ap()
    wrtd = nc.dram_tensor("wrt", [NEXP, DM], F32, kind="ExternalInput").ap()
    cc = {}
    for n, shp in (("c_mask", [128, 384]), ("c_id", [128, 128]),
                   ("c_prot", [128, 128]), ("c_sel", [8, 1024])):
        cc[n] = nc.dram_tensor(n, shp, BF16, kind="ExternalInput").ap()
    for n in ("c_cos", "c_sin"):
        cc[n] = nc.dram_tensor(n, [128, T], F32, kind="ExternalInput").ap()
    out = nc.dram_tensor("out", [NSEQ, T, DM], F32, kind="ExternalOutput").ap()
    wstage = nc.dram_tensor("wstage", [128, 24576 + NEXP * 8192], BF16, kind="Internal").ap()

    import contextlib
    with contextlib.ExitStack() as ctx:
        AB = ctx.enter_context(nc.sbuf_tensor("AB", [128, NB], BF16))
        AFP = ctx.enter_context(nc.sbuf_tensor("AFp", [128, NF], F32))
        pb = [ctx.enter_context(nc.psum_tensor(f"pb{i}", [128, 512], F32))
              for i in range(8)]
        semnames = (("t", "v", "s", "g", "dc", "dcs", "dwi", "dcen", "dwq", "dwo", "hwq", "hwo")
                    + tuple(f"dx{i}" for i in range(4))
                    + tuple(f"do{i}" for i in range(4))
                    + tuple(f"dwe{e}" for e in range(NEXP))
                    + tuple(f"hwe{e}" for e in range(NEXP)))
        sems = {n: ctx.enter_context(nc.semaphore(f"sem_{n}")) for n in semnames}

        P = Prog()
        ab = AB.ap()
        af = AFP.ap()

        def B(off, n, p0=0, p1=128):
            return ab[p0:p1, off:off + n]

        def Fp(off, n, p0=0, p1=128):
            return af[p0:p1, off:off + n]

        bank = [Buf() for _ in range(8)]
        bufs = {}
        def buf(name):
            if name not in bufs:
                bufs[name] = Buf()
            return bufs[name]

        def dma(queue, outap, inap, sem, rbufs=(), wbufs=()):
            waits = {}
            for b in rbufs:
                for s, v in b.need_for_read:
                    waits[s] = max(waits.get(s, 0), v)
            for b in wbufs:
                for s, v in b.need_for_write:
                    waits[s] = max(waits.get(s, 0), v)
            ev = P.emit(queue, lambda e, o=outap, i=inap: e.dma_start(out=o, in_=i),
                        waits=list(waits.items()), inc=sem, inc_by=16)
            for b in rbufs:
                b.read(ev)
            for b in wbufs:
                b.wrote(ev)
            return ev

        def run(eng, fn, read=(), write=(), extra=(), inc=None):
            waits = {}
            for b in read:
                for s, v in b.need_for_read:
                    waits[s] = max(waits.get(s, 0), v)
            for b in write:
                for s, v in b.need_for_write:
                    waits[s] = max(waits.get(s, 0), v)
            for s, v in extra:
                waits[s] = max(waits.get(s, 0), v)
            ev = P.emit(eng, fn, waits=list(waits.items()), inc=inc)
            for b in read:
                b.read(ev if ev is not None else (eng_sem(eng), P.c(eng_sem(eng))))
            for b in write:
                b.wrote(ev)
            return ev

        def eng_sem(eng):
            return {"tensor": "t", "vector": "v", "scalar": "s", "gpsimd": "g"}.get(eng, "t")

        cosb, sinb, maskb, idb, protb, selb = (buf(x) for x in
            ("cos", "sin", "mask", "id", "prot", "sel"))
        wqkv, wob, wcat, bb1, bb2 = (buf(x) for x in ("wqkv", "wob", "wcat", "b1", "b2"))
        ex = [buf(f"ex{e}") for e in range(NEXP)]
        xp = [buf(f"xp{i}") for i in range(4)]
        qf = [buf(f"qf{j}") for j in range(4)]
        kf = [buf(f"kf{j}") for j in range(4)]
        vaug = [buf(f"va{j}") for j in range(4)]
        lnt = [buf("lnt_a"), buf("lnt_b"), buf("lnt_c")]
        sq = buf("sq")
        lnf = [buf(f"lnf{k}") for k in range(8)]
        ptb = [buf(f"pt{i}") for i in range(4)]
        hsb = [buf(f"hs{i}") for i in range(2)]
        ghb, gfm = [buf("gh0"), buf("gh1")], buf("gfm")
        stats, gat, orec, cstage = buf("stats"), buf("gat"), buf("orec"), buf("cstage")

        dma("sync", B(R_MASK, 384), cc["c_mask"], "dc", wbufs=(maskb,))
        dma("sync", B(R_ID, 128), cc["c_id"], "dc", wbufs=(idb,))
        dma("sync", B(R_PROT, 128), cc["c_prot"], "dc", wbufs=(protb,))
        dma("sync", B(R_SEL, 1024, 0, 8), cc["c_sel"], "dc", wbufs=(selb,))
        for b_ in (cosb, sinb, maskb, idb, protb, selb):
            b_.wrote(("dc", P.c("dc")))

        for part_ in range(4):
            dma("gpsimd",
                B(R_X + part_ * 4096, 4096).rearrange("p (i d) -> p i d", d=1024),
                xin[0, part_ * 512:(part_ + 1) * 512, :].rearrange("(i p) d -> p i d", p=128),
                f"dx{part_}", wbufs=(xp[part_],))
        dma("gpsimd", Fp(F_B1, 32), b1d, "dwi", wbufs=(bb1,))
        dma("gpsimd", B(R_B2, 1024, 0, 8), b2d, "dwi", wbufs=(bb2,))
        bb1.wrote(("dwi", P.c("dwi")))
        bb2.wrote(("dwi", P.c("dwi")))

        ones_ap = ab[:, R_VA + 128: R_VA + 64 * 129: 129]

        for idx, (src, colo) in enumerate(((cent, 0), (wrtd, 8))):
            dma("gpsimd", Fp(F_CENT, 1024, 0, 8), src, "dcen", wbufs=(cstage,))
            if idx == 0:
                run("scalar",
                    lambda e: e.activation(B(R_LNT, 1024, 0, 8), Fp(F_CENT, 1024, 0, 8),
                                           AF.Square, accum_out=Fp(F_GAT, 1, 0, 8)),
                    read=(cstage,), write=(lnt[0], gat), inc="s")
                run("scalar", lambda e: e.activation(Fp(F_GAT + 1, 1, 0, 8), Fp(F_GAT, 1, 0, 8), AF.Ln),
                    read=(gat,), write=(gat,), inc="s")
                run("scalar", lambda e: e.activation(Fp(F_GAT + 2, 1, 0, 8), Fp(F_GAT + 1, 1, 0, 8),
                                                     AF.Exp, scale=0.5),
                    read=(gat,), write=(gat,), inc="s")
                run("vector", lambda e: e.tensor_scalar_max(Fp(F_GAT + 3, 1, 0, 8), Fp(F_GAT + 2, 1, 0, 8), 1e-12),
                    read=(gat,), write=(gat,), inc="v")
                run("vector", lambda e: e.reciprocal(Fp(F_GAT + 4, 1, 0, 8), Fp(F_GAT + 3, 1, 0, 8)),
                    read=(gat,), write=(gat,), inc="v")
                run("vector",
                    lambda e: e.tensor_scalar(B(R_LNT, 1024, 0, 8), Fp(F_CENT, 1024, 0, 8),
                                              Fp(F_GAT + 4, 1, 0, 8), None, ALU.mult),
                    read=(cstage, gat), write=(lnt[0],), inc="v")
            else:
                run("vector", lambda e: e.tensor_copy(B(R_LNT, 1024, 0, 8), Fp(F_CENT, 1024, 0, 8)),
                    read=(cstage,), write=(lnt[0],), inc="v")
            for k in range(8):
                run("tensor",
                    lambda e, k=k: e.matmul(pb[0][:, 0:8], B(R_LNT + k * 128, 128, 0, 8),
                                            B(R_ID, 8, 0, 8), start=True, stop=True),
                    read=(lnt[0], idb), write=(bank[0],), inc="t")
                run("scalar",
                    lambda e, k=k, c=colo: e.copy(B(R_WCAT + k * 16 + c, 8), pb[0][:, 0:8]),
                    read=(bank[0],), write=(wcat,), inc="s")

        stg_qkv, stg_wo = buf("stg_qkv"), buf("stg_wo")
        stg_ex = [buf(f"stg_ex{e}") for e in range(NEXP)]

        def load_qkv_weights(staged):
            if staged:
                dma("sync", B(W_QKV, 16384), wstage[:, 0:16384], "hwq",
                    rbufs=(stg_qkv,), wbufs=(wqkv, ex[4], ex[5]))
            else:
                dma("gpsimd", B(W_QKV, 8192).rearrange("p (k n) -> p k n", n=1024),
                    wq.rearrange("(k p) n -> p k n", p=128), "dwq", wbufs=(wqkv, ex[4]))
                dma("gpsimd", B(W_QKV + 8192, 4096).rearrange("p (k n) -> p k n", n=512),
                    wk.rearrange("(k p) n -> p k n", p=128), "dwq", wbufs=(wqkv, ex[4], ex[5]))
                dma("gpsimd", B(W_QKV + 12288, 4096).rearrange("p (k n) -> p k n", n=512),
                    wv.rearrange("(k p) n -> p k n", p=128), "dwq", wbufs=(wqkv, ex[5]))
            wqkv.wrote(("dwq", P.c("dwq")))

        def stage_qkv_wo():
            dma("sync", wstage[:, 0:16384], B(W_QKV, 16384), "hwq", rbufs=(wqkv,),
                wbufs=(stg_qkv,))
            dma("sync", wstage[:, 16384:24576], B(W_O, 8192), "hwo", rbufs=(wob,),
                wbufs=(stg_wo,))

        def load_wo(staged):
            if staged:
                dma("sync", B(W_O, 8192), wstage[:, 16384:24576], "hwo",
                    rbufs=(stg_wo,), wbufs=(wob, ex[6]))
            else:
                dma("gpsimd", B(W_O, 8192).rearrange("p (k n) -> p k n", n=1024),
                    wo.rearrange("(k p) n -> p k n", p=128), "dwo", wbufs=(wob, ex[6]))

        def load_expert(e, staged):
            base = exp_slot(e)
            confl = [ex[e]]
            if e in (4, 5):
                confl.append(wqkv)
            elif e == 6:
                confl.append(wob)
            elif e <= 3:
                confl += qf + kf + vaug + [ptb[0], ptb[1]]
            if staged:
                dma("sync", B(base, 8192), wstage[:, 24576 + e * 8192: 24576 + (e + 1) * 8192],
                    f"hwe{e}", rbufs=(stg_ex[e],), wbufs=tuple(confl))
            else:
                dma("gpsimd", B(base, 4096).rearrange("p (k n) -> p k n", n=512),
                    w1[e].rearrange("(k p) n -> p k n", p=128), f"dwe{e}", wbufs=tuple(confl))
                dma("gpsimd", B(base + 4096, 4096).rearrange("p (k n) -> p k n", n=1024),
                    w2[e].rearrange("(k p) n -> p k n", p=128), f"dwe{e}", wbufs=tuple(confl))
                dma("sync", wstage[:, 24576 + e * 8192: 24576 + (e + 1) * 8192], B(base, 8192),
                    f"hwe{e}", rbufs=tuple([ex[e]] + confl[1:]), wbufs=(stg_ex[e],))
            ex[e].wrote((f"dwe{e}", P.c(f"dwe{e}")))

        load_qkv_weights(staged=False)
        load_wo(staged=False)
        stage_qkv_wo()

        EXPERT_ORDER = [4, 5, 7, 6, 0, 1, 2, 3]

        statsb = [buf("stats0"), buf("stats1")]
        stsum = [buf("stsum0"), buf("stsum1")]
        stsq = [buf("stsq0"), buf("stsq1")]

        def ln_accum(qs, i4):
            pr = qs % 2
            S = _st(pr)
            tile = qs * 4 + i4
            xap = B(R_X + tile * 1024, 1024)
            run("vector", lambda e, i4=i4, xap=xap, S=S: e.reduce_sum(Fp(S["SUM"] + i4, 1), xap, AXX),
                read=(xp[tile // 4],), write=(stsum[pr],), inc="v")
            run("scalar",
                lambda e, i4=i4, xap=xap, S=S: e.activation(B(R_LNT + 3072, 1024), xap, AF.Square,
                                                           accum_out=Fp(S["SQ"] + i4, 1)),
                read=(xp[tile // 4],), write=(lnt[2], stsq[pr]), inc="s")

        def ln_math(qs, want_xnorm, lo=0, hi=4):
            pr = qs % 2
            S = _st(pr)
            st = statsb[pr]
            n_ = hi - lo
            run("vector", lambda e, S=S, lo=lo, n_=n_: e.tensor_scalar(Fp(S["M"] + lo, n_), Fp(S["SUM"] + lo, n_), 1.0 / 1024.0, None, ALU.mult),
                read=(stsum[pr],), write=(st,), inc="v")
            run("vector", lambda e, S=S, lo=lo, n_=n_: e.tensor_scalar(Fp(S["MSQ"] + lo, n_), Fp(S["SQ"] + lo, n_), 1.0 / 1024.0, None, ALU.mult),
                read=(stsq[pr],), write=(st,), inc="v")
            run("vector", lambda e, S=S, lo=lo, n_=n_: e.tensor_tensor(Fp(S["T0"] + lo, n_), Fp(S["M"] + lo, n_), Fp(S["M"] + lo, n_), ALU.mult),
                read=(st,), write=(st,), inc="v")
            run("vector", lambda e, S=S, lo=lo, n_=n_: e.tensor_tensor(Fp(S["V"] + lo, n_), Fp(S["MSQ"] + lo, n_), Fp(S["T0"] + lo, n_), ALU.subtract),
                read=(st,), write=(st,), inc="v")
            run("vector", lambda e, S=S, lo=lo, n_=n_: e.tensor_scalar(Fp(S["T1"] + lo, n_), Fp(S["V"] + lo, n_), 1e-5, None, ALU.add),
                read=(st,), write=(st,), inc="v")
            run("scalar", lambda e, S=S, lo=lo, n_=n_: e.activation(Fp(S["T0"] + lo, n_), Fp(S["T1"] + lo, n_), AF.Ln),
                read=(st,), write=(st,), inc="s")
            run("scalar", lambda e, S=S, lo=lo, n_=n_: e.activation(Fp(S["RS"] + lo, n_), Fp(S["T0"] + lo, n_), AF.Exp, scale=-0.5),
                read=(st,), write=(st,), inc="s")
            run("vector", lambda e, S=S, lo=lo, n_=n_: e.scalar_tensor_tensor(Fp(S["NMR"] + lo, n_), Fp(S["M"] + lo, n_), -1.0, Fp(S["RS"] + lo, n_),
                                                                ALU.mult, ALU.mult),
                read=(st,), write=(st,), inc="v")
            if want_xnorm:
                run("scalar", lambda e, S=S, lo=lo, n_=n_: e.activation(Fp(S["T1"] + lo, n_), Fp(S["V"] + lo, n_), AF.Ln),
                    read=(st,), write=(st,), inc="s")
                run("scalar", lambda e, S=S, lo=lo, n_=n_: e.activation(Fp(S["SV"] + lo, n_), Fp(S["T1"] + lo, n_), AF.Exp, scale=0.5),
                    read=(st,), write=(st,), inc="s")
                run("vector", lambda e, S=S, lo=lo, n_=n_: e.tensor_tensor(Fp(S["XN"] + lo, n_), Fp(S["SV"] + lo, n_), Fp(S["RS"] + lo, n_), ALU.mult),
                    read=(st,), write=(st,), inc="v")
                run("vector", lambda e, S=S, lo=lo, n_=n_: e.tensor_scalar(Fp(S["XN"] + lo, n_), Fp(S["XN"] + lo, n_), 32.0, 1e-12, ALU.mult, ALU.max),
                    read=(st,), write=(st,), inc="v")
                run("vector", lambda e, S=S, lo=lo, n_=n_: e.reciprocal(Fp(S["IXN"] + lo, n_), Fp(S["XN"] + lo, n_)),
                    read=(st,), write=(st,), inc="v")

        def ln_stats(qs, want_xnorm):
            for i4 in range(4):
                ln_accum(qs, i4)
            ln_math(qs, want_xnorm)

        def ln_norm(qs):
            pr = qs % 2
            S = _st(pr)
            st = statsb[pr]
            for i4 in range(4):
                ln_norm_tile(qs, i4)

        def ln_norm_tile(qs, i4):
            pr = qs % 2
            S = _st(pr)
            st = statsb[pr]
            tile = qs * 4 + i4
            lb = lnt[0] if i4 < 2 else (lnt[1] if i4 == 2 else lnt[2])
            run("vector",
                lambda e, i4=i4, tile=tile, S=S: e.tensor_scalar(
                    B(R_LNT + i4 * 1024, 1024), B(R_X + tile * 1024, 1024),
                    Fp(S["RS"] + i4, 1), Fp(S["NMR"] + i4, 1), ALU.mult, ALU.add),
                read=(xp[tile // 4], st), write=(lb,), inc="v")

        def ln_trans(qs):
            for k in range(8):
                tb = k % 2
                for i4 in range(4):
                    lb = lnt[0] if i4 < 2 else (lnt[1] if i4 == 2 else lnt[2])
                    run("tensor",
                        lambda e, k=k, i4=i4, tb=tb: e.matmul(
                            pb[tb][:, i4 * 128:(i4 + 1) * 128],
                            B(R_LNT + i4 * 1024 + k * 128, 128), B(R_ID, 128),
                            start=True, stop=True),
                        read=(lb, idb), write=(bank[tb],),
                        inc="t" if i4 == 3 else None)
                run("scalar", lambda e, k=k, tb=tb: e.copy(B(R_LNF + k * 512, 512), pb[tb][:, 0:512]),
                    read=(bank[tb],), write=(lnf[k],), inc="s")

        def load_x(s, part):
            dma("gpsimd",
                B(R_X + part * 4096, 4096).rearrange("p (i d) -> p i d", d=1024),
                xin[s, part * 512:(part + 1) * 512, :].rearrange("(i p) d -> p i d", p=128),
                f"dx{part}", wbufs=(xp[part],))

        for s in range(NSEQ):
            run("vector", lambda e: e.memset(ones_ap, 1.0), write=tuple(vaug), inc="v")

            for qs in range(NQS):
                if qs == 0 and s == 0:
                    ln_stats(0, want_xnorm=False)
                    ln_norm(0)
                dma("sync", Fp(F_COS, 512), cc["c_cos"][:, qs * 512:(qs + 1) * 512],
                    "dcs", wbufs=(cosb,))
                dma("sync", Fp(F_SIN, 512), cc["c_sin"][:, qs * 512:(qs + 1) * 512],
                    "dcs", wbufs=(sinb,))
                cosb.wrote(("dcs", P.c("dcs")))
                sinb.wrote(("dcs", P.c("dcs")))
                ln_trans(qs)
                cosap = Fp(F_COS, 512)
                sinap = Fp(F_SIN, 512)

                def qkv_epilogue(m, qs=qs):
                    pp = 6 + (m % 2)
                    rb = 2 + (m % 2)
                    sl = (m % 2) * 512
                    if m < 8:
                        dstoff, dstbuf = R_QF + m * T + qs * 512, qf[qs]
                    else:
                        dstoff, dstbuf = R_KF + (m - 8) * T + qs * 512, kf[qs]
                    run("scalar", lambda e, pp=pp, sl=sl: e.copy(B(R_SQ + sl, 512), pb[pp][:, 0:512]),
                        read=(bank[pp],), write=(sq,), inc="s")
                    run("tensor",
                        lambda e, rb=rb, sl=sl: e.matmul(pb[rb][:, 0:512], B(R_PROT, 128),
                                                         B(R_SQ + sl, 512), start=True, stop=True),
                        read=(sq, protb), write=(bank[rb],), inc="t")
                    run("vector",
                        lambda e, cosap=cosap, sl=sl: e.tensor_tensor(
                            B(R_GH + 1024 + sl, 512), B(R_SQ + sl, 512), cosap, ALU.mult),
                        read=(sq, cosb), write=(ghb[1],), inc="v")
                    run("vector",
                        lambda e, sinap=sinap, rb=rb, sl=sl: e.tensor_tensor(
                            B(R_GH + sl, 512), pb[rb][:, 0:512], sinap, ALU.mult),
                        read=(bank[rb], sinb), write=(ghb[0],), inc="v")
                    run("vector",
                        lambda e, dstoff=dstoff, sl=sl: e.tensor_tensor(
                            B(dstoff, 512), B(R_GH + 1024 + sl, 512),
                            B(R_GH + sl, 512), ALU.add),
                        read=(ghb[0], ghb[1]), write=(dstbuf,), inc="v")

                for m in range(12):
                    pp = 6 + (m % 2)
                    if m < 8:
                        wbase, wstride, mm = W_QKV, 1024, m
                    else:
                        wbase, wstride, mm = W_QKV + 8192, 512, m - 8
                    for k in range(8):
                        run("tensor",
                            lambda e, k=k, mm=mm, pp=pp, wbase=wbase, wstride=wstride:
                            e.matmul(pb[pp][:, 0:512],
                                     B(wbase + k * wstride + mm * 128, 128),
                                     B(R_LNF + k * 512, 512),
                                     start=(k == 0), stop=(k == 7)),
                            read=(lnf[k], wqkv), write=(bank[pp],),
                            inc="t" if k == 7 else None)
                    if m >= 1:
                        qkv_epilogue(m - 1)
                qkv_epilogue(11)

                def v_evac(i4, qs=qs):
                    pp = 6 + (i4 % 2)
                    j = qs * 4 + i4
                    for g in range(4):
                        run("vector",
                            lambda e, g=g, j=j, pp=pp: e.tensor_copy(
                                B(R_VA + (g * 16 + j) * 129, 128), pb[pp][:, g * 128:(g + 1) * 128]),
                            read=(bank[pp],), write=(vaug[qs],), inc="v")

                for i4 in range(4):
                    pp = 6 + (i4 % 2)
                    for k in range(8):
                        run("tensor",
                            lambda e, k=k, i4=i4, pp=pp: e.matmul(
                                pb[pp][:, 0:512],
                                B(R_LNF + k * 512 + i4 * 128, 128),
                                B(W_QKV + 12288 + k * 512, 512),
                                start=(k == 0), stop=(k == 7)),
                            read=(lnf[k], wqkv), write=(bank[pp],),
                            inc="t" if k == 7 else None)
                    if i4 >= 1:
                        v_evac(i4 - 1)
                        if qs < NQS - 1:
                            ln_accum(qs + 1, i4 - 1)
                    if i4 == 3 and qs < NQS - 1:
                        ln_accum(qs + 1, 3)
                        ln_math(qs + 1, want_xnorm=False)
                        for i4_ in range(4):
                            ln_norm_tile(qs + 1, i4_)
                v_evac(3)

            pend_otrans = [None]
            p2_fill = []
            for qb in range(NQB):
                if qb == 1:
                    load_expert(4, staged=(s > 0))
                    load_expert(5, staged=(s > 0))
                    if s == 0:
                        load_expert(7, staged=False)
                if qb == 2:
                    p2_fill.extend(
                        [lambda i4_=i4_: ln_accum(0, i4_) for i4_ in range(4)]
                        + [lambda: ln_math(0, want_xnorm=True)]
                        + [lambda i4_=i4_: ln_norm_tile(0, i4_) for i4_ in range(4)])
                nch = 2 * (qb + 1)

                def emit_otrans(g, tb=None):
                    tb = 6 + (g % 2) if tb is None else tb
                    for ri in range(4):
                        run("tensor",
                            lambda e, ri=ri, g=g, tb=tb: e.matmul(
                                pb[tb][:, ri * 128:(ri + 1) * 128],
                                B(R_SQ + (g % 2) * 512 + ri * 128, 128), B(R_ID, 128),
                                start=True, stop=True),
                            read=(sq, idb), write=(bank[tb],),
                            inc="t" if ri == 3 else None)
                    run("vector",
                        lambda e, g=g, tb=tb: e.tensor_copy(B(R_LNF + g * 512, 512), pb[tb][:, 0:512]),
                        read=(bank[tb],), write=(lnf[g],), inc="v")

                for g in range(4):
                    def s_chunk(c, g=g, qb=qb):
                        sb = c % 2
                        pt = c % 4
                        diag = (c >= 2 * qb)
                        if diag:
                            delta = 128 * c - 256 * qb
                            run("tensor",
                                lambda e, sb=sb, delta=delta: e.matmul(
                                    pb[sb][:, 0:512].rearrange("p (h n) -> p h n", n=256),
                                    B(R_ID, 128),
                                    B(R_MASK + 128 - delta, 256).unsqueeze(1).broadcast_to([128, 2, 256]),
                                    start=True, stop=False),
                                read=(idb, maskb), write=(bank[sb],))
                        run("tensor",
                            lambda e, sb=sb, c=c, g=g, qb=qb, diag=diag: e.matmul(
                                pb[sb][:, 0:512].rearrange("p (h n) -> p h n", n=256),
                                B(R_KF + g * T + c * 128, 128),
                                B(R_QF + 2 * g * T + qb * 256, 2 * T).rearrange(
                                    "p (h t) -> p h t", t=T)[:, :, 0:256],
                                start=not diag, stop=True),
                            read=(kf[c // 4], qf[qb // 2]), write=(bank[sb],), inc="t")
                        run("scalar",
                            lambda e, sb=sb, pt=pt: e.activation(B(R_PT + pt * 512, 512), pb[sb][:, 0:512],
                                                                 AF.Exp, scale=SCALE),
                            read=(bank[sb],), write=(ptb[pt],), inc="s")

                    def pv_chunk(c, g=g, nch=nch):
                        pt = c % 4
                        for hl in range(2):
                            for qsub in range(2):
                                ob = 2 + hl * 2 + qsub
                                run("tensor",
                                    lambda e, pt=pt, hl=hl, qsub=qsub, ob=ob, c=c, g=g, nch=nch:
                                    e.matmul(pb[ob][:, 0:129],
                                             B(R_PT + pt * 512 + hl * 256 + qsub * 128, 128),
                                             B(R_VA + (g * 16 + c) * 129, 129),
                                             start=(c == 0), stop=(c == nch - 1)),
                                    read=(ptb[pt], vaug[c // 4]), write=(bank[ob],),
                                    inc="t" if c == nch - 1 else None)

                    PVD = 3
                    ot_pt = min(3, nch - 1)
                    for c in range(nch):
                        s_chunk(c)
                        if c == ot_pt and pend_otrans[0] is not None:
                            emit_otrans(pend_otrans[0])
                            pend_otrans[0] = None
                        if c >= PVD:
                            pv_chunk(c - PVD)
                    for c in range(max(0, nch - PVD), nch):
                        pv_chunk(c)
                    for hl in range(2):
                        for qsub in range(2):
                            ob = 2 + hl * 2 + qsub
                            ri = hl * 2 + qsub + (g % 2) * 4
                            run("vector",
                                lambda e, ob=ob, ri=ri: e.reciprocal(Fp(F_REC + ri, 1), pb[ob][:, 128:129]),
                                read=(bank[ob],), write=(orec,), inc="v")
                            run("vector",
                                lambda e, ob=ob, ri=ri, g=g, hl=hl, qsub=qsub: e.tensor_scalar(
                                    B(R_SQ + (g % 2) * 512 + (hl * 2 + qsub) * 128, 128),
                                    pb[ob][:, 0:128], Fp(F_REC + ri, 1), None, ALU.mult),
                                read=(bank[ob], orec), write=(sq,), inc="v")
                    pend_otrans[0] = g
                    if p2_fill:
                        p2_fill.pop(0)()
                for qsub in range(2):
                    tile = qb * 2 + qsub
                    for half in range(2):
                        pp = 6 + half
                        for hc in range(8):
                            run("tensor",
                                lambda e, hc=hc, qsub=qsub, half=half, pp=pp: e.matmul(
                                    pb[pp][:, 0:512],
                                    B(R_LNF + hc * 256 + qsub * 128, 128),
                                    B(W_O + hc * 1024 + half * 512, 512),
                                    start=(hc == 0), stop=(hc == 7)),
                                read=(lnf[hc // 2], wob), write=(bank[pp],),
                                inc="t" if hc == 7 else None)
                            if qsub == 0 and half == 0 and hc == 5 and pend_otrans[0] is not None:
                                emit_otrans(pend_otrans[0], tb=0)
                                pend_otrans[0] = None
                        run("vector",
                            lambda e, tile=tile, half=half, pp=pp: e.tensor_tensor(
                                B(R_X + tile * 1024 + half * 512, 512),
                                B(R_X + tile * 1024 + half * 512, 512),
                                pb[pp][:, 0:512], ALU.add),
                            read=(bank[pp], xp[tile // 4]), write=(xp[tile // 4],), inc="v")

            for e_ in (6, 0, 1, 2, 3):
                load_expert(e_, staged=(s > 0))

            for qs in range(NQS):
                ln_trans(qs)
                pr = qs % 2
                S = _st(pr)
                st = statsb[pr]
                for sub in range(4):
                    for k in range(8):
                        run("tensor",
                            lambda e, k=k, sub=sub: e.matmul(
                                pb[0][:, sub * 128:sub * 128 + 16],
                                B(R_LNF + k * 512 + sub * 128, 128),
                                B(R_WCAT + k * 16, 16),
                                start=(k == 0), stop=(k == 7)),
                            read=(lnf[k], wcat), write=(bank[0],),
                            inc="t" if k == 7 else None)
                cosp = pb[0][:, 0:512].rearrange("p (s c) -> p s c", c=128)[:, :, 0:8]
                routep = pb[0][:, 0:512].rearrange("p (s c) -> p s c", c=128)[:, :, 8:16]
                def b3(off, n1=4, n2=8):
                    return Fp(off, n1 * n2).rearrange("p (s c) -> p s c", c=n2)
                def bc(off):
                    return Fp(off, 4).unsqueeze(2).broadcast_to([128, 4, 8])
                run("vector",
                    lambda e, S=S: e.tensor_tensor(b3(GB_T), cosp, bc(S["IXN"]), ALU.mult),
                    read=(bank[0], st), write=(gat,), inc="v")
                run("vector", lambda e: e.tensor_tensor(b3(GB_L), b3(GB_T), routep, ALU.add),
                    read=(bank[0], gat), write=(gat,), inc="v")
                run("vector", lambda e: e.reduce_max(Fp(GB_M1, 4), b3(GB_L), AXX),
                    read=(gat,), write=(gat,), inc="v")
                run("vector", lambda e: e.tensor_scalar(Fp(GB_NM1, 4), Fp(GB_M1, 4), -1.0, None, ALU.mult),
                    read=(gat,), write=(gat,), inc="v")
                run("vector", lambda e: e.tensor_tensor(b3(GB_L), b3(GB_L), bc(GB_NM1), ALU.add),
                    read=(gat,), write=(gat,), inc="v")
                run("vector", lambda e: e.tensor_scalar(b3(GB_T), b3(GB_L), 0.0, None, ALU.is_ge),
                    read=(gat,), write=(gat,), inc="v")
                run("vector", lambda e: e.scalar_tensor_tensor(b3(GB_T), b3(GB_T), -1e30, b3(GB_L),
                                                               ALU.mult, ALU.add),
                    read=(gat,), write=(gat,), inc="v")
                run("vector", lambda e: e.reduce_max(Fp(GB_M2, 4), b3(GB_T), AXX),
                    read=(gat,), write=(gat,), inc="v")
                run("vector", lambda e: e.tensor_tensor(b3(GB_T), b3(GB_L), bc(GB_M2), ALU.is_ge),
                    read=(gat,), write=(gat,), inc="v")
                run("scalar", lambda e: e.activation(b3(GB_E), b3(GB_L), AF.Exp),
                    read=(gat,), write=(gat,), inc="s")
                run("vector", lambda e: e.tensor_tensor(b3(GB_L), b3(GB_E), b3(GB_T), ALU.mult),
                    read=(gat,), write=(gat,), inc="v")
                run("vector", lambda e: e.reduce_sum(Fp(GB_S, 4), b3(GB_L), AXX),
                    read=(gat,), write=(gat,), inc="v")
                run("vector", lambda e: e.reciprocal(Fp(GB_R, 4), Fp(GB_S, 4)),
                    read=(gat,), write=(gat,), inc="v")
                run("vector",
                    lambda e: e.tensor_tensor(
                        B(R_SQ, 32).rearrange("p (s c) -> p s c", c=8),
                        b3(GB_L), bc(GB_R), ALU.mult),
                    read=(gat,), write=(sq,), inc="v")
                def emit_gatesT():
                    for sub in range(4):
                        run("tensor",
                            lambda e, sub=sub: e.matmul(
                                pb[1][0:8, sub * 128:(sub + 1) * 128],
                                B(R_SQ + sub * 8, 8), B(R_ID, 128),
                                start=True, stop=True),
                            read=(sq, idb), write=(bank[1],),
                            inc="t" if sub == 3 else None)
                    run("scalar", lambda e: e.copy(B(R_GFM, 512, 0, 8), pb[1][0:8, 0:512]),
                        read=(bank[1],), write=(gfm,), inc="s")
                gatesT_pending = [True]

                for half_t in range(2):
                    if half_t == 1 and qs < NQS - 1:
                        ln_next = (qs + 1, True)
                    elif half_t == 1 and qs == NQS - 1 and s < NSEQ - 1:
                        ln_next = (0, False)
                    else:
                        ln_next = None
                    def w2_mms(ee, ei, stop=False):
                        base = exp_slot(ee)
                        gho = (ei % 2) * 1024
                        for qsub in range(2):
                            for half in range(2):
                                yb = 2 + qsub * 2 + half
                                for ko in range(4):
                                    run("tensor",
                                        lambda e, qsub=qsub, half=half, ko=ko, yb=yb, base=base,
                                               gho=gho, st=(ei == 0 and ko == 0),
                                               sp=(stop and ko == 3):
                                        e.matmul(pb[yb][:, 0:512],
                                                 B(R_GH + gho + ko * 256 + qsub * 128, 128),
                                                 B(base + 4096 + ko * 1024 + half * 512, 512),
                                                 start=st, stop=sp),
                                        read=(ghb[gho // 1024], ex[ee]), write=(bank[yb],),
                                        inc="t" if ko == 3 else None)

                    for ei, ee in enumerate(EXPERT_ORDER):
                        base = exp_slot(ee)
                        gb = ei % 2
                        gho = (ei % 2) * 1024

                        def h_group(ko, base=base, ee=ee, half_t=half_t):
                            pp = 6 + (ko % 2)
                            for k in range(8):
                                run("tensor",
                                    lambda e, k=k, ko=ko, base=base, half_t=half_t, pp=pp: e.matmul(
                                        pb[pp][:, 0:256],
                                        B(base + k * 512 + ko * 128, 128),
                                        B(R_LNF + k * 512 + half_t * 256, 256),
                                        start=(k == 0), stop=(k == 7)),
                                    read=(lnf[k], ex[ee]), write=(bank[pp],),
                                    inc="t" if k == 7 else None)

                        def gelu_gh(ko, ee=ee, gb=gb, gho=gho):
                            pp = 6 + (ko % 2)
                            run("scalar",
                                lambda e, ko=ko, ee=ee, pp=pp: e.activation(
                                    B(R_HS + (ko % 2) * 256, 256), pb[pp][:, 0:256],
                                    AF.Gelu, bias=Fp(F_B1 + ee * 4 + ko, 1)),
                                read=(bank[pp], bb1), write=(hsb[ko % 2],), inc="s")
                            run("vector",
                                lambda e, ko=ko, gb=gb, gho=gho: e.tensor_tensor(
                                    B(R_GH + gho + ko * 256, 256), B(R_HS + (ko % 2) * 256, 256),
                                    pb[gb][:, 0:256], ALU.mult),
                                read=(hsb[ko % 2], bank[gb]), write=(ghb[gho // 1024],), inc="v")

                        def emit_gb(ee=ee, half_t=half_t, gb=gb):
                            run("tensor",
                                lambda e, ee=ee, half_t=half_t, gb=gb: e.matmul(
                                    pb[gb][:, 0:256],
                                    B(R_SEL + ee * 128, 128, 0, 8),
                                    B(R_GFM + half_t * 256, 256, 0, 8),
                                    start=True, stop=True),
                                read=(gfm, selb), write=(bank[gb],), inc="t")

                        if gatesT_pending[0]:
                            h_group(0)
                            emit_gatesT()
                            gatesT_pending[0] = False
                            emit_gb()
                            gelu_gh(0)
                            for ko in range(1, 4):
                                h_group(ko)
                                gelu_gh(ko)
                        else:
                            emit_gb()
                            for ko in range(4):
                                h_group(ko)
                                gelu_gh(ko)
                        if ei >= 1:
                            w2_mms(EXPERT_ORDER[ei - 1], ei - 1)
                        if ln_next is not None and 2 <= ei <= 5:
                            if ei <= 3:
                                ln_accum(ln_next[0], 2 * (ei - 2))
                                ln_accum(ln_next[0], 2 * (ei - 2) + 1)
                            elif ei == 4:
                                ln_math(ln_next[0], want_xnorm=ln_next[1])
                            else:
                                for i4_ in range(4):
                                    ln_norm_tile(ln_next[0], i4_)
                    for qsub in range(2):
                        for half in range(2):
                            yb = 2 + qsub * 2 + half
                            run("tensor",
                                lambda e, qsub=qsub, half=half, yb=yb, half_t=half_t: e.matmul(
                                    pb[yb][:, 0:512],
                                    B(R_GFM + (half_t * 2 + qsub) * 128, 128, 0, 8),
                                    B(R_B2 + half * 512, 512, 0, 8),
                                    start=False, stop=False),
                                read=(gfm, bb2), write=(bank[yb],))
                    w2_mms(EXPERT_ORDER[7], 7, stop=True)
                    if qs == NQS - 1 and half_t == 1 and s < NSEQ - 1:
                        load_qkv_weights(staged=True)
                        load_wo(staged=True)
                    for qsub in range(2):
                        for half in range(2):
                            yb = 2 + qsub * 2 + half
                            tile = qs * 4 + half_t * 2 + qsub
                            run("vector",
                                lambda e, tile=tile, half=half, yb=yb: e.tensor_tensor(
                                    B(R_X + tile * 1024 + half * 512, 512),
                                    B(R_X + tile * 1024 + half * 512, 512),
                                    pb[yb][:, 0:512], ALU.add),
                                read=(bank[yb], xp[tile // 4]), write=(xp[tile // 4],), inc="v")
                dma("gpsimd",
                    out[s, qs * 512:(qs + 1) * 512, :].rearrange("(i p) d -> p i d", p=128),
                    B(R_X + qs * 4096, 4096).rearrange("p (i d) -> p i d", d=1024),
                    f"do{qs}", rbufs=(xp[qs],))
                if s < NSEQ - 1:
                    load_x(s + 1, qs)



        P.emit("gpsimd", lambda e: None,
               waits=[(f"do{i}", 16 * NSEQ) for i in range(4)])

        with nc.Block() as block:
            @block.tensor
            def _(eng):
                for op in P.ops["tensor"]:
                    op(eng, sems)

            @block.vector
            def _(eng):
                for op in P.ops["vector"]:
                    op(eng, sems)

            @block.scalar
            def _(eng):
                for op in P.ops["scalar"]:
                    op(eng, sems)

            @block.gpsimd
            def _(eng):
                for op in P.ops["gpsimd"]:
                    op(eng, sems)

            @block.sync
            def _(eng):
                for op in P.ops["sync"]:
                    op(eng, sems)

    return nc


_NC_CACHE = [None]


def kernel(L0_attn_hidden_states, L0_attn_Wq, L0_attn_Wk, L0_attn_Wv, L0_attn_Wo,
           L0_ffn_w1, L0_ffn_b1, L0_ffn_w2, L0_ffn_b2, L0_ffn_centroids,
           L0_ffn_w_route):
    x = np.ascontiguousarray(np.asarray(L0_attn_hidden_states, dtype=np.float32))
    consts = _host_consts()
    base = dict(wq=np.ascontiguousarray(L0_attn_Wq, dtype=np.float32),
                wk=np.ascontiguousarray(L0_attn_Wk, dtype=np.float32),
                wv=np.ascontiguousarray(L0_attn_Wv, dtype=np.float32),
                wo=np.ascontiguousarray(L0_attn_Wo, dtype=np.float32),
                w1=np.ascontiguousarray(L0_ffn_w1, dtype=np.float32),
                b1=np.ascontiguousarray(np.asarray(L0_ffn_b1, dtype=np.float32)
                                        .reshape(8, 4, 128).transpose(2, 0, 1)
                                        .reshape(128, 32)),
                w2=np.ascontiguousarray(L0_ffn_w2, dtype=np.float32),
                b2=np.ascontiguousarray(L0_ffn_b2, dtype=np.float32),
                cent=np.ascontiguousarray(L0_ffn_centroids, dtype=np.float32),
                wrt=np.ascontiguousarray(L0_ffn_w_route, dtype=np.float32),
                **consts)
    if _NC_CACHE[0] is None:
        _NC_CACHE[0] = build_nc()
    nc = _NC_CACHE[0]
    in_maps = []
    for c in range(NCORE):
        m = dict(base)
        m["x"] = np.ascontiguousarray(x[c * NSEQ:(c + 1) * NSEQ])
        in_maps.append(m)
    res = run_bass_kernel_spmd(nc, in_maps, core_ids=list(range(NCORE)))
    return np.concatenate([r["out"] for r in res.results], axis=0).reshape(32, T, DM)


# version 29
# speedup vs baseline: 1.0668x; 1.0060x over previous
import numpy as np
import ml_dtypes

import concourse.bass as bass
import concourse.mybir as mybir
from concourse.bass_utils import run_bass_kernel_spmd

BF16 = mybir.dt.bfloat16
F32 = mybir.dt.float32
AF = mybir.ActivationFunctionType
ALU = mybir.AluOpType
AXX = mybir.AxisListType.X

NCORE = 8
NSEQ = 4
T = 2048
DM = 1024
NH, NKV, HD = 8, 4, 128
DEXP, NEXP = 512, 8
SCALE = float(np.float32(1.0) / np.sqrt(np.float32(128.0)))
MASKVAL = -1.0e6

NQS = T // 512
NQB = T // 256

_off = [0]
def _alloc(n):
    o = _off[0]; _off[0] += n; return o

R_X    = _alloc(16 * 1024)
R_MASK = _alloc(384)
R_ID   = _alloc(128)
R_PROT = _alloc(128)
R_SEL  = _alloc(1024)
R_WCAT = _alloc(128)
R_B2   = _alloc(1024)
R_LNT  = _alloc(4096)
R_SQ   = _alloc(1024)
R_LNF  = _alloc(4096)
R_HS   = _alloc(512)
R_GH   = _alloc(2048)
R_GFM  = _alloc(512)
R_PT   = _alloc(2048)
R_QF   = _alloc(16384)
R_KF   = _alloc(8192)
R_VA   = _alloc(4 * 16 * 129)
R_E    = _alloc(32768)
NB = _off[0]

EXB = R_QF + 64
def exp_slot(e):
    return EXB + 8192 * e
W_QKV = exp_slot(4)
W_O = exp_slot(6)
assert exp_slot(4) == R_E
assert exp_slot(8) == NB

_f = [0]
def _falloc(n):
    o = _f[0]; _f[0] += n; return o
F_CENT = _falloc(1024)
F_ST   = _falloc(128)
F_REC  = _falloc(8)
F_GAT  = _falloc(128)
F_B1   = _falloc(32)
F_COS  = _falloc(512)
F_SIN  = _falloc(512)
NF = _f[0]
assert NB * 2 + NF * 4 <= 212000, (NB, NF)

def _st(parity):
    b = F_ST + 64 * parity
    return dict(SUM=b, SQ=b + 4, M=b + 8, MSQ=b + 12, V=b + 16, RS=b + 20,
                NMR=b + 24, SV=b + 28, XN=b + 32, IXN=b + 36, T0=b + 40, T1=b + 44)
GB_L, GB_T, GB_M1, GB_NM1, GB_M2, GB_S, GB_R, GB_E = (
    F_GAT + 0, F_GAT + 32, F_GAT + 64, F_GAT + 68, F_GAT + 72, F_GAT + 76, F_GAT + 80,
    F_GAT + 88)


def _host_consts():
    bf = ml_dtypes.bfloat16
    try:
        import jax
        import jax.numpy as jnp
        with jax.default_device(jax.devices("cpu")[0]):
            inv_freq = 1.0 / 10000.0 ** (jnp.arange(0, HD, 2, dtype=jnp.float32) / HD)
            freqs = jnp.outer(jnp.arange(T, dtype=jnp.float32), inv_freq)
            emb = jnp.concatenate((freqs, freqs), axis=-1)
            cos = np.asarray(jnp.cos(emb), dtype=np.float32)
            sin = np.asarray(jnp.sin(emb), dtype=np.float32)
    except Exception:
        inv_freq = (1.0 / (10000.0 ** (np.arange(0, HD, 2, dtype=np.float32) / np.float32(HD)))).astype(np.float32)
        freqs = np.outer(np.arange(T, dtype=np.float32), inv_freq).astype(np.float32)
        emb = np.concatenate([freqs, freqs], axis=1)
        cos = np.cos(emb).astype(np.float32)
        sin = np.sin(emb).astype(np.float32)
    sin_signed = sin.copy()
    sin_signed[:, :64] *= -1.0
    c_cos = np.ascontiguousarray(cos.T).astype(np.float32)
    c_sin = np.ascontiguousarray(sin_signed.T).astype(np.float32)
    p = np.arange(128)[:, None]
    v = np.arange(128, 512)[None, :]
    c_mask = np.where(p <= v - 256, 0.0, MASKVAL).astype(bf)
    c_id = np.eye(128, dtype=np.float32).astype(bf)
    prot = np.zeros((128, 128), dtype=np.float32)
    prot[np.arange(128), (np.arange(128) + 64) % 128] = 1.0
    c_prot = prot.astype(bf)
    sel = np.zeros((8, 8, 128), dtype=np.float32)
    for e in range(8):
        sel[e, e, :] = 1.0
    c_sel = sel.transpose(1, 0, 2).reshape(8, 1024).astype(bf)
    return dict(c_cos=c_cos, c_sin=c_sin, c_mask=c_mask, c_id=c_id,
                c_prot=c_prot, c_sel=np.ascontiguousarray(c_sel))


class Prog:
    ENGS = ("tensor", "vector", "scalar", "gpsimd", "sync")

    def __init__(self):
        self.ops = {e: [] for e in self.ENGS}
        self.cnt = {}
        self.known = {e: {} for e in self.ENGS}

    def emit(self, eng, fn, waits=(), inc=None, inc_by=1):
        w = []
        for s, v in waits:
            if v <= 0 or self.known[eng].get(s, 0) >= v:
                continue
            self.known[eng][s] = v
            w.append((s, v))
        val = None
        if inc is not None:
            self.cnt[inc] = self.cnt.get(inc, 0) + inc_by
            val = self.cnt[inc]

        def op(engine, sems, w=w, fn=fn, inc=inc, inc_by=inc_by):
            for s, v in w:
                engine.wait_ge(sems[s], v)
            ins = fn(engine)
            if inc is not None and ins is not None:
                ins.then_inc(sems[inc], inc_by)
        self.ops[eng].append(op)
        return (inc, val) if inc is not None else None

    def c(self, sem):
        return self.cnt.get(sem, 0)


class Buf:
    def __init__(self):
        self.w = {}
        self.r = {}

    @staticmethod
    def _m(d, ev):
        if ev is None:
            return
        s, v = ev
        if d.get(s, 0) < v:
            d[s] = v

    def wrote(self, ev):
        self._m(self.w, ev)

    def read(self, ev):
        self._m(self.r, ev)

    @property
    def need_for_read(self):
        return list(self.w.items())

    @property
    def need_for_write(self):
        out = dict(self.w)
        for s, v in self.r.items():
            if out.get(s, 0) < v:
                out[s] = v
        return list(out.items())


def build_nc():
    nc = bass.Bass("TRN2", target_bir_lowering=False)
    xin = nc.dram_tensor("x", [NSEQ, T, DM], F32, kind="ExternalInput").ap()
    wq = nc.dram_tensor("wq", [DM, NH * HD], F32, kind="ExternalInput").ap()
    wk = nc.dram_tensor("wk", [DM, NKV * HD], F32, kind="ExternalInput").ap()
    wv = nc.dram_tensor("wv", [DM, NKV * HD], F32, kind="ExternalInput").ap()
    wo = nc.dram_tensor("wo", [NH * HD, DM], F32, kind="ExternalInput").ap()
    w1 = nc.dram_tensor("w1", [NEXP, DM, DEXP], F32, kind="ExternalInput").ap()
    b1d = nc.dram_tensor("b1", [128, 32], F32, kind="ExternalInput").ap()
    w2 = nc.dram_tensor("w2", [NEXP, DEXP, DM], F32, kind="ExternalInput").ap()
    b2d = nc.dram_tensor("b2", [NEXP, DM], F32, kind="ExternalInput").ap()
    cent = nc.dram_tensor("cent", [NEXP, DM], F32, kind="ExternalInput").ap()
    wrtd = nc.dram_tensor("wrt", [NEXP, DM], F32, kind="ExternalInput").ap()
    cc = {}
    for n, shp in (("c_mask", [128, 384]), ("c_id", [128, 128]),
                   ("c_prot", [128, 128]), ("c_sel", [8, 1024])):
        cc[n] = nc.dram_tensor(n, shp, BF16, kind="ExternalInput").ap()
    for n in ("c_cos", "c_sin"):
        cc[n] = nc.dram_tensor(n, [128, T], F32, kind="ExternalInput").ap()
    out = nc.dram_tensor("out", [NSEQ, T, DM], F32, kind="ExternalOutput").ap()
    wstage = nc.dram_tensor("wstage", [128, 24576 + NEXP * 8192], BF16, kind="Internal").ap()

    import contextlib
    with contextlib.ExitStack() as ctx:
        AB = ctx.enter_context(nc.sbuf_tensor("AB", [128, NB], BF16))
        AFP = ctx.enter_context(nc.sbuf_tensor("AFp", [128, NF], F32))
        pb = [ctx.enter_context(nc.psum_tensor(f"pb{i}", [128, 512], F32))
              for i in range(8)]
        semnames = (("t", "v", "s", "g", "dc", "dcs", "dwi", "dcen", "dwq", "dwo", "hwq", "hwo")
                    + tuple(f"dx{i}" for i in range(4))
                    + tuple(f"do{i}" for i in range(4))
                    + tuple(f"dwe{e}" for e in range(NEXP))
                    + tuple(f"hwe{e}" for e in range(NEXP)))
        sems = {n: ctx.enter_context(nc.semaphore(f"sem_{n}")) for n in semnames}

        P = Prog()
        ab = AB.ap()
        af = AFP.ap()

        def B(off, n, p0=0, p1=128):
            return ab[p0:p1, off:off + n]

        def Fp(off, n, p0=0, p1=128):
            return af[p0:p1, off:off + n]

        bank = [Buf() for _ in range(8)]
        bufs = {}
        def buf(name):
            if name not in bufs:
                bufs[name] = Buf()
            return bufs[name]

        def dma(queue, outap, inap, sem, rbufs=(), wbufs=()):
            waits = {}
            for b in rbufs:
                for s, v in b.need_for_read:
                    waits[s] = max(waits.get(s, 0), v)
            for b in wbufs:
                for s, v in b.need_for_write:
                    waits[s] = max(waits.get(s, 0), v)
            ev = P.emit(queue, lambda e, o=outap, i=inap: e.dma_start(out=o, in_=i),
                        waits=list(waits.items()), inc=sem, inc_by=16)
            for b in rbufs:
                b.read(ev)
            for b in wbufs:
                b.wrote(ev)
            return ev

        def run(eng, fn, read=(), write=(), extra=(), inc=None):
            waits = {}
            for b in read:
                for s, v in b.need_for_read:
                    waits[s] = max(waits.get(s, 0), v)
            for b in write:
                for s, v in b.need_for_write:
                    waits[s] = max(waits.get(s, 0), v)
            for s, v in extra:
                waits[s] = max(waits.get(s, 0), v)
            ev = P.emit(eng, fn, waits=list(waits.items()), inc=inc)
            for b in read:
                b.read(ev if ev is not None else (eng_sem(eng), P.c(eng_sem(eng))))
            for b in write:
                b.wrote(ev)
            return ev

        def eng_sem(eng):
            return {"tensor": "t", "vector": "v", "scalar": "s", "gpsimd": "g"}.get(eng, "t")

        cosb, sinb, maskb, idb, protb, selb = (buf(x) for x in
            ("cos", "sin", "mask", "id", "prot", "sel"))
        wqkv, wob, wcat, bb1, bb2 = (buf(x) for x in ("wqkv", "wob", "wcat", "b1", "b2"))
        ex = [buf(f"ex{e}") for e in range(NEXP)]
        xp = [buf(f"xp{i}") for i in range(4)]
        qf = [buf(f"qf{j}") for j in range(4)]
        kf = [buf(f"kf{j}") for j in range(4)]
        vaug = [buf(f"va{j}") for j in range(4)]
        lnt = [buf("lnt_a"), buf("lnt_b"), buf("lnt_c")]
        sq = buf("sq")
        lnf = [buf(f"lnf{k}") for k in range(8)]
        ptb = [buf(f"pt{i}") for i in range(4)]
        hsb = [buf(f"hs{i}") for i in range(2)]
        ghb, gfm = [buf("gh0"), buf("gh1")], buf("gfm")
        stats, gat, orec, cstage = buf("stats"), buf("gat"), buf("orec"), buf("cstage")

        dma("sync", B(R_MASK, 384), cc["c_mask"], "dc", wbufs=(maskb,))
        dma("sync", B(R_ID, 128), cc["c_id"], "dc", wbufs=(idb,))
        dma("sync", B(R_PROT, 128), cc["c_prot"], "dc", wbufs=(protb,))
        dma("sync", B(R_SEL, 1024, 0, 8), cc["c_sel"], "dc", wbufs=(selb,))
        for b_ in (cosb, sinb, maskb, idb, protb, selb):
            b_.wrote(("dc", P.c("dc")))

        for part_ in range(4):
            dma("gpsimd",
                B(R_X + part_ * 4096, 4096).rearrange("p (i d) -> p i d", d=1024),
                xin[0, part_ * 512:(part_ + 1) * 512, :].rearrange("(i p) d -> p i d", p=128),
                f"dx{part_}", wbufs=(xp[part_],))
        dma("gpsimd", Fp(F_B1, 32), b1d, "dwi", wbufs=(bb1,))
        dma("gpsimd", B(R_B2, 1024, 0, 8), b2d, "dwi", wbufs=(bb2,))
        bb1.wrote(("dwi", P.c("dwi")))
        bb2.wrote(("dwi", P.c("dwi")))

        ones_ap = ab[:, R_VA + 128: R_VA + 64 * 129: 129]

        for idx, (src, colo) in enumerate(((cent, 0), (wrtd, 8))):
            dma("gpsimd", Fp(F_CENT, 1024, 0, 8), src, "dcen", wbufs=(cstage,))
            if idx == 0:
                run("scalar",
                    lambda e: e.activation(B(R_LNT, 1024, 0, 8), Fp(F_CENT, 1024, 0, 8),
                                           AF.Square, accum_out=Fp(F_GAT, 1, 0, 8)),
                    read=(cstage,), write=(lnt[0], gat), inc="s")
                run("scalar", lambda e: e.activation(Fp(F_GAT + 1, 1, 0, 8), Fp(F_GAT, 1, 0, 8), AF.Ln),
                    read=(gat,), write=(gat,), inc="s")
                run("scalar", lambda e: e.activation(Fp(F_GAT + 2, 1, 0, 8), Fp(F_GAT + 1, 1, 0, 8),
                                                     AF.Exp, scale=0.5),
                    read=(gat,), write=(gat,), inc="s")
                run("vector", lambda e: e.tensor_scalar_max(Fp(F_GAT + 3, 1, 0, 8), Fp(F_GAT + 2, 1, 0, 8), 1e-12),
                    read=(gat,), write=(gat,), inc="v")
                run("vector", lambda e: e.reciprocal(Fp(F_GAT + 4, 1, 0, 8), Fp(F_GAT + 3, 1, 0, 8)),
                    read=(gat,), write=(gat,), inc="v")
                run("vector",
                    lambda e: e.tensor_scalar(B(R_LNT, 1024, 0, 8), Fp(F_CENT, 1024, 0, 8),
                                              Fp(F_GAT + 4, 1, 0, 8), None, ALU.mult),
                    read=(cstage, gat), write=(lnt[0],), inc="v")
            else:
                run("vector", lambda e: e.tensor_copy(B(R_LNT, 1024, 0, 8), Fp(F_CENT, 1024, 0, 8)),
                    read=(cstage,), write=(lnt[0],), inc="v")
            for k in range(8):
                run("tensor",
                    lambda e, k=k: e.matmul(pb[0][:, 0:8], B(R_LNT + k * 128, 128, 0, 8),
                                            B(R_ID, 8, 0, 8), start=True, stop=True),
                    read=(lnt[0], idb), write=(bank[0],), inc="t")
                run("scalar",
                    lambda e, k=k, c=colo: e.copy(B(R_WCAT + k * 16 + c, 8), pb[0][:, 0:8]),
                    read=(bank[0],), write=(wcat,), inc="s")

        stg_qkv, stg_wo = buf("stg_qkv"), buf("stg_wo")
        stg_ex = [buf(f"stg_ex{e}") for e in range(NEXP)]

        def load_qkv_weights(staged):
            if staged:
                dma("sync", B(W_QKV, 16384), wstage[:, 0:16384], "hwq",
                    rbufs=(stg_qkv,), wbufs=(wqkv, ex[4], ex[5]))
            else:
                dma("gpsimd", B(W_QKV, 8192).rearrange("p (k n) -> p k n", n=1024),
                    wq.rearrange("(k p) n -> p k n", p=128), "dwq", wbufs=(wqkv, ex[4]))
                dma("gpsimd", B(W_QKV + 8192, 4096).rearrange("p (k n) -> p k n", n=512),
                    wk.rearrange("(k p) n -> p k n", p=128), "dwq", wbufs=(wqkv, ex[4], ex[5]))
                dma("gpsimd", B(W_QKV + 12288, 4096).rearrange("p (k n) -> p k n", n=512),
                    wv.rearrange("(k p) n -> p k n", p=128), "dwq", wbufs=(wqkv, ex[5]))
            wqkv.wrote(("dwq", P.c("dwq")))

        def stage_qkv_wo():
            dma("sync", wstage[:, 0:16384], B(W_QKV, 16384), "hwq", rbufs=(wqkv,),
                wbufs=(stg_qkv,))
            dma("sync", wstage[:, 16384:24576], B(W_O, 8192), "hwo", rbufs=(wob,),
                wbufs=(stg_wo,))

        def load_wo(staged):
            if staged:
                dma("sync", B(W_O, 8192), wstage[:, 16384:24576], "hwo",
                    rbufs=(stg_wo,), wbufs=(wob, ex[6]))
            else:
                dma("gpsimd", B(W_O, 8192).rearrange("p (k n) -> p k n", n=1024),
                    wo.rearrange("(k p) n -> p k n", p=128), "dwo", wbufs=(wob, ex[6]))

        def load_expert(e, staged):
            base = exp_slot(e)
            confl = [ex[e]]
            if e in (4, 5):
                confl.append(wqkv)
            elif e == 6:
                confl.append(wob)
            elif e <= 3:
                confl += qf + kf + vaug + [ptb[0], ptb[1]]
            if staged:
                dma("sync", B(base, 8192), wstage[:, 24576 + e * 8192: 24576 + (e + 1) * 8192],
                    f"hwe{e}", rbufs=(stg_ex[e],), wbufs=tuple(confl))
            else:
                dma("gpsimd", B(base, 4096).rearrange("p (k n) -> p k n", n=512),
                    w1[e].rearrange("(k p) n -> p k n", p=128), f"dwe{e}", wbufs=tuple(confl))
                dma("gpsimd", B(base + 4096, 4096).rearrange("p (k n) -> p k n", n=1024),
                    w2[e].rearrange("(k p) n -> p k n", p=128), f"dwe{e}", wbufs=tuple(confl))
                dma("sync", wstage[:, 24576 + e * 8192: 24576 + (e + 1) * 8192], B(base, 8192),
                    f"hwe{e}", rbufs=tuple([ex[e]] + confl[1:]), wbufs=(stg_ex[e],))
            ex[e].wrote((f"dwe{e}", P.c(f"dwe{e}")))

        load_qkv_weights(staged=False)
        load_wo(staged=False)
        stage_qkv_wo()

        EXPERT_ORDER = [4, 5, 7, 6, 0, 1, 2, 3]

        statsb = [buf("stats0"), buf("stats1")]
        stsum = [buf("stsum0"), buf("stsum1")]
        stsq = [buf("stsq0"), buf("stsq1")]

        def ln_accum(qs, i4):
            pr = qs % 2
            S = _st(pr)
            tile = qs * 4 + i4
            xap = B(R_X + tile * 1024, 1024)
            run("vector", lambda e, i4=i4, xap=xap, S=S: e.reduce_sum(Fp(S["SUM"] + i4, 1), xap, AXX),
                read=(xp[tile // 4],), write=(stsum[pr],), inc="v")
            run("scalar",
                lambda e, i4=i4, xap=xap, S=S: e.activation(B(R_LNT + 3072, 1024), xap, AF.Square,
                                                           accum_out=Fp(S["SQ"] + i4, 1)),
                read=(xp[tile // 4],), write=(lnt[2], stsq[pr]), inc="s")

        def ln_math(qs, want_xnorm, lo=0, hi=4):
            pr = qs % 2
            S = _st(pr)
            st = statsb[pr]
            n_ = hi - lo
            run("vector", lambda e, S=S, lo=lo, n_=n_: e.tensor_scalar(Fp(S["M"] + lo, n_), Fp(S["SUM"] + lo, n_), 1.0 / 1024.0, None, ALU.mult),
                read=(stsum[pr],), write=(st,), inc="v")
            run("vector", lambda e, S=S, lo=lo, n_=n_: e.tensor_scalar(Fp(S["MSQ"] + lo, n_), Fp(S["SQ"] + lo, n_), 1.0 / 1024.0, None, ALU.mult),
                read=(stsq[pr],), write=(st,), inc="v")
            run("vector", lambda e, S=S, lo=lo, n_=n_: e.tensor_tensor(Fp(S["T0"] + lo, n_), Fp(S["M"] + lo, n_), Fp(S["M"] + lo, n_), ALU.mult),
                read=(st,), write=(st,), inc="v")
            run("vector", lambda e, S=S, lo=lo, n_=n_: e.tensor_tensor(Fp(S["V"] + lo, n_), Fp(S["MSQ"] + lo, n_), Fp(S["T0"] + lo, n_), ALU.subtract),
                read=(st,), write=(st,), inc="v")
            run("vector", lambda e, S=S, lo=lo, n_=n_: e.tensor_scalar(Fp(S["T1"] + lo, n_), Fp(S["V"] + lo, n_), 1e-5, None, ALU.add),
                read=(st,), write=(st,), inc="v")
            run("scalar", lambda e, S=S, lo=lo, n_=n_: e.activation(Fp(S["T0"] + lo, n_), Fp(S["T1"] + lo, n_), AF.Ln),
                read=(st,), write=(st,), inc="s")
            run("scalar", lambda e, S=S, lo=lo, n_=n_: e.activation(Fp(S["RS"] + lo, n_), Fp(S["T0"] + lo, n_), AF.Exp, scale=-0.5),
                read=(st,), write=(st,), inc="s")
            run("vector", lambda e, S=S, lo=lo, n_=n_: e.scalar_tensor_tensor(Fp(S["NMR"] + lo, n_), Fp(S["M"] + lo, n_), -1.0, Fp(S["RS"] + lo, n_),
                                                                ALU.mult, ALU.mult),
                read=(st,), write=(st,), inc="v")
            if want_xnorm:
                run("scalar", lambda e, S=S, lo=lo, n_=n_: e.activation(Fp(S["T1"] + lo, n_), Fp(S["V"] + lo, n_), AF.Ln),
                    read=(st,), write=(st,), inc="s")
                run("scalar", lambda e, S=S, lo=lo, n_=n_: e.activation(Fp(S["SV"] + lo, n_), Fp(S["T1"] + lo, n_), AF.Exp, scale=0.5),
                    read=(st,), write=(st,), inc="s")
                run("vector", lambda e, S=S, lo=lo, n_=n_: e.tensor_tensor(Fp(S["XN"] + lo, n_), Fp(S["SV"] + lo, n_), Fp(S["RS"] + lo, n_), ALU.mult),
                    read=(st,), write=(st,), inc="v")
                run("vector", lambda e, S=S, lo=lo, n_=n_: e.tensor_scalar(Fp(S["XN"] + lo, n_), Fp(S["XN"] + lo, n_), 32.0, 1e-12, ALU.mult, ALU.max),
                    read=(st,), write=(st,), inc="v")
                run("vector", lambda e, S=S, lo=lo, n_=n_: e.reciprocal(Fp(S["IXN"] + lo, n_), Fp(S["XN"] + lo, n_)),
                    read=(st,), write=(st,), inc="v")

        def ln_stats(qs, want_xnorm):
            for i4 in range(4):
                ln_accum(qs, i4)
            ln_math(qs, want_xnorm)

        def ln_norm(qs):
            pr = qs % 2
            S = _st(pr)
            st = statsb[pr]
            for i4 in range(4):
                ln_norm_tile(qs, i4)

        def ln_norm_tile(qs, i4):
            pr = qs % 2
            S = _st(pr)
            st = statsb[pr]
            tile = qs * 4 + i4
            lb = lnt[0] if i4 < 2 else (lnt[1] if i4 == 2 else lnt[2])
            run("vector",
                lambda e, i4=i4, tile=tile, S=S: e.tensor_scalar(
                    B(R_LNT + i4 * 1024, 1024), B(R_X + tile * 1024, 1024),
                    Fp(S["RS"] + i4, 1), Fp(S["NMR"] + i4, 1), ALU.mult, ALU.add),
                read=(xp[tile // 4], st), write=(lb,), inc="v")

        def ln_trans(qs):
            for k in range(8):
                tb = k % 2
                for i4 in range(4):
                    lb = lnt[0] if i4 < 2 else (lnt[1] if i4 == 2 else lnt[2])
                    run("tensor",
                        lambda e, k=k, i4=i4, tb=tb: e.matmul(
                            pb[tb][:, i4 * 128:(i4 + 1) * 128],
                            B(R_LNT + i4 * 1024 + k * 128, 128), B(R_ID, 128),
                            start=True, stop=True),
                        read=(lb, idb), write=(bank[tb],),
                        inc="t" if i4 == 3 else None)
                run("scalar", lambda e, k=k, tb=tb: e.copy(B(R_LNF + k * 512, 512), pb[tb][:, 0:512]),
                    read=(bank[tb],), write=(lnf[k],), inc="s")

        def load_x(s, part):
            dma("gpsimd",
                B(R_X + part * 4096, 4096).rearrange("p (i d) -> p i d", d=1024),
                xin[s, part * 512:(part + 1) * 512, :].rearrange("(i p) d -> p i d", p=128),
                f"dx{part}", wbufs=(xp[part],))

        for s in range(NSEQ):
            run("vector", lambda e: e.memset(ones_ap, 1.0), write=tuple(vaug), inc="v")

            for qs in range(NQS):
                if qs == 0 and s == 0:
                    ln_stats(0, want_xnorm=False)
                    ln_norm(0)
                dma("sync", Fp(F_COS, 512), cc["c_cos"][:, qs * 512:(qs + 1) * 512],
                    "dcs", wbufs=(cosb,))
                dma("sync", Fp(F_SIN, 512), cc["c_sin"][:, qs * 512:(qs + 1) * 512],
                    "dcs", wbufs=(sinb,))
                cosb.wrote(("dcs", P.c("dcs")))
                sinb.wrote(("dcs", P.c("dcs")))
                ln_trans(qs)
                cosap = Fp(F_COS, 512)
                sinap = Fp(F_SIN, 512)

                def qkv_epilogue(m, qs=qs):
                    pp = 6 + (m % 2)
                    rb = 2 + (m % 2)
                    sl = (m % 2) * 512
                    if m < 8:
                        dstoff, dstbuf = R_QF + m * T + qs * 512, qf[qs]
                    else:
                        dstoff, dstbuf = R_KF + (m - 8) * T + qs * 512, kf[qs]
                    run("scalar", lambda e, pp=pp, sl=sl: e.copy(B(R_SQ + sl, 512), pb[pp][:, 0:512]),
                        read=(bank[pp],), write=(sq,), inc="s")
                    run("tensor",
                        lambda e, rb=rb, sl=sl: e.matmul(pb[rb][:, 0:512], B(R_PROT, 128),
                                                         B(R_SQ + sl, 512), start=True, stop=True),
                        read=(sq, protb), write=(bank[rb],), inc="t")
                    run("vector",
                        lambda e, cosap=cosap, sl=sl: e.tensor_tensor(
                            B(R_GH + 1024 + sl, 512), B(R_SQ + sl, 512), cosap, ALU.mult),
                        read=(sq, cosb), write=(ghb[1],), inc="v")
                    run("vector",
                        lambda e, sinap=sinap, rb=rb, sl=sl: e.tensor_tensor(
                            B(R_GH + sl, 512), pb[rb][:, 0:512], sinap, ALU.mult),
                        read=(bank[rb], sinb), write=(ghb[0],), inc="v")
                    run("vector",
                        lambda e, dstoff=dstoff, sl=sl: e.tensor_tensor(
                            B(dstoff, 512), B(R_GH + 1024 + sl, 512),
                            B(R_GH + sl, 512), ALU.add),
                        read=(ghb[0], ghb[1]), write=(dstbuf,), inc="v")

                for m in range(12):
                    pp = 6 + (m % 2)
                    if m < 8:
                        wbase, wstride, mm = W_QKV, 1024, m
                    else:
                        wbase, wstride, mm = W_QKV + 8192, 512, m - 8
                    for k in range(8):
                        run("tensor",
                            lambda e, k=k, mm=mm, pp=pp, wbase=wbase, wstride=wstride:
                            e.matmul(pb[pp][:, 0:512],
                                     B(wbase + k * wstride + mm * 128, 128),
                                     B(R_LNF + k * 512, 512),
                                     start=(k == 0), stop=(k == 7)),
                            read=(lnf[k], wqkv), write=(bank[pp],),
                            inc="t" if k == 7 else None)
                    if m >= 1:
                        qkv_epilogue(m - 1)
                qkv_epilogue(11)

                def v_evac(i4, qs=qs):
                    pp = 6 + (i4 % 2)
                    j = qs * 4 + i4
                    for g in range(4):
                        run("vector",
                            lambda e, g=g, j=j, pp=pp: e.tensor_copy(
                                B(R_VA + (g * 16 + j) * 129, 128), pb[pp][:, g * 128:(g + 1) * 128]),
                            read=(bank[pp],), write=(vaug[qs],), inc="v")

                for i4 in range(4):
                    pp = 6 + (i4 % 2)
                    for k in range(8):
                        run("tensor",
                            lambda e, k=k, i4=i4, pp=pp: e.matmul(
                                pb[pp][:, 0:512],
                                B(R_LNF + k * 512 + i4 * 128, 128),
                                B(W_QKV + 12288 + k * 512, 512),
                                start=(k == 0), stop=(k == 7)),
                            read=(lnf[k], wqkv), write=(bank[pp],),
                            inc="t" if k == 7 else None)
                    if qs < NQS - 1:
                        ln_accum(qs + 1, i4)
                        if i4 == 2:
                            ln_math(qs + 1, want_xnorm=False, lo=0, hi=3)
                            for i4_ in range(3):
                                ln_norm_tile(qs + 1, i4_)
                        elif i4 == 3:
                            ln_math(qs + 1, want_xnorm=False, lo=3, hi=4)
                            ln_norm_tile(qs + 1, 3)
                    if i4 >= 1:
                        v_evac(i4 - 1)
                v_evac(3)

            pend_otrans = [None]
            p2_fill = []
            for qb in range(NQB):
                if qb == 1:
                    load_expert(4, staged=(s > 0))
                    load_expert(5, staged=(s > 0))
                    if s == 0:
                        load_expert(7, staged=False)
                if qb == 2:
                    p2_fill.extend(
                        [lambda i4_=i4_: ln_accum(0, i4_) for i4_ in range(4)]
                        + [lambda: ln_math(0, want_xnorm=True)]
                        + [lambda i4_=i4_: ln_norm_tile(0, i4_) for i4_ in range(4)])
                nch = 2 * (qb + 1)

                def emit_otrans(g, tb=None):
                    tb = 6 + (g % 2) if tb is None else tb
                    for ri in range(4):
                        run("tensor",
                            lambda e, ri=ri, g=g, tb=tb: e.matmul(
                                pb[tb][:, ri * 128:(ri + 1) * 128],
                                B(R_SQ + (g % 2) * 512 + ri * 128, 128), B(R_ID, 128),
                                start=True, stop=True),
                            read=(sq, idb), write=(bank[tb],),
                            inc="t" if ri == 3 else None)
                    run("vector",
                        lambda e, g=g, tb=tb: e.tensor_copy(B(R_LNF + g * 512, 512), pb[tb][:, 0:512]),
                        read=(bank[tb],), write=(lnf[g],), inc="v")

                for g in range(4):
                    def s_chunk(c, g=g, qb=qb):
                        sb = c % 2
                        pt = c % 4
                        diag = (c >= 2 * qb)
                        if diag:
                            delta = 128 * c - 256 * qb
                            run("tensor",
                                lambda e, sb=sb, delta=delta: e.matmul(
                                    pb[sb][:, 0:512].rearrange("p (h n) -> p h n", n=256),
                                    B(R_ID, 128),
                                    B(R_MASK + 128 - delta, 256).unsqueeze(1).broadcast_to([128, 2, 256]),
                                    start=True, stop=False),
                                read=(idb, maskb), write=(bank[sb],))
                        run("tensor",
                            lambda e, sb=sb, c=c, g=g, qb=qb, diag=diag: e.matmul(
                                pb[sb][:, 0:512].rearrange("p (h n) -> p h n", n=256),
                                B(R_KF + g * T + c * 128, 128),
                                B(R_QF + 2 * g * T + qb * 256, 2 * T).rearrange(
                                    "p (h t) -> p h t", t=T)[:, :, 0:256],
                                start=not diag, stop=True),
                            read=(kf[c // 4], qf[qb // 2]), write=(bank[sb],), inc="t")
                        run("scalar",
                            lambda e, sb=sb, pt=pt: e.activation(B(R_PT + pt * 512, 512), pb[sb][:, 0:512],
                                                                 AF.Exp, scale=SCALE),
                            read=(bank[sb],), write=(ptb[pt],), inc="s")

                    def pv_chunk(c, g=g, nch=nch):
                        pt = c % 4
                        for hl in range(2):
                            for qsub in range(2):
                                ob = 2 + hl * 2 + qsub
                                run("tensor",
                                    lambda e, pt=pt, hl=hl, qsub=qsub, ob=ob, c=c, g=g, nch=nch:
                                    e.matmul(pb[ob][:, 0:129],
                                             B(R_PT + pt * 512 + hl * 256 + qsub * 128, 128),
                                             B(R_VA + (g * 16 + c) * 129, 129),
                                             start=(c == 0), stop=(c == nch - 1)),
                                    read=(ptb[pt], vaug[c // 4]), write=(bank[ob],),
                                    inc="t" if c == nch - 1 else None)

                    PVD = 3
                    ot_pt = min(3, nch - 1)
                    for c in range(nch):
                        s_chunk(c)
                        if c == ot_pt and pend_otrans[0] is not None:
                            emit_otrans(pend_otrans[0])
                            pend_otrans[0] = None
                        if c >= PVD:
                            pv_chunk(c - PVD)
                    for c in range(max(0, nch - PVD), nch):
                        pv_chunk(c)
                    for hl in range(2):
                        for qsub in range(2):
                            ob = 2 + hl * 2 + qsub
                            ri = hl * 2 + qsub + (g % 2) * 4
                            run("vector",
                                lambda e, ob=ob, ri=ri: e.reciprocal(Fp(F_REC + ri, 1), pb[ob][:, 128:129]),
                                read=(bank[ob],), write=(orec,), inc="v")
                            run("vector",
                                lambda e, ob=ob, ri=ri, g=g, hl=hl, qsub=qsub: e.tensor_scalar(
                                    B(R_SQ + (g % 2) * 512 + (hl * 2 + qsub) * 128, 128),
                                    pb[ob][:, 0:128], Fp(F_REC + ri, 1), None, ALU.mult),
                                read=(bank[ob], orec), write=(sq,), inc="v")
                    pend_otrans[0] = g
                    if p2_fill:
                        p2_fill.pop(0)()
                for qsub in range(2):
                    tile = qb * 2 + qsub
                    for half in range(2):
                        pp = 6 + half
                        for hc in range(8):
                            run("tensor",
                                lambda e, hc=hc, qsub=qsub, half=half, pp=pp: e.matmul(
                                    pb[pp][:, 0:512],
                                    B(R_LNF + hc * 256 + qsub * 128, 128),
                                    B(W_O + hc * 1024 + half * 512, 512),
                                    start=(hc == 0), stop=(hc == 7)),
                                read=(lnf[hc // 2], wob), write=(bank[pp],),
                                inc="t" if hc == 7 else None)
                            if qsub == 0 and half == 0 and hc == 5 and pend_otrans[0] is not None:
                                emit_otrans(pend_otrans[0], tb=0)
                                pend_otrans[0] = None
                        run("vector",
                            lambda e, tile=tile, half=half, pp=pp: e.tensor_tensor(
                                B(R_X + tile * 1024 + half * 512, 512),
                                B(R_X + tile * 1024 + half * 512, 512),
                                pb[pp][:, 0:512], ALU.add),
                            read=(bank[pp], xp[tile // 4]), write=(xp[tile // 4],), inc="v")

            for e_ in (6, 0, 1, 2, 3):
                load_expert(e_, staged=(s > 0))

            for qs in range(NQS):
                ln_trans(qs)
                pr = qs % 2
                S = _st(pr)
                st = statsb[pr]
                for sub in range(4):
                    for k in range(8):
                        run("tensor",
                            lambda e, k=k, sub=sub: e.matmul(
                                pb[0][:, sub * 128:sub * 128 + 16],
                                B(R_LNF + k * 512 + sub * 128, 128),
                                B(R_WCAT + k * 16, 16),
                                start=(k == 0), stop=(k == 7)),
                            read=(lnf[k], wcat), write=(bank[0],),
                            inc="t" if k == 7 else None)
                cosp = pb[0][:, 0:512].rearrange("p (s c) -> p s c", c=128)[:, :, 0:8]
                routep = pb[0][:, 0:512].rearrange("p (s c) -> p s c", c=128)[:, :, 8:16]
                def b3(off, n1=4, n2=8):
                    return Fp(off, n1 * n2).rearrange("p (s c) -> p s c", c=n2)
                def bc(off):
                    return Fp(off, 4).unsqueeze(2).broadcast_to([128, 4, 8])
                run("vector",
                    lambda e, S=S: e.tensor_tensor(b3(GB_T), cosp, bc(S["IXN"]), ALU.mult),
                    read=(bank[0], st), write=(gat,), inc="v")
                run("vector", lambda e: e.tensor_tensor(b3(GB_L), b3(GB_T), routep, ALU.add),
                    read=(bank[0], gat), write=(gat,), inc="v")
                run("vector", lambda e: e.reduce_max(Fp(GB_M1, 4), b3(GB_L), AXX),
                    read=(gat,), write=(gat,), inc="v")
                run("vector", lambda e: e.tensor_scalar(Fp(GB_NM1, 4), Fp(GB_M1, 4), -1.0, None, ALU.mult),
                    read=(gat,), write=(gat,), inc="v")
                run("vector", lambda e: e.tensor_tensor(b3(GB_L), b3(GB_L), bc(GB_NM1), ALU.add),
                    read=(gat,), write=(gat,), inc="v")
                run("vector", lambda e: e.tensor_scalar(b3(GB_T), b3(GB_L), 0.0, None, ALU.is_ge),
                    read=(gat,), write=(gat,), inc="v")
                run("vector", lambda e: e.scalar_tensor_tensor(b3(GB_T), b3(GB_T), -1e30, b3(GB_L),
                                                               ALU.mult, ALU.add),
                    read=(gat,), write=(gat,), inc="v")
                run("vector", lambda e: e.reduce_max(Fp(GB_M2, 4), b3(GB_T), AXX),
                    read=(gat,), write=(gat,), inc="v")
                run("vector", lambda e: e.tensor_tensor(b3(GB_T), b3(GB_L), bc(GB_M2), ALU.is_ge),
                    read=(gat,), write=(gat,), inc="v")
                run("scalar", lambda e: e.activation(b3(GB_E), b3(GB_L), AF.Exp),
                    read=(gat,), write=(gat,), inc="s")
                run("vector", lambda e: e.tensor_tensor(b3(GB_L), b3(GB_E), b3(GB_T), ALU.mult),
                    read=(gat,), write=(gat,), inc="v")
                run("vector", lambda e: e.reduce_sum(Fp(GB_S, 4), b3(GB_L), AXX),
                    read=(gat,), write=(gat,), inc="v")
                run("vector", lambda e: e.reciprocal(Fp(GB_R, 4), Fp(GB_S, 4)),
                    read=(gat,), write=(gat,), inc="v")
                run("vector",
                    lambda e: e.tensor_tensor(
                        B(R_SQ, 32).rearrange("p (s c) -> p s c", c=8),
                        b3(GB_L), bc(GB_R), ALU.mult),
                    read=(gat,), write=(sq,), inc="v")
                def emit_gatesT():
                    for sub in range(4):
                        run("tensor",
                            lambda e, sub=sub: e.matmul(
                                pb[1][0:8, sub * 128:(sub + 1) * 128],
                                B(R_SQ + sub * 8, 8), B(R_ID, 128),
                                start=True, stop=True),
                            read=(sq, idb), write=(bank[1],),
                            inc="t" if sub == 3 else None)
                    run("scalar", lambda e: e.copy(B(R_GFM, 512, 0, 8), pb[1][0:8, 0:512]),
                        read=(bank[1],), write=(gfm,), inc="s")
                gatesT_pending = [True]

                for half_t in range(2):
                    if half_t == 1 and qs < NQS - 1:
                        ln_next = (qs + 1, True)
                    elif half_t == 1 and qs == NQS - 1 and s < NSEQ - 1:
                        ln_next = (0, False)
                    else:
                        ln_next = None
                    def w2_mms(ee, ei, stop=False):
                        base = exp_slot(ee)
                        gho = (ei % 2) * 1024
                        for qsub in range(2):
                            for half in range(2):
                                yb = 2 + qsub * 2 + half
                                for ko in range(4):
                                    run("tensor",
                                        lambda e, qsub=qsub, half=half, ko=ko, yb=yb, base=base,
                                               gho=gho, st=(ei == 0 and ko == 0),
                                               sp=(stop and ko == 3):
                                        e.matmul(pb[yb][:, 0:512],
                                                 B(R_GH + gho + ko * 256 + qsub * 128, 128),
                                                 B(base + 4096 + ko * 1024 + half * 512, 512),
                                                 start=st, stop=sp),
                                        read=(ghb[gho // 1024], ex[ee]), write=(bank[yb],),
                                        inc="t" if ko == 3 else None)

                    for ei, ee in enumerate(EXPERT_ORDER):
                        base = exp_slot(ee)
                        gb = ei % 2
                        gho = (ei % 2) * 1024

                        def h_group(ko, base=base, ee=ee, half_t=half_t):
                            pp = 6 + (ko % 2)
                            for k in range(8):
                                run("tensor",
                                    lambda e, k=k, ko=ko, base=base, half_t=half_t, pp=pp: e.matmul(
                                        pb[pp][:, 0:256],
                                        B(base + k * 512 + ko * 128, 128),
                                        B(R_LNF + k * 512 + half_t * 256, 256),
                                        start=(k == 0), stop=(k == 7)),
                                    read=(lnf[k], ex[ee]), write=(bank[pp],),
                                    inc="t" if k == 7 else None)

                        def gelu_gh(ko, ee=ee, gb=gb, gho=gho):
                            pp = 6 + (ko % 2)
                            run("scalar",
                                lambda e, ko=ko, ee=ee, pp=pp: e.activation(
                                    B(R_HS + (ko % 2) * 256, 256), pb[pp][:, 0:256],
                                    AF.Gelu, bias=Fp(F_B1 + ee * 4 + ko, 1)),
                                read=(bank[pp], bb1), write=(hsb[ko % 2],), inc="s")
                            run("vector",
                                lambda e, ko=ko, gb=gb, gho=gho: e.tensor_tensor(
                                    B(R_GH + gho + ko * 256, 256), B(R_HS + (ko % 2) * 256, 256),
                                    pb[gb][:, 0:256], ALU.mult),
                                read=(hsb[ko % 2], bank[gb]), write=(ghb[gho // 1024],), inc="v")

                        def emit_gb(ee=ee, half_t=half_t, gb=gb):
                            run("tensor",
                                lambda e, ee=ee, half_t=half_t, gb=gb: e.matmul(
                                    pb[gb][:, 0:256],
                                    B(R_SEL + ee * 128, 128, 0, 8),
                                    B(R_GFM + half_t * 256, 256, 0, 8),
                                    start=True, stop=True),
                                read=(gfm, selb), write=(bank[gb],), inc="t")

                        if gatesT_pending[0]:
                            h_group(0)
                            emit_gatesT()
                            gatesT_pending[0] = False
                            emit_gb()
                            gelu_gh(0)
                            for ko in range(1, 4):
                                h_group(ko)
                                gelu_gh(ko)
                        else:
                            emit_gb()
                            for ko in range(4):
                                h_group(ko)
                                gelu_gh(ko)
                        if ei >= 1:
                            w2_mms(EXPERT_ORDER[ei - 1], ei - 1)
                        if ln_next is not None and 2 <= ei <= 5:
                            if ei <= 3:
                                ln_accum(ln_next[0], 2 * (ei - 2))
                                ln_accum(ln_next[0], 2 * (ei - 2) + 1)
                            elif ei == 4:
                                ln_math(ln_next[0], want_xnorm=ln_next[1])
                            else:
                                for i4_ in range(4):
                                    ln_norm_tile(ln_next[0], i4_)
                    for qsub in range(2):
                        for half in range(2):
                            yb = 2 + qsub * 2 + half
                            run("tensor",
                                lambda e, qsub=qsub, half=half, yb=yb, half_t=half_t: e.matmul(
                                    pb[yb][:, 0:512],
                                    B(R_GFM + (half_t * 2 + qsub) * 128, 128, 0, 8),
                                    B(R_B2 + half * 512, 512, 0, 8),
                                    start=False, stop=False),
                                read=(gfm, bb2), write=(bank[yb],))
                    w2_mms(EXPERT_ORDER[7], 7, stop=True)
                    if qs == NQS - 1 and half_t == 1 and s < NSEQ - 1:
                        load_qkv_weights(staged=True)
                        load_wo(staged=True)
                    for qsub in range(2):
                        for half in range(2):
                            yb = 2 + qsub * 2 + half
                            tile = qs * 4 + half_t * 2 + qsub
                            run("vector",
                                lambda e, tile=tile, half=half, yb=yb: e.tensor_tensor(
                                    B(R_X + tile * 1024 + half * 512, 512),
                                    B(R_X + tile * 1024 + half * 512, 512),
                                    pb[yb][:, 0:512], ALU.add),
                                read=(bank[yb], xp[tile // 4]), write=(xp[tile // 4],), inc="v")
                dma("gpsimd",
                    out[s, qs * 512:(qs + 1) * 512, :].rearrange("(i p) d -> p i d", p=128),
                    B(R_X + qs * 4096, 4096).rearrange("p (i d) -> p i d", d=1024),
                    f"do{qs}", rbufs=(xp[qs],))
                if s < NSEQ - 1:
                    load_x(s + 1, qs)



        P.emit("gpsimd", lambda e: None,
               waits=[(f"do{i}", 16 * NSEQ) for i in range(4)])

        with nc.Block() as block:
            @block.tensor
            def _(eng):
                for op in P.ops["tensor"]:
                    op(eng, sems)

            @block.vector
            def _(eng):
                for op in P.ops["vector"]:
                    op(eng, sems)

            @block.scalar
            def _(eng):
                for op in P.ops["scalar"]:
                    op(eng, sems)

            @block.gpsimd
            def _(eng):
                for op in P.ops["gpsimd"]:
                    op(eng, sems)

            @block.sync
            def _(eng):
                for op in P.ops["sync"]:
                    op(eng, sems)

    return nc


_NC_CACHE = [None]


def kernel(L0_attn_hidden_states, L0_attn_Wq, L0_attn_Wk, L0_attn_Wv, L0_attn_Wo,
           L0_ffn_w1, L0_ffn_b1, L0_ffn_w2, L0_ffn_b2, L0_ffn_centroids,
           L0_ffn_w_route):
    x = np.ascontiguousarray(np.asarray(L0_attn_hidden_states, dtype=np.float32))
    consts = _host_consts()
    base = dict(wq=np.ascontiguousarray(L0_attn_Wq, dtype=np.float32),
                wk=np.ascontiguousarray(L0_attn_Wk, dtype=np.float32),
                wv=np.ascontiguousarray(L0_attn_Wv, dtype=np.float32),
                wo=np.ascontiguousarray(L0_attn_Wo, dtype=np.float32),
                w1=np.ascontiguousarray(L0_ffn_w1, dtype=np.float32),
                b1=np.ascontiguousarray(np.asarray(L0_ffn_b1, dtype=np.float32)
                                        .reshape(8, 4, 128).transpose(2, 0, 1)
                                        .reshape(128, 32)),
                w2=np.ascontiguousarray(L0_ffn_w2, dtype=np.float32),
                b2=np.ascontiguousarray(L0_ffn_b2, dtype=np.float32),
                cent=np.ascontiguousarray(L0_ffn_centroids, dtype=np.float32),
                wrt=np.ascontiguousarray(L0_ffn_w_route, dtype=np.float32),
                **consts)
    if _NC_CACHE[0] is None:
        _NC_CACHE[0] = build_nc()
    nc = _NC_CACHE[0]
    in_maps = []
    for c in range(NCORE):
        m = dict(base)
        m["x"] = np.ascontiguousarray(x[c * NSEQ:(c + 1) * NSEQ])
        in_maps.append(m)
    res = run_bass_kernel_spmd(nc, in_maps, core_ids=list(range(NCORE)))
    return np.concatenate([r["out"] for r in res.results], axis=0).reshape(32, T, DM)


# version 31
# speedup vs baseline: 1.0702x; 1.0031x over previous
import numpy as np
import ml_dtypes

import concourse.bass as bass
import concourse.mybir as mybir
from concourse.bass_utils import run_bass_kernel_spmd

BF16 = mybir.dt.bfloat16
F32 = mybir.dt.float32
AF = mybir.ActivationFunctionType
ALU = mybir.AluOpType
AXX = mybir.AxisListType.X

NCORE = 8
NSEQ = 4
T = 2048
DM = 1024
NH, NKV, HD = 8, 4, 128
DEXP, NEXP = 512, 8
SCALE = float(np.float32(1.0) / np.sqrt(np.float32(128.0)))
MASKVAL = -1.0e6

NQS = T // 512
NQB = T // 256

_off = [0]
def _alloc(n):
    o = _off[0]; _off[0] += n; return o

R_X    = _alloc(16 * 1024)
R_MASK = _alloc(384)
R_ID   = _alloc(128)
R_PROT = _alloc(128)
R_SEL  = _alloc(1024)
R_WCAT = _alloc(128)
R_B2   = _alloc(1024)
R_LNT  = _alloc(4096)
R_SQ   = _alloc(1024)
R_LNF  = _alloc(4096)
R_HS   = _alloc(512)
R_GH   = _alloc(2048)
R_GFM  = _alloc(512)
R_PT   = _alloc(2048)
R_QF   = _alloc(16384)
R_KF   = _alloc(8192)
R_VA   = _alloc(4 * 16 * 129)
R_E    = _alloc(32768)
NB = _off[0]

EXB = R_QF + 64
def exp_slot(e):
    return EXB + 8192 * e
W_QKV = exp_slot(4)
W_O = exp_slot(6)
assert exp_slot(4) == R_E
assert exp_slot(8) == NB

_f = [0]
def _falloc(n):
    o = _f[0]; _f[0] += n; return o
F_CENT = _falloc(1024)
F_ST   = _falloc(128)
F_REC  = _falloc(8)
F_GAT  = _falloc(128)
F_B1   = _falloc(32)
F_COS  = _falloc(512)
F_SIN  = _falloc(512)
NF = _f[0]
assert NB * 2 + NF * 4 <= 212000, (NB, NF)

def _st(parity):
    b = F_ST + 64 * parity
    return dict(SUM=b, SQ=b + 4, M=b + 8, MSQ=b + 12, V=b + 16, RS=b + 20,
                NMR=b + 24, SV=b + 28, XN=b + 32, IXN=b + 36, T0=b + 40, T1=b + 44)
GB_L, GB_T, GB_M1, GB_NM1, GB_M2, GB_S, GB_R, GB_E = (
    F_GAT + 0, F_GAT + 32, F_GAT + 64, F_GAT + 68, F_GAT + 72, F_GAT + 76, F_GAT + 80,
    F_GAT + 88)


def _host_consts():
    bf = ml_dtypes.bfloat16
    try:
        import jax
        import jax.numpy as jnp
        with jax.default_device(jax.devices("cpu")[0]):
            inv_freq = 1.0 / 10000.0 ** (jnp.arange(0, HD, 2, dtype=jnp.float32) / HD)
            freqs = jnp.outer(jnp.arange(T, dtype=jnp.float32), inv_freq)
            emb = jnp.concatenate((freqs, freqs), axis=-1)
            cos = np.asarray(jnp.cos(emb), dtype=np.float32)
            sin = np.asarray(jnp.sin(emb), dtype=np.float32)
    except Exception:
        inv_freq = (1.0 / (10000.0 ** (np.arange(0, HD, 2, dtype=np.float32) / np.float32(HD)))).astype(np.float32)
        freqs = np.outer(np.arange(T, dtype=np.float32), inv_freq).astype(np.float32)
        emb = np.concatenate([freqs, freqs], axis=1)
        cos = np.cos(emb).astype(np.float32)
        sin = np.sin(emb).astype(np.float32)
    sin_signed = sin.copy()
    sin_signed[:, :64] *= -1.0
    c_cos = np.ascontiguousarray(cos.T).astype(np.float32)
    c_sin = np.ascontiguousarray(sin_signed.T).astype(np.float32)
    p = np.arange(128)[:, None]
    v = np.arange(128, 512)[None, :]
    c_mask = np.where(p <= v - 256, 0.0, MASKVAL).astype(bf)
    c_id = np.eye(128, dtype=np.float32).astype(bf)
    prot = np.zeros((128, 128), dtype=np.float32)
    prot[np.arange(128), (np.arange(128) + 64) % 128] = 1.0
    c_prot = prot.astype(bf)
    sel = np.zeros((8, 8, 128), dtype=np.float32)
    for e in range(8):
        sel[e, e, :] = 1.0
    c_sel = sel.transpose(1, 0, 2).reshape(8, 1024).astype(bf)
    return dict(c_cos=c_cos, c_sin=c_sin, c_mask=c_mask, c_id=c_id,
                c_prot=c_prot, c_sel=np.ascontiguousarray(c_sel))


class Prog:
    ENGS = ("tensor", "vector", "scalar", "gpsimd", "sync")

    def __init__(self):
        self.ops = {e: [] for e in self.ENGS}
        self.cnt = {}
        self.known = {e: {} for e in self.ENGS}

    def emit(self, eng, fn, waits=(), inc=None, inc_by=1):
        w = []
        for s, v in waits:
            if v <= 0 or self.known[eng].get(s, 0) >= v:
                continue
            self.known[eng][s] = v
            w.append((s, v))
        val = None
        if inc is not None:
            self.cnt[inc] = self.cnt.get(inc, 0) + inc_by
            val = self.cnt[inc]

        def op(engine, sems, w=w, fn=fn, inc=inc, inc_by=inc_by):
            for s, v in w:
                engine.wait_ge(sems[s], v)
            ins = fn(engine)
            if inc is not None and ins is not None:
                ins.then_inc(sems[inc], inc_by)
        self.ops[eng].append(op)
        return (inc, val) if inc is not None else None

    def c(self, sem):
        return self.cnt.get(sem, 0)


class Buf:
    def __init__(self):
        self.w = {}
        self.r = {}

    @staticmethod
    def _m(d, ev):
        if ev is None:
            return
        s, v = ev
        if d.get(s, 0) < v:
            d[s] = v

    def wrote(self, ev):
        self._m(self.w, ev)

    def read(self, ev):
        self._m(self.r, ev)

    @property
    def need_for_read(self):
        return list(self.w.items())

    @property
    def need_for_write(self):
        out = dict(self.w)
        for s, v in self.r.items():
            if out.get(s, 0) < v:
                out[s] = v
        return list(out.items())


def build_nc():
    nc = bass.Bass("TRN2", target_bir_lowering=False)
    xin = nc.dram_tensor("x", [NSEQ, T, DM], F32, kind="ExternalInput").ap()
    wq = nc.dram_tensor("wq", [DM, NH * HD], F32, kind="ExternalInput").ap()
    wk = nc.dram_tensor("wk", [DM, NKV * HD], F32, kind="ExternalInput").ap()
    wv = nc.dram_tensor("wv", [DM, NKV * HD], F32, kind="ExternalInput").ap()
    wo = nc.dram_tensor("wo", [NH * HD, DM], F32, kind="ExternalInput").ap()
    w1 = nc.dram_tensor("w1", [NEXP, DM, DEXP], F32, kind="ExternalInput").ap()
    b1d = nc.dram_tensor("b1", [128, 32], F32, kind="ExternalInput").ap()
    w2 = nc.dram_tensor("w2", [NEXP, DEXP, DM], F32, kind="ExternalInput").ap()
    b2d = nc.dram_tensor("b2", [NEXP, DM], F32, kind="ExternalInput").ap()
    cent = nc.dram_tensor("cent", [NEXP, DM], F32, kind="ExternalInput").ap()
    wrtd = nc.dram_tensor("wrt", [NEXP, DM], F32, kind="ExternalInput").ap()
    cc = {}
    for n, shp in (("c_mask", [128, 384]), ("c_id", [128, 128]),
                   ("c_prot", [128, 128]), ("c_sel", [8, 1024])):
        cc[n] = nc.dram_tensor(n, shp, BF16, kind="ExternalInput").ap()
    for n in ("c_cos", "c_sin"):
        cc[n] = nc.dram_tensor(n, [128, T], F32, kind="ExternalInput").ap()
    out = nc.dram_tensor("out", [NSEQ, T, DM], F32, kind="ExternalOutput").ap()
    wstage = nc.dram_tensor("wstage", [128, 24576 + NEXP * 8192], BF16, kind="Internal").ap()

    import contextlib
    with contextlib.ExitStack() as ctx:
        AB = ctx.enter_context(nc.sbuf_tensor("AB", [128, NB], BF16))
        AFP = ctx.enter_context(nc.sbuf_tensor("AFp", [128, NF], F32))
        pb = [ctx.enter_context(nc.psum_tensor(f"pb{i}", [128, 512], F32))
              for i in range(8)]
        semnames = (("t", "v", "s", "g", "dc", "dcs", "dwi", "dcen", "dwq", "dwo", "hwq", "hwo")
                    + tuple(f"dx{i}" for i in range(4))
                    + tuple(f"do{i}" for i in range(4))
                    + tuple(f"dwe{e}" for e in range(NEXP))
                    + tuple(f"hwe{e}" for e in range(NEXP)))
        sems = {n: ctx.enter_context(nc.semaphore(f"sem_{n}")) for n in semnames}

        P = Prog()
        ab = AB.ap()
        af = AFP.ap()

        def B(off, n, p0=0, p1=128):
            return ab[p0:p1, off:off + n]

        def Fp(off, n, p0=0, p1=128):
            return af[p0:p1, off:off + n]

        bank = [Buf() for _ in range(8)]
        bufs = {}
        def buf(name):
            if name not in bufs:
                bufs[name] = Buf()
            return bufs[name]

        def dma(queue, outap, inap, sem, rbufs=(), wbufs=()):
            waits = {}
            for b in rbufs:
                for s, v in b.need_for_read:
                    waits[s] = max(waits.get(s, 0), v)
            for b in wbufs:
                for s, v in b.need_for_write:
                    waits[s] = max(waits.get(s, 0), v)
            ev = P.emit(queue, lambda e, o=outap, i=inap: e.dma_start(out=o, in_=i),
                        waits=list(waits.items()), inc=sem, inc_by=16)
            for b in rbufs:
                b.read(ev)
            for b in wbufs:
                b.wrote(ev)
            return ev

        def run(eng, fn, read=(), write=(), extra=(), inc=None):
            waits = {}
            for b in read:
                for s, v in b.need_for_read:
                    waits[s] = max(waits.get(s, 0), v)
            for b in write:
                for s, v in b.need_for_write:
                    waits[s] = max(waits.get(s, 0), v)
            for s, v in extra:
                waits[s] = max(waits.get(s, 0), v)
            ev = P.emit(eng, fn, waits=list(waits.items()), inc=inc)
            for b in read:
                b.read(ev if ev is not None else (eng_sem(eng), P.c(eng_sem(eng))))
            for b in write:
                b.wrote(ev)
            return ev

        def eng_sem(eng):
            return {"tensor": "t", "vector": "v", "scalar": "s", "gpsimd": "g"}.get(eng, "t")

        cosb, sinb, maskb, idb, protb, selb = (buf(x) for x in
            ("cos", "sin", "mask", "id", "prot", "sel"))
        wqkv, wob, wcat, bb1, bb2 = (buf(x) for x in ("wqkv", "wob", "wcat", "b1", "b2"))
        ex = [buf(f"ex{e}") for e in range(NEXP)]
        xp = [buf(f"xp{i}") for i in range(4)]
        qf = [buf(f"qf{j}") for j in range(4)]
        kf = [buf(f"kf{j}") for j in range(4)]
        vaug = [buf(f"va{j}") for j in range(4)]
        lnt = [buf("lnt_a"), buf("lnt_b"), buf("lnt_c")]
        sq = buf("sq")
        lnf = [buf(f"lnf{k}") for k in range(8)]
        ptb = [buf(f"pt{i}") for i in range(4)]
        hsb = [buf(f"hs{i}") for i in range(2)]
        ghb, gfm = [buf("gh0"), buf("gh1")], buf("gfm")
        stats, gat, orec, cstage = buf("stats"), buf("gat"), buf("orec"), buf("cstage")

        dma("sync", B(R_MASK, 384), cc["c_mask"], "dc", wbufs=(maskb,))
        dma("sync", B(R_ID, 128), cc["c_id"], "dc", wbufs=(idb,))
        dma("sync", B(R_PROT, 128), cc["c_prot"], "dc", wbufs=(protb,))
        dma("sync", B(R_SEL, 1024, 0, 8), cc["c_sel"], "dc", wbufs=(selb,))
        for b_ in (cosb, sinb, maskb, idb, protb, selb):
            b_.wrote(("dc", P.c("dc")))

        for part_ in range(1):
            dma("gpsimd",
                B(R_X + part_ * 4096, 4096).rearrange("p (i d) -> p i d", d=1024),
                xin[0, part_ * 512:(part_ + 1) * 512, :].rearrange("(i p) d -> p i d", p=128),
                f"dx{part_}", wbufs=(xp[part_],))
        dma("gpsimd", Fp(F_B1, 32), b1d, "dwi", wbufs=(bb1,))
        dma("gpsimd", B(R_B2, 1024, 0, 8), b2d, "dwi", wbufs=(bb2,))
        bb1.wrote(("dwi", P.c("dwi")))
        bb2.wrote(("dwi", P.c("dwi")))

        ones_ap = ab[:, R_VA + 128: R_VA + 64 * 129: 129]

        for idx, (src, colo) in enumerate(((cent, 0), (wrtd, 8))):
            dma("gpsimd", Fp(F_CENT, 1024, 0, 8), src, "dcen", wbufs=(cstage,))
            if idx == 0:
                run("scalar",
                    lambda e: e.activation(B(R_LNT, 1024, 0, 8), Fp(F_CENT, 1024, 0, 8),
                                           AF.Square, accum_out=Fp(F_GAT, 1, 0, 8)),
                    read=(cstage,), write=(lnt[0], gat), inc="s")
                run("scalar", lambda e: e.activation(Fp(F_GAT + 1, 1, 0, 8), Fp(F_GAT, 1, 0, 8), AF.Ln),
                    read=(gat,), write=(gat,), inc="s")
                run("scalar", lambda e: e.activation(Fp(F_GAT + 2, 1, 0, 8), Fp(F_GAT + 1, 1, 0, 8),
                                                     AF.Exp, scale=0.5),
                    read=(gat,), write=(gat,), inc="s")
                run("vector", lambda e: e.tensor_scalar_max(Fp(F_GAT + 3, 1, 0, 8), Fp(F_GAT + 2, 1, 0, 8), 1e-12),
                    read=(gat,), write=(gat,), inc="v")
                run("vector", lambda e: e.reciprocal(Fp(F_GAT + 4, 1, 0, 8), Fp(F_GAT + 3, 1, 0, 8)),
                    read=(gat,), write=(gat,), inc="v")
                run("vector",
                    lambda e: e.tensor_scalar(B(R_LNT, 1024, 0, 8), Fp(F_CENT, 1024, 0, 8),
                                              Fp(F_GAT + 4, 1, 0, 8), None, ALU.mult),
                    read=(cstage, gat), write=(lnt[0],), inc="v")
            else:
                run("vector", lambda e: e.tensor_copy(B(R_LNT, 1024, 0, 8), Fp(F_CENT, 1024, 0, 8)),
                    read=(cstage,), write=(lnt[0],), inc="v")
            for k in range(8):
                run("tensor",
                    lambda e, k=k: e.matmul(pb[0][:, 0:8], B(R_LNT + k * 128, 128, 0, 8),
                                            B(R_ID, 8, 0, 8), start=True, stop=True),
                    read=(lnt[0], idb), write=(bank[0],), inc="t")
                run("scalar",
                    lambda e, k=k, c=colo: e.copy(B(R_WCAT + k * 16 + c, 8), pb[0][:, 0:8]),
                    read=(bank[0],), write=(wcat,), inc="s")

        stg_qkv, stg_wo = buf("stg_qkv"), buf("stg_wo")
        stg_ex = [buf(f"stg_ex{e}") for e in range(NEXP)]

        def load_qkv_weights(staged):
            if staged:
                dma("sync", B(W_QKV, 16384), wstage[:, 0:16384], "hwq",
                    rbufs=(stg_qkv,), wbufs=(wqkv, ex[4], ex[5]))
            else:
                dma("gpsimd", B(W_QKV, 8192).rearrange("p (k n) -> p k n", n=1024),
                    wq.rearrange("(k p) n -> p k n", p=128), "dwq", wbufs=(wqkv, ex[4]))
                dma("gpsimd", B(W_QKV + 8192, 4096).rearrange("p (k n) -> p k n", n=512),
                    wk.rearrange("(k p) n -> p k n", p=128), "dwq", wbufs=(wqkv, ex[4], ex[5]))
                dma("gpsimd", B(W_QKV + 12288, 4096).rearrange("p (k n) -> p k n", n=512),
                    wv.rearrange("(k p) n -> p k n", p=128), "dwq", wbufs=(wqkv, ex[5]))
            wqkv.wrote(("dwq", P.c("dwq")))

        def stage_qkv_wo():
            dma("sync", wstage[:, 0:16384], B(W_QKV, 16384), "hwq", rbufs=(wqkv,),
                wbufs=(stg_qkv,))
            dma("sync", wstage[:, 16384:24576], B(W_O, 8192), "hwo", rbufs=(wob,),
                wbufs=(stg_wo,))

        def load_wo(staged):
            if staged:
                dma("sync", B(W_O, 8192), wstage[:, 16384:24576], "hwo",
                    rbufs=(stg_wo,), wbufs=(wob, ex[6]))
            else:
                dma("gpsimd", B(W_O, 8192).rearrange("p (k n) -> p k n", n=1024),
                    wo.rearrange("(k p) n -> p k n", p=128), "dwo", wbufs=(wob, ex[6]))

        def load_expert(e, staged):
            base = exp_slot(e)
            confl = [ex[e]]
            if e in (4, 5):
                confl.append(wqkv)
            elif e == 6:
                confl.append(wob)
            elif e <= 3:
                confl += qf + kf + vaug + [ptb[0], ptb[1]]
            if staged:
                dma("sync", B(base, 8192), wstage[:, 24576 + e * 8192: 24576 + (e + 1) * 8192],
                    f"hwe{e}", rbufs=(stg_ex[e],), wbufs=tuple(confl))
            else:
                dma("gpsimd", B(base, 4096).rearrange("p (k n) -> p k n", n=512),
                    w1[e].rearrange("(k p) n -> p k n", p=128), f"dwe{e}", wbufs=tuple(confl))
                dma("gpsimd", B(base + 4096, 4096).rearrange("p (k n) -> p k n", n=1024),
                    w2[e].rearrange("(k p) n -> p k n", p=128), f"dwe{e}", wbufs=tuple(confl))
                dma("sync", wstage[:, 24576 + e * 8192: 24576 + (e + 1) * 8192], B(base, 8192),
                    f"hwe{e}", rbufs=tuple([ex[e]] + confl[1:]), wbufs=(stg_ex[e],))
            ex[e].wrote((f"dwe{e}", P.c(f"dwe{e}")))

        load_qkv_weights(staged=False)
        for part_ in range(1, 4):
            dma("gpsimd",
                B(R_X + part_ * 4096, 4096).rearrange("p (i d) -> p i d", d=1024),
                xin[0, part_ * 512:(part_ + 1) * 512, :].rearrange("(i p) d -> p i d", p=128),
                f"dx{part_}", wbufs=(xp[part_],))
        load_wo(staged=False)
        stage_qkv_wo()

        EXPERT_ORDER = [4, 5, 7, 6, 0, 1, 2, 3]

        statsb = [buf("stats0"), buf("stats1")]
        stsum = [buf("stsum0"), buf("stsum1")]
        stsq = [buf("stsq0"), buf("stsq1")]

        def ln_accum(qs, i4):
            pr = qs % 2
            S = _st(pr)
            tile = qs * 4 + i4
            xap = B(R_X + tile * 1024, 1024)
            run("vector", lambda e, i4=i4, xap=xap, S=S: e.reduce_sum(Fp(S["SUM"] + i4, 1), xap, AXX),
                read=(xp[tile // 4],), write=(stsum[pr],), inc="v")
            run("scalar",
                lambda e, i4=i4, xap=xap, S=S: e.activation(B(R_LNT + 3072, 1024), xap, AF.Square,
                                                           accum_out=Fp(S["SQ"] + i4, 1)),
                read=(xp[tile // 4],), write=(lnt[2], stsq[pr]), inc="s")

        def ln_math(qs, want_xnorm, lo=0, hi=4):
            pr = qs % 2
            S = _st(pr)
            st = statsb[pr]
            n_ = hi - lo
            run("vector", lambda e, S=S, lo=lo, n_=n_: e.tensor_scalar(Fp(S["M"] + lo, n_), Fp(S["SUM"] + lo, n_), 1.0 / 1024.0, None, ALU.mult),
                read=(stsum[pr],), write=(st,), inc="v")
            run("vector", lambda e, S=S, lo=lo, n_=n_: e.tensor_scalar(Fp(S["MSQ"] + lo, n_), Fp(S["SQ"] + lo, n_), 1.0 / 1024.0, None, ALU.mult),
                read=(stsq[pr],), write=(st,), inc="v")
            run("vector", lambda e, S=S, lo=lo, n_=n_: e.tensor_tensor(Fp(S["T0"] + lo, n_), Fp(S["M"] + lo, n_), Fp(S["M"] + lo, n_), ALU.mult),
                read=(st,), write=(st,), inc="v")
            run("vector", lambda e, S=S, lo=lo, n_=n_: e.tensor_tensor(Fp(S["V"] + lo, n_), Fp(S["MSQ"] + lo, n_), Fp(S["T0"] + lo, n_), ALU.subtract),
                read=(st,), write=(st,), inc="v")
            run("vector", lambda e, S=S, lo=lo, n_=n_: e.tensor_scalar(Fp(S["T1"] + lo, n_), Fp(S["V"] + lo, n_), 1e-5, None, ALU.add),
                read=(st,), write=(st,), inc="v")
            run("scalar", lambda e, S=S, lo=lo, n_=n_: e.activation(Fp(S["T0"] + lo, n_), Fp(S["T1"] + lo, n_), AF.Ln),
                read=(st,), write=(st,), inc="s")
            run("scalar", lambda e, S=S, lo=lo, n_=n_: e.activation(Fp(S["RS"] + lo, n_), Fp(S["T0"] + lo, n_), AF.Exp, scale=-0.5),
                read=(st,), write=(st,), inc="s")
            run("vector", lambda e, S=S, lo=lo, n_=n_: e.scalar_tensor_tensor(Fp(S["NMR"] + lo, n_), Fp(S["M"] + lo, n_), -1.0, Fp(S["RS"] + lo, n_),
                                                                ALU.mult, ALU.mult),
                read=(st,), write=(st,), inc="v")
            if want_xnorm:
                run("scalar", lambda e, S=S, lo=lo, n_=n_: e.activation(Fp(S["T1"] + lo, n_), Fp(S["V"] + lo, n_), AF.Ln),
                    read=(st,), write=(st,), inc="s")
                run("scalar", lambda e, S=S, lo=lo, n_=n_: e.activation(Fp(S["SV"] + lo, n_), Fp(S["T1"] + lo, n_), AF.Exp, scale=0.5),
                    read=(st,), write=(st,), inc="s")
                run("vector", lambda e, S=S, lo=lo, n_=n_: e.tensor_tensor(Fp(S["XN"] + lo, n_), Fp(S["SV"] + lo, n_), Fp(S["RS"] + lo, n_), ALU.mult),
                    read=(st,), write=(st,), inc="v")
                run("vector", lambda e, S=S, lo=lo, n_=n_: e.tensor_scalar(Fp(S["XN"] + lo, n_), Fp(S["XN"] + lo, n_), 32.0, 1e-12, ALU.mult, ALU.max),
                    read=(st,), write=(st,), inc="v")
                run("vector", lambda e, S=S, lo=lo, n_=n_: e.reciprocal(Fp(S["IXN"] + lo, n_), Fp(S["XN"] + lo, n_)),
                    read=(st,), write=(st,), inc="v")

        def ln_stats(qs, want_xnorm):
            for i4 in range(4):
                ln_accum(qs, i4)
            ln_math(qs, want_xnorm)

        def ln_norm(qs):
            pr = qs % 2
            S = _st(pr)
            st = statsb[pr]
            for i4 in range(4):
                ln_norm_tile(qs, i4)

        def ln_norm_tile(qs, i4):
            pr = qs % 2
            S = _st(pr)
            st = statsb[pr]
            tile = qs * 4 + i4
            lb = lnt[0] if i4 < 2 else (lnt[1] if i4 == 2 else lnt[2])
            run("vector",
                lambda e, i4=i4, tile=tile, S=S: e.tensor_scalar(
                    B(R_LNT + i4 * 1024, 1024), B(R_X + tile * 1024, 1024),
                    Fp(S["RS"] + i4, 1), Fp(S["NMR"] + i4, 1), ALU.mult, ALU.add),
                read=(xp[tile // 4], st), write=(lb,), inc="v")

        def ln_trans(qs):
            for k in range(8):
                tb = k % 2
                for i4 in range(4):
                    lb = lnt[0] if i4 < 2 else (lnt[1] if i4 == 2 else lnt[2])
                    run("tensor",
                        lambda e, k=k, i4=i4, tb=tb: e.matmul(
                            pb[tb][:, i4 * 128:(i4 + 1) * 128],
                            B(R_LNT + i4 * 1024 + k * 128, 128), B(R_ID, 128),
                            start=True, stop=True),
                        read=(lb, idb), write=(bank[tb],),
                        inc="t" if i4 == 3 else None)
                run("scalar", lambda e, k=k, tb=tb: e.copy(B(R_LNF + k * 512, 512), pb[tb][:, 0:512]),
                    read=(bank[tb],), write=(lnf[k],), inc="s")

        def load_x(s, part):
            dma("gpsimd",
                B(R_X + part * 4096, 4096).rearrange("p (i d) -> p i d", d=1024),
                xin[s, part * 512:(part + 1) * 512, :].rearrange("(i p) d -> p i d", p=128),
                f"dx{part}", wbufs=(xp[part],))

        for s in range(NSEQ):
            run("vector", lambda e: e.memset(ones_ap, 1.0), write=tuple(vaug), inc="v")

            for qs in range(NQS):
                if qs == 0 and s == 0:
                    ln_stats(0, want_xnorm=False)
                    ln_norm(0)
                dma("sync", Fp(F_COS, 512), cc["c_cos"][:, qs * 512:(qs + 1) * 512],
                    "dcs", wbufs=(cosb,))
                dma("sync", Fp(F_SIN, 512), cc["c_sin"][:, qs * 512:(qs + 1) * 512],
                    "dcs", wbufs=(sinb,))
                cosb.wrote(("dcs", P.c("dcs")))
                sinb.wrote(("dcs", P.c("dcs")))
                ln_trans(qs)
                cosap = Fp(F_COS, 512)
                sinap = Fp(F_SIN, 512)

                def qkv_epilogue(m, qs=qs):
                    pp = 6 + (m % 2)
                    rb = 2 + (m % 2)
                    sl = (m % 2) * 512
                    if m < 8:
                        dstoff, dstbuf = R_QF + m * T + qs * 512, qf[qs]
                    else:
                        dstoff, dstbuf = R_KF + (m - 8) * T + qs * 512, kf[qs]
                    run("scalar", lambda e, pp=pp, sl=sl: e.copy(B(R_SQ + sl, 512), pb[pp][:, 0:512]),
                        read=(bank[pp],), write=(sq,), inc="s")
                    run("tensor",
                        lambda e, rb=rb, sl=sl: e.matmul(pb[rb][:, 0:512], B(R_PROT, 128),
                                                         B(R_SQ + sl, 512), start=True, stop=True),
                        read=(sq, protb), write=(bank[rb],), inc="t")
                    run("vector",
                        lambda e, cosap=cosap, sl=sl: e.tensor_tensor(
                            B(R_GH + 1024 + sl, 512), B(R_SQ + sl, 512), cosap, ALU.mult),
                        read=(sq, cosb), write=(ghb[1],), inc="v")
                    run("vector",
                        lambda e, sinap=sinap, rb=rb, sl=sl: e.tensor_tensor(
                            B(R_GH + sl, 512), pb[rb][:, 0:512], sinap, ALU.mult),
                        read=(bank[rb], sinb), write=(ghb[0],), inc="v")
                    run("vector",
                        lambda e, dstoff=dstoff, sl=sl: e.tensor_tensor(
                            B(dstoff, 512), B(R_GH + 1024 + sl, 512),
                            B(R_GH + sl, 512), ALU.add),
                        read=(ghb[0], ghb[1]), write=(dstbuf,), inc="v")

                for m in range(12):
                    pp = 6 + (m % 2)
                    if m < 8:
                        wbase, wstride, mm = W_QKV, 1024, m
                    else:
                        wbase, wstride, mm = W_QKV + 8192, 512, m - 8
                    for k in range(8):
                        run("tensor",
                            lambda e, k=k, mm=mm, pp=pp, wbase=wbase, wstride=wstride:
                            e.matmul(pb[pp][:, 0:512],
                                     B(wbase + k * wstride + mm * 128, 128),
                                     B(R_LNF + k * 512, 512),
                                     start=(k == 0), stop=(k == 7)),
                            read=(lnf[k], wqkv), write=(bank[pp],),
                            inc="t" if k == 7 else None)
                    if m >= 1:
                        qkv_epilogue(m - 1)
                qkv_epilogue(11)

                def v_evac(i4, qs=qs):
                    pp = 6 + (i4 % 2)
                    j = qs * 4 + i4
                    for g in range(4):
                        run("vector",
                            lambda e, g=g, j=j, pp=pp: e.tensor_copy(
                                B(R_VA + (g * 16 + j) * 129, 128), pb[pp][:, g * 128:(g + 1) * 128]),
                            read=(bank[pp],), write=(vaug[qs],), inc="v")

                for i4 in range(4):
                    pp = 6 + (i4 % 2)
                    for k in range(8):
                        run("tensor",
                            lambda e, k=k, i4=i4, pp=pp: e.matmul(
                                pb[pp][:, 0:512],
                                B(R_LNF + k * 512 + i4 * 128, 128),
                                B(W_QKV + 12288 + k * 512, 512),
                                start=(k == 0), stop=(k == 7)),
                            read=(lnf[k], wqkv), write=(bank[pp],),
                            inc="t" if k == 7 else None)
                    if i4 >= 1:
                        v_evac(i4 - 1)
                        if qs < NQS - 1:
                            ln_accum(qs + 1, i4 - 1)
                    if i4 == 3 and qs < NQS - 1:
                        ln_accum(qs + 1, 3)
                        ln_math(qs + 1, want_xnorm=False)
                        for i4_ in range(4):
                            ln_norm_tile(qs + 1, i4_)
                v_evac(3)

            pend_otrans = [None]
            p2_fill = []
            for qb in range(NQB):
                if qb == 1:
                    load_expert(4, staged=(s > 0))
                    load_expert(5, staged=(s > 0))
                    if s == 0:
                        load_expert(7, staged=False)
                if qb == 2:
                    p2_fill.extend(
                        [lambda i4_=i4_: ln_accum(0, i4_) for i4_ in range(4)]
                        + [lambda: ln_math(0, want_xnorm=True)]
                        + [lambda i4_=i4_: ln_norm_tile(0, i4_) for i4_ in range(4)])
                nch = 2 * (qb + 1)

                def emit_otrans(g, tb=None):
                    tb = 6 + (g % 2) if tb is None else tb
                    for ri in range(4):
                        run("tensor",
                            lambda e, ri=ri, g=g, tb=tb: e.matmul(
                                pb[tb][:, ri * 128:(ri + 1) * 128],
                                B(R_SQ + (g % 2) * 512 + ri * 128, 128), B(R_ID, 128),
                                start=True, stop=True),
                            read=(sq, idb), write=(bank[tb],),
                            inc="t" if ri == 3 else None)
                    run("vector",
                        lambda e, g=g, tb=tb: e.tensor_copy(B(R_LNF + g * 512, 512), pb[tb][:, 0:512]),
                        read=(bank[tb],), write=(lnf[g],), inc="v")

                for g in range(4):
                    def s_chunk(c, g=g, qb=qb):
                        sb = c % 2
                        pt = c % 4
                        diag = (c >= 2 * qb)
                        if diag:
                            delta = 128 * c - 256 * qb
                            run("tensor",
                                lambda e, sb=sb, delta=delta: e.matmul(
                                    pb[sb][:, 0:512].rearrange("p (h n) -> p h n", n=256),
                                    B(R_ID, 128),
                                    B(R_MASK + 128 - delta, 256).unsqueeze(1).broadcast_to([128, 2, 256]),
                                    start=True, stop=False),
                                read=(idb, maskb), write=(bank[sb],))
                        run("tensor",
                            lambda e, sb=sb, c=c, g=g, qb=qb, diag=diag: e.matmul(
                                pb[sb][:, 0:512].rearrange("p (h n) -> p h n", n=256),
                                B(R_KF + g * T + c * 128, 128),
                                B(R_QF + 2 * g * T + qb * 256, 2 * T).rearrange(
                                    "p (h t) -> p h t", t=T)[:, :, 0:256],
                                start=not diag, stop=True),
                            read=(kf[c // 4], qf[qb // 2]), write=(bank[sb],), inc="t")
                        run("scalar",
                            lambda e, sb=sb, pt=pt: e.activation(B(R_PT + pt * 512, 512), pb[sb][:, 0:512],
                                                                 AF.Exp, scale=SCALE),
                            read=(bank[sb],), write=(ptb[pt],), inc="s")

                    def pv_chunk(c, g=g, nch=nch):
                        pt = c % 4
                        for hl in range(2):
                            for qsub in range(2):
                                ob = 2 + hl * 2 + qsub
                                run("tensor",
                                    lambda e, pt=pt, hl=hl, qsub=qsub, ob=ob, c=c, g=g, nch=nch:
                                    e.matmul(pb[ob][:, 0:129],
                                             B(R_PT + pt * 512 + hl * 256 + qsub * 128, 128),
                                             B(R_VA + (g * 16 + c) * 129, 129),
                                             start=(c == 0), stop=(c == nch - 1)),
                                    read=(ptb[pt], vaug[c // 4]), write=(bank[ob],),
                                    inc="t" if c == nch - 1 else None)

                    PVD = 3
                    ot_pt = min(3, nch - 1)
                    for c in range(nch):
                        s_chunk(c)
                        if c == ot_pt and pend_otrans[0] is not None:
                            emit_otrans(pend_otrans[0])
                            pend_otrans[0] = None
                        if c >= PVD:
                            pv_chunk(c - PVD)
                    for c in range(max(0, nch - PVD), nch):
                        pv_chunk(c)
                    for hl in range(2):
                        for qsub in range(2):
                            ob = 2 + hl * 2 + qsub
                            ri = hl * 2 + qsub + (g % 2) * 4
                            run("vector",
                                lambda e, ob=ob, ri=ri: e.reciprocal(Fp(F_REC + ri, 1), pb[ob][:, 128:129]),
                                read=(bank[ob],), write=(orec,), inc="v")
                            run("vector",
                                lambda e, ob=ob, ri=ri, g=g, hl=hl, qsub=qsub: e.tensor_scalar(
                                    B(R_SQ + (g % 2) * 512 + (hl * 2 + qsub) * 128, 128),
                                    pb[ob][:, 0:128], Fp(F_REC + ri, 1), None, ALU.mult),
                                read=(bank[ob], orec), write=(sq,), inc="v")
                    pend_otrans[0] = g
                    if p2_fill:
                        p2_fill.pop(0)()
                for qsub in range(2):
                    tile = qb * 2 + qsub
                    for half in range(2):
                        pp = 6 + half
                        for hc in range(8):
                            run("tensor",
                                lambda e, hc=hc, qsub=qsub, half=half, pp=pp: e.matmul(
                                    pb[pp][:, 0:512],
                                    B(R_LNF + hc * 256 + qsub * 128, 128),
                                    B(W_O + hc * 1024 + half * 512, 512),
                                    start=(hc == 0), stop=(hc == 7)),
                                read=(lnf[hc // 2], wob), write=(bank[pp],),
                                inc="t" if hc == 7 else None)
                            if qsub == 0 and half == 0 and hc == 5 and pend_otrans[0] is not None:
                                emit_otrans(pend_otrans[0], tb=0)
                                pend_otrans[0] = None
                        run("vector",
                            lambda e, tile=tile, half=half, pp=pp: e.tensor_tensor(
                                B(R_X + tile * 1024 + half * 512, 512),
                                B(R_X + tile * 1024 + half * 512, 512),
                                pb[pp][:, 0:512], ALU.add),
                            read=(bank[pp], xp[tile // 4]), write=(xp[tile // 4],), inc="v")

            for e_ in (6, 0, 1, 2, 3):
                load_expert(e_, staged=(s > 0))

            for qs in range(NQS):
                ln_trans(qs)
                pr = qs % 2
                S = _st(pr)
                st = statsb[pr]
                for sub in range(4):
                    for k in range(8):
                        run("tensor",
                            lambda e, k=k, sub=sub: e.matmul(
                                pb[0][:, sub * 128:sub * 128 + 16],
                                B(R_LNF + k * 512 + sub * 128, 128),
                                B(R_WCAT + k * 16, 16),
                                start=(k == 0), stop=(k == 7)),
                            read=(lnf[k], wcat), write=(bank[0],),
                            inc="t" if k == 7 else None)
                cosp = pb[0][:, 0:512].rearrange("p (s c) -> p s c", c=128)[:, :, 0:8]
                routep = pb[0][:, 0:512].rearrange("p (s c) -> p s c", c=128)[:, :, 8:16]
                def b3(off, n1=4, n2=8):
                    return Fp(off, n1 * n2).rearrange("p (s c) -> p s c", c=n2)
                def bc(off):
                    return Fp(off, 4).unsqueeze(2).broadcast_to([128, 4, 8])
                run("vector",
                    lambda e, S=S: e.tensor_tensor(b3(GB_T), cosp, bc(S["IXN"]), ALU.mult),
                    read=(bank[0], st), write=(gat,), inc="v")
                run("vector", lambda e: e.tensor_tensor(b3(GB_L), b3(GB_T), routep, ALU.add),
                    read=(bank[0], gat), write=(gat,), inc="v")
                run("vector", lambda e: e.reduce_max(Fp(GB_M1, 4), b3(GB_L), AXX),
                    read=(gat,), write=(gat,), inc="v")
                run("vector", lambda e: e.tensor_scalar(Fp(GB_NM1, 4), Fp(GB_M1, 4), -1.0, None, ALU.mult),
                    read=(gat,), write=(gat,), inc="v")
                run("vector", lambda e: e.tensor_tensor(b3(GB_L), b3(GB_L), bc(GB_NM1), ALU.add),
                    read=(gat,), write=(gat,), inc="v")
                run("vector", lambda e: e.tensor_scalar(b3(GB_T), b3(GB_L), 0.0, None, ALU.is_ge),
                    read=(gat,), write=(gat,), inc="v")
                run("vector", lambda e: e.scalar_tensor_tensor(b3(GB_T), b3(GB_T), -1e30, b3(GB_L),
                                                               ALU.mult, ALU.add),
                    read=(gat,), write=(gat,), inc="v")
                run("vector", lambda e: e.reduce_max(Fp(GB_M2, 4), b3(GB_T), AXX),
                    read=(gat,), write=(gat,), inc="v")
                run("vector", lambda e: e.tensor_tensor(b3(GB_T), b3(GB_L), bc(GB_M2), ALU.is_ge),
                    read=(gat,), write=(gat,), inc="v")
                run("scalar", lambda e: e.activation(b3(GB_E), b3(GB_L), AF.Exp),
                    read=(gat,), write=(gat,), inc="s")
                run("vector", lambda e: e.tensor_tensor(b3(GB_L), b3(GB_E), b3(GB_T), ALU.mult),
                    read=(gat,), write=(gat,), inc="v")
                run("vector", lambda e: e.reduce_sum(Fp(GB_S, 4), b3(GB_L), AXX),
                    read=(gat,), write=(gat,), inc="v")
                run("vector", lambda e: e.reciprocal(Fp(GB_R, 4), Fp(GB_S, 4)),
                    read=(gat,), write=(gat,), inc="v")
                run("vector",
                    lambda e: e.tensor_tensor(
                        B(R_SQ, 32).rearrange("p (s c) -> p s c", c=8),
                        b3(GB_L), bc(GB_R), ALU.mult),
                    read=(gat,), write=(sq,), inc="v")
                def emit_gatesT():
                    for sub in range(4):
                        run("tensor",
                            lambda e, sub=sub: e.matmul(
                                pb[1][0:8, sub * 128:(sub + 1) * 128],
                                B(R_SQ + sub * 8, 8), B(R_ID, 128),
                                start=True, stop=True),
                            read=(sq, idb), write=(bank[1],),
                            inc="t" if sub == 3 else None)
                    run("scalar", lambda e: e.copy(B(R_GFM, 512, 0, 8), pb[1][0:8, 0:512]),
                        read=(bank[1],), write=(gfm,), inc="s")
                gatesT_pending = [True]

                for half_t in range(2):
                    if half_t == 1 and qs < NQS - 1:
                        ln_next = (qs + 1, True)
                    elif half_t == 1 and qs == NQS - 1 and s < NSEQ - 1:
                        ln_next = (0, False)
                    else:
                        ln_next = None
                    def w2_mms(ee, ei, stop=False):
                        base = exp_slot(ee)
                        gho = (ei % 2) * 1024
                        for qsub in range(2):
                            for half in range(2):
                                yb = 2 + qsub * 2 + half
                                for ko in range(4):
                                    run("tensor",
                                        lambda e, qsub=qsub, half=half, ko=ko, yb=yb, base=base,
                                               gho=gho, st=(ei == 0 and ko == 0),
                                               sp=(stop and ko == 3):
                                        e.matmul(pb[yb][:, 0:512],
                                                 B(R_GH + gho + ko * 256 + qsub * 128, 128),
                                                 B(base + 4096 + ko * 1024 + half * 512, 512),
                                                 start=st, stop=sp),
                                        read=(ghb[gho // 1024], ex[ee]), write=(bank[yb],),
                                        inc="t" if ko == 3 else None)

                    for ei, ee in enumerate(EXPERT_ORDER):
                        base = exp_slot(ee)
                        gb = ei % 2
                        gho = (ei % 2) * 1024

                        def h_group(ko, base=base, ee=ee, half_t=half_t):
                            pp = 6 + (ko % 2)
                            for k in range(8):
                                run("tensor",
                                    lambda e, k=k, ko=ko, base=base, half_t=half_t, pp=pp: e.matmul(
                                        pb[pp][:, 0:256],
                                        B(base + k * 512 + ko * 128, 128),
                                        B(R_LNF + k * 512 + half_t * 256, 256),
                                        start=(k == 0), stop=(k == 7)),
                                    read=(lnf[k], ex[ee]), write=(bank[pp],),
                                    inc="t" if k == 7 else None)

                        def gelu_gh(ko, ee=ee, gb=gb, gho=gho):
                            pp = 6 + (ko % 2)
                            run("scalar",
                                lambda e, ko=ko, ee=ee, pp=pp: e.activation(
                                    B(R_HS + (ko % 2) * 256, 256), pb[pp][:, 0:256],
                                    AF.Gelu, bias=Fp(F_B1 + ee * 4 + ko, 1)),
                                read=(bank[pp], bb1), write=(hsb[ko % 2],), inc="s")
                            run("vector",
                                lambda e, ko=ko, gb=gb, gho=gho: e.tensor_tensor(
                                    B(R_GH + gho + ko * 256, 256), B(R_HS + (ko % 2) * 256, 256),
                                    pb[gb][:, 0:256], ALU.mult),
                                read=(hsb[ko % 2], bank[gb]), write=(ghb[gho // 1024],), inc="v")

                        def emit_gb(ee=ee, half_t=half_t, gb=gb):
                            run("tensor",
                                lambda e, ee=ee, half_t=half_t, gb=gb: e.matmul(
                                    pb[gb][:, 0:256],
                                    B(R_SEL + ee * 128, 128, 0, 8),
                                    B(R_GFM + half_t * 256, 256, 0, 8),
                                    start=True, stop=True),
                                read=(gfm, selb), write=(bank[gb],), inc="t")

                        if gatesT_pending[0]:
                            h_group(0)
                            emit_gatesT()
                            gatesT_pending[0] = False
                            emit_gb()
                            gelu_gh(0)
                            for ko in range(1, 4):
                                h_group(ko)
                                gelu_gh(ko)
                        else:
                            emit_gb()
                            for ko in range(4):
                                h_group(ko)
                                gelu_gh(ko)
                        if ei >= 1:
                            w2_mms(EXPERT_ORDER[ei - 1], ei - 1)
                        if ln_next is not None and 2 <= ei <= 5:
                            if ei <= 3:
                                ln_accum(ln_next[0], 2 * (ei - 2))
                                ln_accum(ln_next[0], 2 * (ei - 2) + 1)
                            elif ei == 4:
                                ln_math(ln_next[0], want_xnorm=ln_next[1])
                            else:
                                for i4_ in range(4):
                                    ln_norm_tile(ln_next[0], i4_)
                    for qsub in range(2):
                        for half in range(2):
                            yb = 2 + qsub * 2 + half
                            run("tensor",
                                lambda e, qsub=qsub, half=half, yb=yb, half_t=half_t: e.matmul(
                                    pb[yb][:, 0:512],
                                    B(R_GFM + (half_t * 2 + qsub) * 128, 128, 0, 8),
                                    B(R_B2 + half * 512, 512, 0, 8),
                                    start=False, stop=False),
                                read=(gfm, bb2), write=(bank[yb],))
                    w2_mms(EXPERT_ORDER[7], 7, stop=True)
                    if qs == NQS - 1 and half_t == 1 and s < NSEQ - 1:
                        load_qkv_weights(staged=True)
                        load_wo(staged=True)
                    for qsub in range(2):
                        for half in range(2):
                            yb = 2 + qsub * 2 + half
                            tile = qs * 4 + half_t * 2 + qsub
                            run("vector",
                                lambda e, tile=tile, half=half, yb=yb: e.tensor_tensor(
                                    B(R_X + tile * 1024 + half * 512, 512),
                                    B(R_X + tile * 1024 + half * 512, 512),
                                    pb[yb][:, 0:512], ALU.add),
                                read=(bank[yb], xp[tile // 4]), write=(xp[tile // 4],), inc="v")
                dma("gpsimd",
                    out[s, qs * 512:(qs + 1) * 512, :].rearrange("(i p) d -> p i d", p=128),
                    B(R_X + qs * 4096, 4096).rearrange("p (i d) -> p i d", d=1024),
                    f"do{qs}", rbufs=(xp[qs],))
                if s < NSEQ - 1:
                    load_x(s + 1, qs)



        P.emit("gpsimd", lambda e: None,
               waits=[(f"do{i}", 16 * NSEQ) for i in range(4)])

        with nc.Block() as block:
            @block.tensor
            def _(eng):
                for op in P.ops["tensor"]:
                    op(eng, sems)

            @block.vector
            def _(eng):
                for op in P.ops["vector"]:
                    op(eng, sems)

            @block.scalar
            def _(eng):
                for op in P.ops["scalar"]:
                    op(eng, sems)

            @block.gpsimd
            def _(eng):
                for op in P.ops["gpsimd"]:
                    op(eng, sems)

            @block.sync
            def _(eng):
                for op in P.ops["sync"]:
                    op(eng, sems)

    return nc


_NC_CACHE = [None]


def kernel(L0_attn_hidden_states, L0_attn_Wq, L0_attn_Wk, L0_attn_Wv, L0_attn_Wo,
           L0_ffn_w1, L0_ffn_b1, L0_ffn_w2, L0_ffn_b2, L0_ffn_centroids,
           L0_ffn_w_route):
    x = np.ascontiguousarray(np.asarray(L0_attn_hidden_states, dtype=np.float32))
    consts = _host_consts()
    base = dict(wq=np.ascontiguousarray(L0_attn_Wq, dtype=np.float32),
                wk=np.ascontiguousarray(L0_attn_Wk, dtype=np.float32),
                wv=np.ascontiguousarray(L0_attn_Wv, dtype=np.float32),
                wo=np.ascontiguousarray(L0_attn_Wo, dtype=np.float32),
                w1=np.ascontiguousarray(L0_ffn_w1, dtype=np.float32),
                b1=np.ascontiguousarray(np.asarray(L0_ffn_b1, dtype=np.float32)
                                        .reshape(8, 4, 128).transpose(2, 0, 1)
                                        .reshape(128, 32)),
                w2=np.ascontiguousarray(L0_ffn_w2, dtype=np.float32),
                b2=np.ascontiguousarray(L0_ffn_b2, dtype=np.float32),
                cent=np.ascontiguousarray(L0_ffn_centroids, dtype=np.float32),
                wrt=np.ascontiguousarray(L0_ffn_w_route, dtype=np.float32),
                **consts)
    if _NC_CACHE[0] is None:
        _NC_CACHE[0] = build_nc()
    nc = _NC_CACHE[0]
    in_maps = []
    for c in range(NCORE):
        m = dict(base)
        m["x"] = np.ascontiguousarray(x[c * NSEQ:(c + 1) * NSEQ])
        in_maps.append(m)
    res = run_bass_kernel_spmd(nc, in_maps, core_ids=list(range(NCORE)))
    return np.concatenate([r["out"] for r in res.results], axis=0).reshape(32, T, DM)
